# version 21
# baseline (speedup 1.0000x reference)
import contextlib
import os

import numpy as np
import ml_dtypes
import concourse.bass as bass
import concourse.mybir as mybir
from concourse.bass_utils import run_bass_kernel_spmd

F32 = mybir.dt.float32
BF16 = mybir.dt.bfloat16
AF = mybir.ActivationFunctionType
ALU = mybir.AluOpType
AX = mybir.AxisListType

NCORE = 8
SEQ = 2048
DM = 1024
NT = 16
NB = 8
NH = 16
FQKV = 3072
NQC = 6
NE = 8
HID = 2048
NHB = 16
NCH = 4
EPS = 1e-5

_SEM_OWNER = {"sT": "tensor", "sV": "vector", "sS": "scalar"}


class Prog:
    ENG = ("tensor", "vector", "scalar", "gpsimd", "sync")

    def __init__(self):
        self.streams = {e: [] for e in self.ENG}
        self.sems = {}
        self.cnt = {}
        self.lastwait = {}

    def sem(self, name, handle):
        self.sems[name] = handle
        self.cnt[name] = 0

    def op(self, eng, fn, inc=None, inc_by=1):
        if inc is not None:
            self.cnt[inc] += inc_by
            h = self.sems[inc]
            self.streams[eng].append(lambda E, f=fn, h=h, k=inc_by: f(E).then_inc(h, k))
            return self.cnt[inc]
        self.streams[eng].append(fn)
        return None

    def wait(self, eng, sem, val):
        if val is None or val <= 0:
            return
        if _SEM_OWNER.get(sem) == eng:
            return
        assert val <= self.cnt[sem], (eng, sem, val, self.cnt[sem])
        key = (eng, sem)
        if self.lastwait.get(key, 0) >= val:
            return
        self.lastwait[key] = val
        h = self.sems[sem]
        self.streams[eng].append(lambda E, h=h, v=val: E.wait_ge(h, v))

    def dma_op(self, eng, sem, fn):
        self.wait(eng, sem, self.cnt[sem])
        return self.op(eng, fn, inc=sem, inc_by=16)

    def barrier_inc(self):
        h = self.sems["sB"]
        for e in self.ENG:
            if e == "sync":
                self.streams[e].append(lambda E, h=h: E.sem_inc(h, 1))
            else:
                self.streams[e].append(lambda E, h=h: E.drain().then_inc(h, 1))
        self.cnt["sB"] += 5

    def barrier_wait(self):
        h = self.sems["sB"]
        v = self.cnt["sB"]
        for e in self.ENG:
            self.streams[e].append(lambda E, h=h, v=v: E.wait_ge(h, v))

    def play_into(self, nc):
        with nc.Block() as block:
            streams = self.streams

            @block.tensor
            def _(E):
                for f in streams["tensor"]:
                    f(E)

            @block.vector
            def _(E):
                for f in streams["vector"]:
                    f(E)

            @block.scalar
            def _(E):
                for f in streams["scalar"]:
                    f(E)

            @block.gpsimd
            def _(E):
                for f in streams["gpsimd"]:
                    f(E)

            @block.sync
            def _(E):
                for f in streams["sync"]:
                    f(E)

        self.streams = {e: [] for e in self.ENG}
        self.lastwait = {}


def build_nc():
    nc = bass.Bass("TRN2", target_bir_lowering=False, detect_race_conditions=False)

    x_h = nc.dram_tensor("x", [SEQ, DM], F32, kind="ExternalInput").ap()
    wqkv_h = nc.dram_tensor("wqkv", [DM, FQKV], F32, kind="ExternalInput").ap()
    wgt_h = nc.dram_tensor("wgt", [NE, DM], F32, kind="ExternalInput").ap()
    b1_h = nc.dram_tensor("b1", [NE, HID], F32, kind="ExternalInput").ap()
    b2_h = nc.dram_tensor("b2", [NE, DM], F32, kind="ExternalInput").ap()
    w1_h = [nc.dram_tensor("w1_%d" % e, [DM, HID], F32, kind="ExternalInput").ap() for e in range(NE)]
    w2_h = [nc.dram_tensor("w2_%d" % e, [HID, DM], F32, kind="ExternalInput").ap() for e in range(NE)]
    i128f_h = nc.dram_tensor("i128f", [128, 128], F32, kind="ExternalInput").ap()
    i128b_h = nc.dram_tensor("i128b", [128, 128], BF16, kind="ExternalInput").ap()
    ustr_h = nc.dram_tensor("ustr", [128, 128], BF16, kind="ExternalInput").ap()
    ones1_h = nc.dram_tensor("ones1", [1, 128], BF16, kind="ExternalInput").ap()
    onesc_h = nc.dram_tensor("onesc", [128, 1], BF16, kind="ExternalInput").ap()
    esel_h = nc.dram_tensor("esel", [NE, NE * 128], BF16, kind="ExternalInput").ap()
    iota_h = nc.dram_tensor("iota", [128, NT], F32, kind="ExternalInput").ap()
    out_h = nc.dram_tensor("out", [SEQ, DM], F32, kind="ExternalOutput").ap()

    p = Prog()
    C = {}

    with contextlib.ExitStack() as stk:
        for s in ("sT", "sV", "sS", "sVf", "sSf", "dX", "dX2", "dW", "dO", "sB"):
            p.sem(s, stk.enter_context(nc.semaphore(s)))

        xres = stk.enter_context(nc.sbuf_tensor("xres", [128, NT * DM], F32))
        w1a = stk.enter_context(nc.sbuf_tensor("w1a", [128, NB * 1024], BF16))
        xnT = stk.enter_context(nc.sbuf_tensor("xnT", [128, NB * SEQ], BF16))
        scr32 = stk.enter_context(nc.sbuf_tensor("scr32", [128, DM], F32))
        xn2s = stk.enter_context(nc.sbuf_tensor("xn2s", [128, DM], F32))
        b1T = stk.enter_context(nc.sbuf_tensor("b1T", [128, 128], F32))
        wg_sb = stk.enter_context(nc.sbuf_tensor("wg_sb", [128, 64], F32))
        scoresT = stk.enter_context(nc.sbuf_tensor("scoresT", [8, SEQ], BF16))
        carryb = stk.enter_context(nc.sbuf_tensor("carryb", [1, DM], BF16))
        i128f = stk.enter_context(nc.sbuf_tensor("i128f_s", [128, 128], F32))
        i128b = stk.enter_context(nc.sbuf_tensor("i128b_s", [128, 128], BF16))
        ustr = stk.enter_context(nc.sbuf_tensor("ustr_s", [128, 128], BF16))
        ones1 = stk.enter_context(nc.sbuf_tensor("ones1_s", [1, 128], BF16))
        onesc = stk.enter_context(nc.sbuf_tensor("onesc_s", [128, 1], BF16))
        esel = stk.enter_context(nc.sbuf_tensor("esel_s", [NE, NE * 128], BF16))
        iota = stk.enter_context(nc.sbuf_tensor("iota_s", [128, NT], F32))
        sdot = stk.enter_context(nc.sbuf_tensor("sdot", [128, NH], F32))
        tmin = stk.enter_context(nc.sbuf_tensor("tmin", [128, NH], F32))
        tneg = stk.enter_context(nc.sbuf_tensor("tneg", [128, NH], F32))
        e_sb = stk.enter_context(nc.sbuf_tensor("e_sb", [128, NH], F32))
        z_sb = stk.enter_context(nc.sbuf_tensor("z_sb", [128, NH], F32))
        den = stk.enter_context(nc.sbuf_tensor("den", [128, NH], F32))
        rr = stk.enter_context(nc.sbuf_tensor("rr", [128, NH], F32))
        a_sb = stk.enter_context(nc.sbuf_tensor("a_sb", [128, NH], F32))
        bb_sb = stk.enter_context(nc.sbuf_tensor("bb_sb", [128, NH], F32))
        ln = {}
        for t in ("sum", "sumsq", "var", "lnv", "rstd", "negc"):
            for s in (2,):
                ln[(t, s)] = stk.enter_context(nc.sbuf_tensor("ln_%s_%d" % (t, s), [128, 1], F32))
        l1 = {}
        for t in ("sum", "sumsq", "d", "lnv", "rstd", "negc"):
            l1[t] = stk.enter_context(nc.sbuf_tensor("l1_%s" % t, [128, NT], F32))
        lgall = stk.enter_context(nc.sbuf_tensor("lgall", [128, 128], F32))
        gA = stk.enter_context(nc.sbuf_tensor("gA", [128, 128], F32))
        keepall = stk.enter_context(nc.sbuf_tensor("keepall", [128, 128], F32))
        exall = stk.enter_context(nc.sbuf_tensor("exall", [128, 128], F32))
        gm1a = stk.enter_context(nc.sbuf_tensor("gm1a", [128, NT], F32))
        gm2a = stk.enter_context(nc.sbuf_tensor("gm2a", [128, NT], F32))
        gnm1a = stk.enter_context(nc.sbuf_tensor("gnm1a", [128, NT], F32))
        gzsa = stk.enter_context(nc.sbuf_tensor("gzsa", [128, NT], F32))
        grza = stk.enter_context(nc.sbuf_tensor("grza", [128, NT], F32))

        pb = [stk.enter_context(nc.psum_tensor("pb%d" % i, [128, 512], F32)) for i in range(8)]

        def emit_ln(which, j, x_ap, out_ap, junk):
            key = "norm%d" % which
            s_sum = ln[("sum", which)]; s_sq = ln[("sumsq", which)]
            s_d = ln[("var", which)]
            s_lnv = ln[("lnv", which)]; s_rstd = ln[("rstd", which)]; s_negc = ln[("negc", which)]
            prev = C.setdefault(key, {}).get(j - 1)
            if prev:
                p.wait("vector", "sS", prev)
            p.op("vector", lambda E: E.reduce_sum(s_sum[:, :], x_ap, AX.X))
            f1 = p.op("vector", lambda E: E.scalar_tensor_tensor(
                junk[:, :], x_ap, 1.0, x_ap, ALU.mult, ALU.mult, accum_out=s_sq[:, :]),
                inc="sVf")
            p.wait("vector", "sVf", f1)
            f2 = p.op("vector", lambda E: E.tensor_scalar(
                s_d[:, :], s_sum[:, :], s_sum[:, :], 1.0 / DM, ALU.mult, ALU.mult), inc="sVf")
            p.wait("vector", "sVf", f2)
            vstats = p.op("vector", lambda E: E.scalar_tensor_tensor(
                s_d[:, :], s_d[:, :], -1.0, s_sq[:, :], ALU.mult, ALU.add), inc="sV")
            p.wait("scalar", "sV", vstats)
            g1 = p.op("scalar", lambda E: E.activation(
                s_lnv[:, :], s_d[:, :], AF.Ln,
                bias=np.float32(EPS), scale=np.float32(1.0 / DM)), inc="sSf")
            p.wait("scalar", "sSf", g1)
            srst = p.op("scalar", lambda E: E.activation(s_rstd[:, :], s_lnv[:, :], AF.Exp,
                                                         scale=np.float32(-0.5)),
                        inc="sS")
            p.wait("vector", "sS", srst)
            vneg = p.op("vector", lambda E: E.tensor_scalar(
                s_negc[:, :], s_sum[:, :], s_rstd[:, :], -1.0 / DM,
                ALU.mult, ALU.mult), inc="sV")
            p.wait("scalar", "sV", vneg)
            nrm = p.op("scalar", lambda E: E.activation(
                out_ap, x_ap, AF.Identity, bias=s_negc[:, :], scale=s_rstd[:, :]), inc="sS")
            C[key][j] = nrm
            return nrm

        with (
            nc.sbuf_tensor("wqkvsb", [128, NB * FQKV], BF16) as wqkvsb,
            nc.sbuf_tensor("qkvt", [128, 2 * FQKV], BF16) as qkvt,
            nc.sbuf_tensor("junk", [128, DM], BF16) as junk,
            nc.sbuf_tensor("xn1tm", [128, 2 * DM], F32) as xn1tm,
        ):
            wgtsb = xn2s
            qkvt_sl = [qkvt[:, FQKV * k:FQKV * (k + 1)] for k in range(2)]
            qkvt3s = [qkvt[:, FQKV * k:FQKV * (k + 1)].rearrange("p (h c) -> p h c", h=NH)
                      for k in range(2)]
            xnT3 = xnT[:, :].rearrange("p (b t) -> p b t", b=NB)

            C["dx_x"] = {}
            C["dx_sem"] = {}
            for j in range(4):
                src = x_h[128 * j:128 * (j + 1), :]
                dst = xres[:, DM * j:DM * (j + 1)]
                sem = "dX" if j % 2 == 0 else "dX2"
                C["dx_x"][j] = p.dma_op("sync", sem, lambda E, s=src, d=dst: E.dma_start(d, s))
                C["dx_sem"][j] = sem
            for src, dst in ((i128f_h, i128f), (i128b_h, i128b), (ustr_h, ustr),
                             (ones1_h, ones1), (onesc_h, onesc), (iota_h, iota), (esel_h, esel)):
                p.dma_op("sync", "dX", lambda E, s=src, d=dst: E.dma_start(d[:, :], s))
            p.dma_op("sync", "dX", lambda E: E.dma_start(wgtsb[0:8, :], wgt_h))
            C["dx_consts"] = p.cnt["dX"]
            for j in range(4, 8):
                src = x_h[128 * j:128 * (j + 1), :]
                dst = xres[:, DM * j:DM * (j + 1)]
                sem = "dX" if j % 2 == 0 else "dX2"
                C["dx_x"][j] = p.dma_op("sync", sem, lambda E, s=src, d=dst: E.dma_start(d, s))
                C["dx_sem"][j] = sem
            for j in range(8, NT):
                src = x_h[128 * j:128 * (j + 1), :]
                dst = xres[:, DM * j:DM * (j + 1)]
                C["dx_x"][j] = p.dma_op("scalar", "dO", lambda E, s=src, d=dst: E.dma_start(d, s))
                C["dx_sem"][j] = "dO"

            wq_src = wqkv_h.rearrange("(b p) f -> p b f", p=128)
            wq_dst = wqkvsb[:, :].rearrange("p (b f) -> p b f", b=NB)
            C["dw_wqkv"] = {}
            for c in range(NQC):
                s = wq_src[:, :, 512 * c:512 * (c + 1)]
                d = wq_dst[:, :, 512 * c:512 * (c + 1)]
                C["dw_wqkv"][c] = p.dma_op("gpsimd", "dW", lambda E, s=s, d=d: E.dma_start(d, s))
            C["dw_w1"] = {}
            s0 = w1_h[0].rearrange("(b p) h -> p b h", p=128)[:, :, 0:1024]
            w1a3 = w1a[:, :].rearrange("p (b h) -> p b h", b=NB)
            C["dw_w1"][(0, 0)] = p.dma_op("gpsimd", "dW", lambda E: E.dma_start(w1a3, s0))

            p.wait("tensor", "dX", C["dx_consts"])
            for b in range(NB):
                i_ = wgtsb[0:8, 128 * b:128 * (b + 1)]
                o_ = pb[4][:, 8 * b:8 * (b + 1)]
                last = p.op("tensor", lambda E, o=o_, i2=i_: E.transpose(o, i2, i128f[0:8, 0:8]),
                            inc="sT" if b == NB - 1 else None)
            C["wgT"] = last
            p.wait("vector", "sT", C["wgT"])
            C["wg_evac"] = p.op("vector", lambda E: E.tensor_copy(wg_sb[:, :], pb[4][:, 0:64]),
                                inc="sV")

            for name in ("t1a", "t1b", "xn1T_evac", "qkv_mm", "qkv_evac", "ab", "cums",
                         "carryF", "combS", "combV1", "combV2", "prepTr", "xn2f32", "pb5free",
                         "xn2bf", "logits", "lgc"):
                C[name] = {}

            C["carry0"] = p.op("vector", lambda E: E.memset(carryb[:, :], 0), inc="sV")

            qkv_banks = [pb[0], pb[1]]

            C["l1negc"] = {}
            GRPS = [(0, 2), (2, 4), (4, 8), (8, 12), (12, 16)]
            C["tile2grp"] = {}
            for g, (glo, ghi) in enumerate(GRPS):
                for j in range(glo, ghi):
                    C["tile2grp"][j] = g
                fX = None
                for j in range(glo, ghi):
                    p.wait("vector", C["dx_sem"][j], C["dx_x"][j])
                    x_t = xres[:, DM * j:DM * (j + 1)]
                    p.op("vector", lambda E, x=x_t, jj=j: E.reduce_sum(
                        l1["sum"][:, jj:jj + 1], x, AX.X))
                    fX = p.op("vector", lambda E, x=x_t, jj=j: E.scalar_tensor_tensor(
                        junk[:, :], x, 1.0, x, ALU.mult, ALU.mult,
                        accum_out=l1["sumsq"][:, jj:jj + 1]), inc="sVf")
                p.wait("vector", "sVf", fX)
                sl0, sl1 = glo, ghi
                f2 = p.op("vector", lambda E, a=sl0, b=sl1: E.tensor_mul(
                    l1["d"][:, a:b], l1["sum"][:, a:b], l1["sum"][:, a:b]), inc="sVf")
                p.wait("vector", "sVf", f2)
                vst = p.op("vector", lambda E, a=sl0, b=sl1: E.scalar_tensor_tensor(
                    l1["d"][:, a:b], l1["d"][:, a:b], -1.0 / DM, l1["sumsq"][:, a:b],
                    ALU.mult, ALU.add), inc="sV")
                p.wait("scalar", "sV", vst)
                g1 = p.op("scalar", lambda E, a=sl0, b=sl1: E.activation(
                    l1["lnv"][:, a:b], l1["d"][:, a:b], AF.Ln,
                    bias=np.float32(EPS), scale=np.float32(1.0 / DM)), inc="sSf")
                p.wait("scalar", "sSf", g1)
                g2 = p.op("scalar", lambda E, a=sl0, b=sl1: E.activation(
                    l1["rstd"][:, a:b], l1["lnv"][:, a:b], AF.Exp,
                    scale=np.float32(-0.5)), inc="sS")
                p.wait("vector", "sS", g2)
                f3 = p.op("vector", lambda E, a=sl0, b=sl1: E.tensor_mul(
                    l1["negc"][:, a:b], l1["sum"][:, a:b], l1["rstd"][:, a:b]), inc="sVf")
                p.wait("vector", "sVf", f3)
                C["l1negc"][g] = p.op("vector", lambda E, a=sl0, b=sl1: E.tensor_scalar_mul(
                    l1["negc"][:, a:b], l1["negc"][:, a:b], -1.0 / DM), inc="sV")

            LIM = int(os.environ.get("KMK_NTI", str(NT)))
            ST = os.environ.get("KMK_ST", "I")
            def emit_gate(tlo, thi):
                for i in range(tlo, thi):
                    p.op("vector", lambda E, ii=i: E.reduce_max(
                        gm1a[:, ii:ii + 1], lgall[:, 8 * ii:8 * (ii + 1)], AX.X))
                fg = p.op("vector", lambda E: E.engine_nop(), inc="sVf")
                p.wait("vector", "sVf", fg)
                for i in range(tlo, thi):
                    p.op("vector", lambda E, ii=i: E.tensor_scalar(
                        gA[:, 8 * ii:8 * (ii + 1)], lgall[:, 8 * ii:8 * (ii + 1)],
                        gm1a[:, ii:ii + 1], None, ALU.is_equal))
                fg = p.op("vector", lambda E: E.engine_nop(), inc="sVf")
                p.wait("vector", "sVf", fg)
                for i in range(tlo, thi):
                    p.op("vector", lambda E, ii=i: E.scalar_tensor_tensor(
                        gA[:, 8 * ii:8 * (ii + 1)], gA[:, 8 * ii:8 * (ii + 1)], -1e30,
                        lgall[:, 8 * ii:8 * (ii + 1)], ALU.mult, ALU.add))
                fg = p.op("vector", lambda E: E.engine_nop(), inc="sVf")
                p.wait("vector", "sVf", fg)
                for i in range(tlo, thi):
                    p.op("vector", lambda E, ii=i: E.reduce_max(
                        gm2a[:, ii:ii + 1], gA[:, 8 * ii:8 * (ii + 1)], AX.X))
                fg = p.op("vector", lambda E: E.engine_nop(), inc="sVf")
                p.wait("vector", "sVf", fg)
                for i in range(tlo, thi):
                    p.op("vector", lambda E, ii=i: E.tensor_scalar(
                        keepall[:, 8 * ii:8 * (ii + 1)], lgall[:, 8 * ii:8 * (ii + 1)],
                        gm2a[:, ii:ii + 1], None, ALU.is_ge))
                g1 = p.op("vector", lambda E, a=tlo, b=thi: E.tensor_scalar_mul(
                    gnm1a[:, a:b], gm1a[:, a:b], -1.0), inc="sV")
                p.wait("scalar", "sV", g1)
                for i in range(tlo, thi):
                    cnt = p.op("scalar", lambda E, ii=i: E.activation(
                        exall[:, 8 * ii:8 * (ii + 1)], lgall[:, 8 * ii:8 * (ii + 1)],
                        AF.Exp, bias=gnm1a[:, ii:ii + 1]),
                        inc="sS" if i == thi - 1 else None)
                p.wait("vector", "sS", cnt)
                fg = p.op("vector", lambda E, a=tlo, b=thi: E.tensor_mul(
                    exall[:, 8 * a:8 * b], exall[:, 8 * a:8 * b], keepall[:, 8 * a:8 * b]),
                    inc="sVf")
                p.wait("vector", "sVf", fg)
                for i in range(tlo, thi):
                    p.op("vector", lambda E, ii=i: E.reduce_sum(
                        gzsa[:, ii:ii + 1], exall[:, 8 * ii:8 * (ii + 1)], AX.X))
                fg = p.op("vector", lambda E: E.engine_nop(), inc="sVf")
                p.wait("vector", "sVf", fg)
                fg = p.op("vector", lambda E, a=tlo, b=thi: E.reciprocal(
                    grza[:, a:b], gzsa[:, a:b]), inc="sVf")
                p.wait("vector", "sVf", fg)
                for i in range(tlo, thi):
                    cnt = p.op("vector", lambda E, ii=i: E.tensor_scalar(
                        gA[:, 8 * ii:8 * (ii + 1)], exall[:, 8 * ii:8 * (ii + 1)],
                        grza[:, ii:ii + 1], None, ALU.mult),
                        inc="sV" if i == thi - 1 else None)
                return cnt

            def emit_norm1(jj):
                x_t2 = xres[:, DM * jj:DM * (jj + 1)]
                if jj >= 2:
                    p.wait("scalar", "sT", C["t1b"][jj - 2])
                p.wait("scalar", "sV", C["l1negc"][C["tile2grp"][jj]])
                C.setdefault("norm1", {})[jj] = p.op(
                    "scalar", lambda E, x=x_t2, j3=jj:
                    E.activation(xn1tm[:, DM * (j3 % 2):DM * (j3 % 2 + 1)], x, AF.Identity,
                                 bias=l1["negc"][:, j3:j3 + 1],
                                 scale=l1["rstd"][:, j3:j3 + 1]), inc="sS")

            for it in range(LIM + 1):
                j = it
                i = it - 1
                if j < LIM:
                    x_t = xres[:, DM * j:DM * (j + 1)]
                    if j == 0:
                        emit_norm1(0)
                    if ST < "B":
                        continue
                    p.wait("tensor", "sS", C["norm1"][j])
                    for b in range(NB):
                        if b == 0 and i - 1 >= 0:
                            p.wait("tensor", "sS", C["pb5free"][i - 1])
                        if b == 4 and i - 1 >= 0:
                            p.wait("tensor", "sS", C["xn2bf"][i - 1])
                        bank = pb[5] if b < 4 else pb[6]
                        o_ = bank[:, 128 * (b % 4):128 * (b % 4 + 1)]
                        i_ = xn1tm[:, DM * (j % 2) + 128 * b:DM * (j % 2) + 128 * (b + 1)]
                        cnt = p.op("tensor", lambda E, o=o_, i2=i_: E.transpose(o, i2, i128f[:, :]),
                                   inc="sT" if b in (3, 7) else None)
                        if b == 3:
                            C["t1a"][j] = cnt
                        elif b == 7:
                            C["t1b"][j] = cnt
                    for half in range(2):
                        p.wait("scalar", "sT", C["t1a"][j] if half == 0 else C["t1b"][j])
                        bank = pb[5] if half == 0 else pb[6]
                        i_ = bank[:, 0:512].rearrange("p (b t) -> p b t", b=4)
                        o_ = xnT3[:, 4 * half:4 * (half + 1), 128 * j:128 * (j + 1)]
                        cnt = p.op("scalar", lambda E, o=o_, i2=i_: E.copy(o, i2),
                                   inc="sS" if half == 1 else None)
                    C["xn1T_evac"][j] = cnt
                    if j + 1 < LIM:
                        emit_norm1(j + 1)

                if i >= 0 and ST >= "D":
                    p.wait("tensor", "sS", C["qkv_evac"][NQC * i + NQC - 1])
                    p.wait("tensor", "sV", C["carryF"].get(i - 1, C["carry0"]))
                    if i >= 1:
                        p.wait("tensor", "sV", C["combV2"][i - 1])
                    for half in range(2):
                        bank = pb[2 + half]
                        v_ap = qkvt3s[i % 2][:, 8 * half:8 * (half + 1), 128:192]
                        c_ap = carryb[0:1, 512 * half:512 * (half + 1)]
                        p.op("tensor", lambda E, o=bank[:, :], v=v_ap:
                             E.matmul(o, ustr[:, :], v, start=True, stop=False))
                        cnt = p.op("tensor", lambda E, o=bank[:, :], cc=c_ap:
                                   E.matmul(o, ones1[:, :], cc, start=False, stop=True), inc="sT")
                    C["cums"][i] = cnt
                    if ST < "F":
                        continue
                    for half in range(2):
                        v_ap = qkvt3s[i % 2][:, 8 * half:8 * (half + 1), 128:192]
                        if half == 1:
                            p.wait("tensor", "sV", C["colA0"])
                        colT = p.op("tensor", lambda E, v=v_ap:
                                    E.matmul(pb[7][0:1, :], onesc[:, :], v, start=True, stop=True),
                                    inc="sT")
                        C.setdefault("colT", {})[i] = colT
                        p.wait("vector", "sT", colT)
                        c_ap = carryb[0:1, 512 * half:512 * (half + 1)]
                        cnt = p.op("vector", lambda E, cc=c_ap: E.tensor_add(cc, pb[7][0:1, :], cc),
                                   inc="sV")
                        if half == 0:
                            C["colA0"] = cnt
                    C["carryF"][i] = cnt
                    p.wait("vector", "sS", C["qkv_evac"][NQC * i + NQC - 1])
                    if i >= 1:
                        p.wait("vector", "sS", C["combS"][i - 1])
                    q3 = qkvt3s[i % 2]
                    p.op("vector", lambda E, q3=q3: E.scalar_tensor_tensor(
                        q3[:, :, 0:64], q3[:, :, 0:64], 1.0, q3[:, :, 64:128],
                        ALU.mult, ALU.mult))
                    fD = p.op("vector", lambda E, q3=q3: E.tensor_reduce(
                        sdot[:, :], q3[:, :, 0:64], AX.X, ALU.add), inc="sVf")
                    p.wait("vector", "sVf", fD)
                    p.op("vector", lambda E: E.tensor_scalar(
                        tmin[:, :], sdot[:, :], 0.125, 0.0, ALU.mult, ALU.min))
                    dmm = p.op("vector", lambda E: E.tensor_scalar(
                        tneg[:, :], sdot[:, :], -0.125, 0.0, ALU.mult, ALU.min), inc="sV")
                    p.wait("scalar", "sV", dmm)
                    p.op("scalar", lambda E: E.activation(e_sb[:, :], tmin[:, :], AF.Exp))
                    eS = p.op("scalar", lambda E: E.activation(z_sb[:, :], tneg[:, :], AF.Exp),
                              inc="sS")
                    p.wait("vector", "sS", eS)
                    fD2 = p.op("vector", lambda E, ii=i: E.scalar_tensor_tensor(
                        den[:, :], z_sb[:, :], iota[:, ii:ii + 1], e_sb[:, :], ALU.mult, ALU.add),
                        inc="sVf")
                    p.wait("vector", "sVf", fD2)
                    fD3 = p.op("vector", lambda E: E.reciprocal(rr[:, :], den[:, :]), inc="sVf")
                    p.wait("vector", "sVf", fD3)
                    p.op("vector", lambda E: E.tensor_mul(a_sb[:, :], z_sb[:, :], rr[:, :]))
                    C["ab"][i] = p.op("vector", lambda E: E.tensor_mul(
                        bb_sb[:, :], e_sb[:, :], rr[:, :]), inc="sV")
                    if ST < "G":
                        continue
                    p.wait("scalar", "sT", C["cums"][i])
                    p.wait("scalar", "sV", C["carryF"][i])
                    p.wait("scalar", "sV", C["ab"][i])
                    combs_cnt = {}
                    for h in range(NH):
                        bank = pb[2] if h < 8 else pb[3]
                        sl = bank[:, 64 * (h % 8):64 * (h % 8 + 1)]
                        combs_cnt[h] = p.op("scalar", lambda E, s=sl, hh=h: E.activation(
                            s, s, AF.Copy, scale=a_sb[:, hh:hh + 1]), inc="sS")
                    C["combS"][i] = combs_cnt[NH - 1]
                    for h in range(NH):
                        p.wait("vector", "sS", combs_cnt[7] if h < 8 else combs_cnt[15])
                        bank = pb[2] if h < 8 else pb[3]
                        sl = bank[:, 64 * (h % 8):64 * (h % 8 + 1)]
                        v_ap = qkvt_sl[i % 2][:, 192 * h + 128:192 * h + 192]
                        cnt = p.op("vector", lambda E, s=sl, v=v_ap, hh=h: E.scalar_tensor_tensor(
                            s, v, bb_sb[:, hh:hh + 1], s, ALU.mult, ALU.add),
                            inc="sV" if h == NH - 1 else None)
                    C["combV1"][i] = cnt
                    fC = p.op("vector", lambda E: E.engine_nop(), inc="sVf")
                    p.wait("vector", "sVf", fC)
                    for half in range(2):
                        bank = pb[2 + half]
                        xsl = xres[:, DM * i + 512 * half: DM * i + 512 * (half + 1)]
                        cnt = p.op("vector", lambda E, x=xsl, b2_=bank[:, :]:
                                   E.tensor_add(x, x, b2_), inc="sV" if half == 1 else None)
                    C["combV2"][i] = cnt

                if j < LIM and ST >= "C":
                    if j >= 2 and ST >= "G":
                        p.wait("scalar", "sT", C["colT"][j - 2])
                        p.wait("scalar", "sV", C["combV1"][j - 2])
                    for c in range(NQC):
                        g = NQC * j + c
                        bank = qkv_banks[g % 2]
                        p.wait("tensor", "dW", C["dw_wqkv"][c])
                        p.wait("tensor", "sS", C["xn1T_evac"][j])
                        if g - 2 >= 0:
                            p.wait("tensor", "sS", C["qkv_evac"][g - 2])
                        for b in range(NB):
                            l_ = xnT[:, SEQ * b + 128 * j: SEQ * b + 128 * (j + 1)]
                            r_ = wqkvsb[:, FQKV * b + 512 * c: FQKV * b + 512 * (c + 1)]
                            cnt = p.op("tensor", lambda E, o=bank[:, :], l=l_, r=r_,
                                       st=(b == 0), sp=(b == NB - 1):
                                       E.matmul(o, l, r, start=st, stop=sp),
                                       inc="sT" if b == NB - 1 else None)
                        C["qkv_mm"][(j, c)] = cnt
                        p.wait("scalar", "sT", cnt)
                        o_ = qkvt_sl[j % 2][:, 512 * c:512 * (c + 1)]
                        C["qkv_evac"][g] = p.op("scalar", lambda E, o=o_, i2=bank[:, :]:
                                                E.copy(o, i2), inc="sS")

                if i >= 0 and ST >= "H":
                    if i >= 1:
                        p.wait("scalar", "sT", C["prepTr"][i - 1])
                    x_i = xres[:, DM * i:DM * (i + 1)]
                    emit_ln(2, i, x_i, scr32[:, :], junk)
                    if ST < "I":
                        continue
                    p.wait("tensor", "sS", C["norm2"][i])
                    p.wait("tensor", "sS", C["xn1T_evac"].get(i + 1, C["xn1T_evac"][i]))
                    for b in range(NB):
                        bank = pb[5] if b < 4 else pb[6]
                        o_ = bank[:, 128 * (b % 4):128 * (b % 4 + 1)]
                        i_ = scr32[:, 128 * b:128 * (b + 1)]
                        cnt = p.op("tensor", lambda E, o=o_, i2=i_: E.transpose(o, i2, i128f[:, :]),
                                   inc="sT" if b == NB - 1 else None)
                    C["prepTr"][i] = cnt
                    p.wait("scalar", "sT", C["prepTr"][i])
                    if i >= 1:
                        p.wait("scalar", "sT", C["logits"][i - 1])
                    else:
                        p.wait("scalar", "sT", C["wgT"])
                    p.op("scalar", lambda E: E.copy(xn2s[:, 0:512], pb[5][:, 0:512]))
                    o_ = xnT3[:, 0:4, 128 * i:128 * (i + 1)]
                    i_ = pb[5][:, 0:512].rearrange("p (b t) -> p b t", b=4)
                    c5 = p.op("scalar", lambda E, o=o_, i2=i_: E.copy(o, i2), inc="sS")
                    cF = p.op("scalar", lambda E: E.copy(xn2s[:, 512:1024], pb[6][:, 0:512]),
                              inc="sS")
                    o_ = xnT3[:, 4:8, 128 * i:128 * (i + 1)]
                    i_ = pb[6][:, 0:512].rearrange("p (b t) -> p b t", b=4)
                    cnt = p.op("scalar", lambda E, o=o_, i2=i_: E.copy(o, i2), inc="sS")
                    C["pb5free"][i] = c5
                    C["xn2bf"][i] = cnt
                    p.wait("tensor", "sS", cF)
                    p.wait("tensor", "sV", C["lgc"].get(i - 1, C["wg_evac"]))
                    for b in range(NB):
                        l_ = xn2s[:, 128 * b:128 * (b + 1)]
                        r_ = wg_sb[:, 8 * b:8 * (b + 1)]
                        cnt = p.op("tensor", lambda E, l=l_, r=r_, st=(b == 0), sp=(b == NB - 1):
                                   E.matmul(pb[4][:, 0:8], l, r, start=st, stop=sp),
                                   inc="sT" if b == NB - 1 else None)
                    C["logits"][i] = cnt
                    p.wait("vector", "sT", C["logits"][i])
                    C["lgc"][i] = p.op("vector", lambda E, ii=i: E.tensor_copy(
                        lgall[:, 8 * ii:8 * (ii + 1)], pb[4][:, 0:8]), inc="sV")
                    if i == 7 and LIM == NT:
                        C["gate2a"] = emit_gate(0, 8)
                    if i == 11 and LIM == NT:
                        C["gate2b1"] = emit_gate(8, 12)

            if LIM == NT:
                gate2all = max(C["gate2a"], C["gate2b1"], emit_gate(12, NT))
            elif LIM >= 1:
                gate2all = emit_gate(0, LIM)
            else:
                gate2all = 0
            p.wait("tensor", "sV", gate2all)
            if LIM >= 1:
                p.wait("tensor", "sV", C["lgc"][LIM - 1])
            if LIM >= 1 and (LIM - 1) in C["xn2bf"]:
                p.wait("tensor", "sS", C["xn2bf"][LIM - 1])
                p.wait("tensor", "sV", C["carryF"][LIM - 1])
            sco_tr = {}
            for i in range(LIM):
                bank = pb[4 + i // 4]
                o_ = bank[0:8, 128 * (i % 4):128 * (i % 4 + 1)]
                i_ = gA[:, 8 * i:8 * (i + 1)]
                cnt = p.op("tensor", lambda E, o=o_, i2=i_: E.transpose(o, i2, i128f[:, :]),
                           inc="sT" if i % 4 == 3 else None)
                if i % 4 == 3:
                    sco_tr[i // 4] = cnt
            scoresT3 = scoresT[0:8, :].rearrange("e (t p) -> e t p", t=NT)
            for k in range(LIM // 4):
                p.wait("vector", "sT", sco_tr[k])
                o_ = scoresT3[0:8, 4 * k:4 * (k + 1), :]
                i_ = pb[4 + k][0:8, 0:512].rearrange("e (t p) -> e t p", t=4)
                p.op("vector", lambda E, o=o_, i2=i_: E.tensor_copy(o, i2), inc="sV")

            if os.environ.get("KMK_PH", "ABC") != "ABC":
                p.barrier_inc()
            else:
                C["ph_sT"] = p.cnt["sT"]
                C["ph_sV"] = p.cnt["sV"]
                C["ph_sS"] = p.cnt["sS"]
                C["sco_last"] = p.cnt["sV"]
            p.play_into(nc)

        if os.environ.get("KMK_DBG"):
            dbg_xnT = nc.dram_tensor("dbg_xnT", [128, NB * SEQ], F32, kind="ExternalOutput").ap()
            dbg_qkvt = nc.dram_tensor("dbg_qkvt", [128, FQKV], F32, kind="ExternalOutput").ap()
            dbg_misc = nc.dram_tensor("dbg_misc", [128, 128], F32, kind="ExternalOutput").ap()
            dbg_xn1 = nc.dram_tensor("dbg_xn1", [128, DM], F32, kind="ExternalOutput").ap()
            dbg_carry = nc.dram_tensor("dbg_carry", [1, DM], F32, kind="ExternalOutput").ap()
            misc_srcs = [sdot, tmin, tneg, e_sb, z_sb, den, rr, a_sb]
            p.wait("gpsimd", "sT", p.cnt["sT"])
            p.wait("gpsimd", "sV", p.cnt["sV"])
            p.wait("gpsimd", "sS", p.cnt["sS"])
            p.dma_op("gpsimd", "dW", lambda E: E.dma_start(dbg_xnT, xnT[:, :]))
            p.dma_op("gpsimd", "dW", lambda E: E.dma_start(dbg_qkvt, qkvt[:, 0:FQKV]))
            p.dma_op("gpsimd", "dW", lambda E: E.dma_start(dbg_xn1, xn1tm[:, :]))
            p.dma_op("gpsimd", "dW", lambda E: E.dma_start(dbg_carry, carryb[:, :]))
            for mi, msrc in enumerate(misc_srcs):
                p.dma_op("gpsimd", "dW",
                         lambda E, o=dbg_misc[0:128, 16 * mi:16 * (mi + 1)], s=msrc:
                         E.dma_start(o, s[:, :]))
            dbg_ln = nc.dram_tensor("dbg_ln", [128, 12], F32, kind="ExternalOutput").ap()
            ln_srcs = [ln[(t, s)] for s in (1, 2)
                       for t in ("sum", "sumsq", "var", "lnv", "rstd", "negc")]
            for mi, msrc in enumerate(ln_srcs):
                p.dma_op("gpsimd", "dW",
                         lambda E, o=dbg_ln[0:128, mi:mi + 1], s=msrc:
                         E.dma_start(o, s[:, :], allow_slow_non_contiguous=True))
            p.wait("gpsimd", "dW", p.cnt["dW"])

        _PH = os.environ.get("KMK_PH", "ABC")
        if _PH == "A":
            p.barrier_wait()
            for tau in range(NT):
                o_ = out_h[128 * tau:128 * (tau + 1), :]
                i_ = xres[:, DM * tau:DM * (tau + 1)]
                p.dma_op("scalar", "dO", lambda E, o=o_, i2=i_: E.dma_start(o, i2))
            p.wait("scalar", "dO", 16 * NT)
            p.play_into(nc)
            return nc

        with (
            nc.sbuf_tensor("w2sb", [128, NHB * DM], BF16) as w2sb,
            nc.sbuf_tensor("hs", [128, NHB * 512], BF16) as hs,
            nc.sbuf_tensor("sbc", [128, SEQ], BF16) as sbc,
            nc.sbuf_tensor("b2sb", [8, DM], BF16) as b2sb,
            nc.sbuf_tensor("w1b", [128, NB * 1024], BF16) as w1b,
        ):
            if os.environ.get("KMK_PH", "ABC") != "ABC":
                p.barrier_wait()
            else:
                LL = LIM - 1
                p.wait("gpsimd", "sT", C["logits"][LL])
                p.wait("gpsimd", "sV", C["combV1"][LL])
                p.wait("gpsimd", "sS", C["xn2bf"][LL])
                p.wait("sync", "sT", C["logits"][LL])
                p.wait("tensor", "sV", C["sco_last"])
                p.wait("tensor", "sS", C["xn2bf"][LL])
            for half in range(2):
                dxb = p.dma_op("sync", "dX", lambda E, h2=half: E.dma_start(
                    xn2s[0:8, :], b1_h[0:8, 1024 * h2:1024 * (h2 + 1)]))
                p.wait("tensor", "dX", dxb)
                for k in range(8):
                    kk = 8 * half + k
                    i_ = xn2s[0:8, 128 * k:128 * (k + 1)]
                    o_ = pb[5][:, 8 * kk:8 * (kk + 1)]
                    cnt = p.op("tensor", lambda E, o=o_, i2=i_: E.transpose(o, i2, i128f[0:8, 0:8]),
                               inc="sT")
                if half == 0:
                    p.wait("sync", "sT", cnt)
            C["b1Tr"] = cnt
            p.wait("vector", "sT", C["b1Tr"])
            C["b1Te"] = p.op("vector", lambda E: E.tensor_copy(b1T[:, :], pb[5][:, 0:128]), inc="sV")
            w1_dst = [w1a[:, :].rearrange("p (b h) -> p b h", b=NB),
                      w1b[:, :].rearrange("p (b h) -> p b h", b=NB)]
            s = w1_h[0].rearrange("(b p) h -> p b h", p=128)[:, :, 1024:2048]
            C["dw_w1"][(0, 1)] = p.dma_op("gpsimd", "dW", lambda E, s=s: E.dma_start(w1_dst[1], s))
            C["dw_b2"] = p.dma_op("gpsimd", "dW", lambda E: E.dma_start(b2sb[:, :], b2_h))
            w2_dst3 = w2sb[:, :].rearrange("p (k d) -> p k d", k=NHB)
            C["dw_w2"] = {}

            def emit_w2_dma(e, dc, wait_sT):
                s = w2_h[e].rearrange("(k p) d -> p k d", p=128)[:, :, 512 * dc:512 * (dc + 1)]
                d = w2_dst3[:, :, 512 * dc:512 * (dc + 1)]
                p.wait("gpsimd", "sT", wait_sT)
                C["dw_w2"][(e, dc)] = p.dma_op("gpsimd", "dW", lambda E, s=s, d=d: E.dma_start(d, s))

            def emit_w1_dma(e, half, wait_sT):
                s = w1_h[e].rearrange("(b p) h -> p b h", p=128)[:, :, 1024 * half:1024 * (half + 1)]
                p.wait("gpsimd", "sT", wait_sT)
                C["dw_w1"][(e, half)] = p.dma_op("gpsimd", "dW",
                                                 lambda E, s=s, h2=half: E.dma_start(w1_dst[h2], s))

            emit_w2_dma(0, 0, 0)
            emit_w2_dma(0, 1, 0)

            for name in ("relu", "hsmul", "w2mm", "oadd", "sbc_mm", "sbc_ev",
                         "h_last", "w2_last"):
                C[name] = {}

            NEXP = int(os.environ.get("KMK_NE", str(NE)))
            for e in range(NEXP):
                p.wait("tensor", "sV", C["b1Te"])
                for c4 in range(4):
                    bank = pb[4 + (c4 % 2)]
                    prior = (e, c4 - 2) if c4 >= 2 else ((e - 1, c4 + 2) if e >= 1 else None)
                    if prior is not None:
                        p.wait("tensor", "sV", C["sbc_ev"][prior])
                    l_ = esel[0:8, 128 * e:128 * (e + 1)]
                    r_ = scoresT[0:8, 512 * c4:512 * (c4 + 1)]
                    C["sbc_mm"][(e, c4)] = p.op("tensor", lambda E, o=bank[:, :], l=l_, r=r_:
                                                E.matmul(o, l, r, start=True, stop=True),
                                                inc="sT")
                    p.wait("vector", "sT", C["sbc_mm"][(e, c4)])
                    o_ = sbc[:, 512 * c4:512 * (c4 + 1)]
                    C["sbc_ev"][(e, c4)] = p.op("vector", lambda E, o=o_, i2=bank[:, :]:
                                                E.tensor_copy(o, i2), inc="sV")
                    if c4 == 3:
                        fsb = p.op("vector", lambda E: E.engine_nop(), inc="sVf")
                        p.wait("vector", "sVf", fsb)

                for c in range(NCH):
                    for k in range(NHB):
                        bank = pb[k % 2]
                        if k == 0:
                            p.wait("tensor", "dW", C["dw_w1"][(e, 0)])
                        if k == 8:
                            p.wait("tensor", "dW", C["dw_w1"][(e, 1)])
                        if k >= 2:
                            p.wait("tensor", "sS", C["relu"][(e, c, k - 2)])
                        elif c > 0:
                            p.wait("tensor", "sS", C["relu"][(e, c - 1, NHB - 2 + k)])
                        elif e > 0:
                            p.wait("tensor", "sS", C["relu"][(e - 1, NCH - 1, NHB - 2 + k)])
                        w1x = w1a if k < 8 else w1b
                        kk = k % 8
                        for b in range(NB):
                            l_ = w1x[:, 1024 * b + 128 * kk: 1024 * b + 128 * (kk + 1)]
                            r_ = xnT[:, SEQ * b + 512 * c: SEQ * b + 512 * (c + 1)]
                            cnt = p.op("tensor", lambda E, o=bank[:, :], l=l_, r=r_,
                                       st=(b == 0), sp=(b == NB - 1):
                                       E.matmul(o, l, r, start=st, stop=sp),
                                       inc="sT" if b == NB - 1 else None)
                        if c == NCH - 1 and k == 7:
                            C["h_last"][(e, 0)] = cnt
                        if c == NCH - 1 and k == NHB - 1:
                            C["h_last"][(e, 1)] = cnt
                        p.wait("scalar", "sT", cnt)
                        o_ = hs[:, 512 * k:512 * (k + 1)]
                        C["relu"][(e, c, k)] = p.op(
                            "scalar",
                            lambda E, o=o_, i2=bank[:, :], bi=b1T[:, 8 * k + e:8 * k + e + 1]:
                            E.activation(o, i2, AF.Relu, bias=bi), inc="sS")
                        p.wait("vector", "sS", C["relu"][(e, c, k)])
                        sb_ap = sbc[:, 512 * c:512 * (c + 1)]
                        C["hsmul"][(e, c, k)] = p.op(
                            "vector", lambda E, o=o_, s=sb_ap: E.tensor_tensor(o, o, s, ALU.mult),
                            inc="sV")
                    if e == 0 and c == 0:
                        p.wait("tensor", "dW", C["dw_b2"])
                    for dc in range(2):
                        p.wait("tensor", "dW", C["dw_w2"][(e, dc)])
                        for tt in range(4):
                            gidx = dc * 4 + tt
                            bank = pb[2 + (gidx % 2)]
                            tau = 4 * c + tt
                            if gidx >= 2:
                                p.wait("tensor", "sV", C["oadd"][(e, c, gidx - 2)])
                            elif c > 0:
                                p.wait("tensor", "sV", C["oadd"][(e, c - 1, 6 + gidx)])
                            elif e > 0:
                                p.wait("tensor", "sV", C["oadd"][(e - 1, NCH - 1, 6 + gidx)])
                            for k in range(NHB):
                                p.wait("tensor", "sV", C["hsmul"][(e, c, k)])
                                l_ = hs[:, 512 * k + 128 * tt: 512 * k + 128 * (tt + 1)]
                                r_ = w2sb[:, DM * k + 512 * dc: DM * k + 512 * (dc + 1)]
                                is_last = (k == NHB - 1 and e != 0)
                                cnt = p.op("tensor", lambda E, o=bank[:, :], l=l_, r=r_,
                                           st=(k == 0), sp=is_last:
                                           E.matmul(o, l, r, start=st, stop=sp),
                                           inc="sT" if is_last else None)
                            if e == 0:
                                l_ = scoresT[0:8, 128 * tau:128 * (tau + 1)]
                                r_ = b2sb[0:8, 512 * dc:512 * (dc + 1)]
                                cnt = p.op("tensor", lambda E, o=bank[:, :], l=l_, r=r_:
                                           E.matmul(o, l, r, start=False, stop=True), inc="sT")
                            C["w2mm"][(e, c, gidx)] = cnt
                            if c == NCH - 1 and gidx == 7:
                                C["w2_last"][e] = cnt
                            p.wait("vector", "sT", cnt)
                            xsl = xres[:, DM * tau + 512 * dc: DM * tau + 512 * (dc + 1)]
                            C["oadd"][(e, c, gidx)] = p.op(
                                "vector", lambda E, x=xsl, b2_=bank[:, :]: E.tensor_add(x, x, b2_),
                                inc="sV")
                            if e == NEXP - 1 and dc == 1:
                                w0 = C["oadd"][(e, c, tt)]
                                w1v = C["oadd"][(e, c, 4 + tt)]
                                p.wait("scalar", "sV", max(w0, w1v))
                                o_ = out_h[128 * tau:128 * (tau + 1), :]
                                i_ = xres[:, DM * tau:DM * (tau + 1)]
                                p.dma_op("scalar", "dO", lambda E, o=o_, i2=i_: E.dma_start(o, i2))
                if e + 1 < NEXP:
                    emit_w1_dma(e + 1, 0, C["h_last"][(e, 0)])
                    emit_w1_dma(e + 1, 1, C["h_last"][(e, 1)])
                    emit_w2_dma(e + 1, 0, C["w2_last"][e])
                    emit_w2_dma(e + 1, 1, C["w2_last"][e])

            p.wait("scalar", "dO", p.cnt["dO"])
            p.play_into(nc)

    return nc


_NC_CACHE = None


def kernel(L0_attn_x, L0_attn_wqkv, L0_ffn_Wg, L0_ffn_W1, L0_ffn_b1, L0_ffn_W2, L0_ffn_b2):
    global _NC_CACHE
    x = np.ascontiguousarray(np.asarray(L0_attn_x, dtype=np.float32))
    wqkv = np.ascontiguousarray(np.asarray(L0_attn_wqkv, dtype=np.float32))
    wg = np.asarray(L0_ffn_Wg, dtype=np.float32)
    wgt = np.ascontiguousarray(wg.T)
    w1 = np.ascontiguousarray(np.asarray(L0_ffn_W1, dtype=np.float32))
    b1 = np.ascontiguousarray(np.asarray(L0_ffn_b1, dtype=np.float32))
    w2 = np.ascontiguousarray(np.asarray(L0_ffn_W2, dtype=np.float32))
    b2 = np.ascontiguousarray(np.asarray(L0_ffn_b2, dtype=np.float32))

    i128f = np.eye(128, dtype=np.float32)
    i128b = np.eye(128, dtype=np.float32).astype(ml_dtypes.bfloat16)
    ustr = np.triu(np.ones((128, 128), dtype=np.float32), k=1).astype(ml_dtypes.bfloat16)
    ones1 = np.ones((1, 128), dtype=ml_dtypes.bfloat16)
    onesc = np.ones((128, 1), dtype=ml_dtypes.bfloat16)
    esel = np.zeros((NE, NE * 128), dtype=ml_dtypes.bfloat16)
    for e in range(NE):
        esel[e, 128 * e:128 * (e + 1)] = 1
    iota = (np.arange(NT, dtype=np.float32)[None, :] * 128.0
            + np.arange(128, dtype=np.float32)[:, None])

    if _NC_CACHE is None:
        _NC_CACHE = build_nc()
    nc = _NC_CACHE

    base = {
        "wqkv": wqkv, "wgt": wgt, "b1": b1, "b2": b2,
        "i128f": i128f, "i128b": i128b, "ustr": ustr, "ones1": ones1,
        "onesc": onesc, "esel": esel, "iota": np.ascontiguousarray(iota),
    }
    for e in range(NE):
        base["w1_%d" % e] = w1[e]
        base["w2_%d" % e] = w2[e]
    in_maps = [dict(base, x=x[bidx]) for bidx in range(NCORE)]
    res = run_bass_kernel_spmd(nc, in_maps, core_ids=list(range(NCORE)))
    global LAST_RES
    LAST_RES = res
    return np.stack([res.results[bidx]["out"] for bidx in range(NCORE)], axis=0)


# version 22
# speedup vs baseline: 1.0329x; 1.0329x over previous
import contextlib
import os

import numpy as np
import ml_dtypes
import concourse.bass as bass
import concourse.mybir as mybir
from concourse.bass_utils import run_bass_kernel_spmd

F32 = mybir.dt.float32
BF16 = mybir.dt.bfloat16
AF = mybir.ActivationFunctionType
ALU = mybir.AluOpType
AX = mybir.AxisListType

NCORE = 8
SEQ = 2048
DM = 1024
NT = 16
NB = 8
NH = 16
FQKV = 3072
NQC = 6
NE = 8
HID = 2048
NHB = 16
NCH = 4
EPS = 1e-5

_SEM_OWNER = {"sT": "tensor", "sV": "vector", "sS": "scalar"}


class Prog:
    ENG = ("tensor", "vector", "scalar", "gpsimd", "sync")

    def __init__(self):
        self.streams = {e: [] for e in self.ENG}
        self.sems = {}
        self.cnt = {}
        self.lastwait = {}

    def sem(self, name, handle):
        self.sems[name] = handle
        self.cnt[name] = 0

    def op(self, eng, fn, inc=None, inc_by=1):
        if inc is not None:
            self.cnt[inc] += inc_by
            h = self.sems[inc]
            self.streams[eng].append(lambda E, f=fn, h=h, k=inc_by: f(E).then_inc(h, k))
            return self.cnt[inc]
        self.streams[eng].append(fn)
        return None

    def wait(self, eng, sem, val):
        if val is None or val <= 0:
            return
        if _SEM_OWNER.get(sem) == eng:
            return
        assert val <= self.cnt[sem], (eng, sem, val, self.cnt[sem])
        key = (eng, sem)
        if self.lastwait.get(key, 0) >= val:
            return
        self.lastwait[key] = val
        h = self.sems[sem]
        self.streams[eng].append(lambda E, h=h, v=val: E.wait_ge(h, v))

    def dma_op(self, eng, sem, fn):
        self.wait(eng, sem, self.cnt[sem])
        return self.op(eng, fn, inc=sem, inc_by=16)

    def barrier_inc(self):
        h = self.sems["sB"]
        for e in self.ENG:
            if e == "sync":
                self.streams[e].append(lambda E, h=h: E.sem_inc(h, 1))
            else:
                self.streams[e].append(lambda E, h=h: E.drain().then_inc(h, 1))
        self.cnt["sB"] += 5

    def barrier_wait(self):
        h = self.sems["sB"]
        v = self.cnt["sB"]
        for e in self.ENG:
            self.streams[e].append(lambda E, h=h, v=v: E.wait_ge(h, v))

    def play_into(self, nc):
        with nc.Block() as block:
            streams = self.streams

            @block.tensor
            def _(E):
                for f in streams["tensor"]:
                    f(E)

            @block.vector
            def _(E):
                for f in streams["vector"]:
                    f(E)

            @block.scalar
            def _(E):
                for f in streams["scalar"]:
                    f(E)

            @block.gpsimd
            def _(E):
                for f in streams["gpsimd"]:
                    f(E)

            @block.sync
            def _(E):
                for f in streams["sync"]:
                    f(E)

        self.streams = {e: [] for e in self.ENG}
        self.lastwait = {}


def build_nc():
    nc = bass.Bass("TRN2", target_bir_lowering=False, detect_race_conditions=False)

    x_h = nc.dram_tensor("x", [SEQ, DM], F32, kind="ExternalInput").ap()
    wqkv_h = nc.dram_tensor("wqkv", [DM, FQKV], F32, kind="ExternalInput").ap()
    wgt_h = nc.dram_tensor("wgt", [NE, DM], F32, kind="ExternalInput").ap()
    b1_h = nc.dram_tensor("b1", [NE, HID], F32, kind="ExternalInput").ap()
    b2_h = nc.dram_tensor("b2", [NE, DM], F32, kind="ExternalInput").ap()
    w1_h = [nc.dram_tensor("w1_%d" % e, [DM, HID], F32, kind="ExternalInput").ap() for e in range(NE)]
    w2_h = [nc.dram_tensor("w2_%d" % e, [HID, DM], F32, kind="ExternalInput").ap() for e in range(NE)]
    i128f_h = nc.dram_tensor("i128f", [128, 128], F32, kind="ExternalInput").ap()
    i128b_h = nc.dram_tensor("i128b", [128, 128], BF16, kind="ExternalInput").ap()
    ustr_h = nc.dram_tensor("ustr", [128, 128], BF16, kind="ExternalInput").ap()
    ones1_h = nc.dram_tensor("ones1", [1, 128], BF16, kind="ExternalInput").ap()
    onesc_h = nc.dram_tensor("onesc", [128, 1], BF16, kind="ExternalInput").ap()
    esel_h = nc.dram_tensor("esel", [NE, NE * 128], BF16, kind="ExternalInput").ap()
    iota_h = nc.dram_tensor("iota", [128, NT], F32, kind="ExternalInput").ap()
    out_h = nc.dram_tensor("out", [SEQ, DM], F32, kind="ExternalOutput").ap()

    p = Prog()
    C = {}

    with contextlib.ExitStack() as stk:
        for s in ("sT", "sV", "sS", "sVf", "sSf", "dX", "dX2", "dW", "dO", "sB"):
            p.sem(s, stk.enter_context(nc.semaphore(s)))

        xres = stk.enter_context(nc.sbuf_tensor("xres", [128, NT * DM], F32))
        w1a = stk.enter_context(nc.sbuf_tensor("w1a", [128, NB * 1024], BF16))
        xnT = stk.enter_context(nc.sbuf_tensor("xnT", [128, NB * SEQ], BF16))
        scr32 = stk.enter_context(nc.sbuf_tensor("scr32", [128, DM], F32))
        xn2s = stk.enter_context(nc.sbuf_tensor("xn2s", [128, DM], F32))
        b1T = stk.enter_context(nc.sbuf_tensor("b1T", [128, 128], F32))
        wg_sb = stk.enter_context(nc.sbuf_tensor("wg_sb", [128, 64], F32))
        scoresT = stk.enter_context(nc.sbuf_tensor("scoresT", [8, SEQ], BF16))
        carryb = stk.enter_context(nc.sbuf_tensor("carryb", [1, DM], BF16))
        i128f = stk.enter_context(nc.sbuf_tensor("i128f_s", [128, 128], F32))
        i128b = stk.enter_context(nc.sbuf_tensor("i128b_s", [128, 128], BF16))
        ustr = stk.enter_context(nc.sbuf_tensor("ustr_s", [128, 128], BF16))
        ones1 = stk.enter_context(nc.sbuf_tensor("ones1_s", [1, 128], BF16))
        onesc = stk.enter_context(nc.sbuf_tensor("onesc_s", [128, 1], BF16))
        esel = stk.enter_context(nc.sbuf_tensor("esel_s", [NE, NE * 128], BF16))
        iota = stk.enter_context(nc.sbuf_tensor("iota_s", [128, NT], F32))
        sdot = stk.enter_context(nc.sbuf_tensor("sdot", [128, NH], F32))
        tmin = stk.enter_context(nc.sbuf_tensor("tmin", [128, NH], F32))
        tneg = stk.enter_context(nc.sbuf_tensor("tneg", [128, NH], F32))
        e_sb = stk.enter_context(nc.sbuf_tensor("e_sb", [128, NH], F32))
        z_sb = stk.enter_context(nc.sbuf_tensor("z_sb", [128, NH], F32))
        den = stk.enter_context(nc.sbuf_tensor("den", [128, NH], F32))
        rr = stk.enter_context(nc.sbuf_tensor("rr", [128, NH], F32))
        a_sb = stk.enter_context(nc.sbuf_tensor("a_sb", [128, NH], F32))
        bb_sb = stk.enter_context(nc.sbuf_tensor("bb_sb", [128, NH], F32))
        ln = {}
        for t in ("sum", "sumsq", "var", "lnv", "rstd", "negc"):
            for s in (2,):
                ln[(t, s)] = stk.enter_context(nc.sbuf_tensor("ln_%s_%d" % (t, s), [128, 1], F32))
        l1 = {}
        for t in ("sum", "sumsq", "d", "lnv", "rstd", "negc"):
            l1[t] = stk.enter_context(nc.sbuf_tensor("l1_%s" % t, [128, NT], F32))
        lgall = stk.enter_context(nc.sbuf_tensor("lgall", [128, 128], F32))
        gA = stk.enter_context(nc.sbuf_tensor("gA", [128, 128], F32))
        keepall = stk.enter_context(nc.sbuf_tensor("keepall", [128, 128], F32))
        exall = stk.enter_context(nc.sbuf_tensor("exall", [128, 128], F32))
        gm1a = stk.enter_context(nc.sbuf_tensor("gm1a", [128, NT], F32))
        gm2a = stk.enter_context(nc.sbuf_tensor("gm2a", [128, NT], F32))
        gnm1a = stk.enter_context(nc.sbuf_tensor("gnm1a", [128, NT], F32))
        gzsa = stk.enter_context(nc.sbuf_tensor("gzsa", [128, NT], F32))
        grza = stk.enter_context(nc.sbuf_tensor("grza", [128, NT], F32))

        pb = [stk.enter_context(nc.psum_tensor("pb%d" % i, [128, 512], F32)) for i in range(8)]

        def emit_ln(which, j, x_ap, out_ap, junk):
            key = "norm%d" % which
            s_sum = ln[("sum", which)]; s_sq = ln[("sumsq", which)]
            s_d = ln[("var", which)]
            s_lnv = ln[("lnv", which)]; s_rstd = ln[("rstd", which)]; s_negc = ln[("negc", which)]
            prev = C.setdefault(key, {}).get(j - 1)
            if prev:
                p.wait("vector", "sS", prev)
            p.op("vector", lambda E: E.reduce_sum(s_sum[:, :], x_ap, AX.X))
            f1 = p.op("vector", lambda E: E.scalar_tensor_tensor(
                junk[:, :], x_ap, 1.0, x_ap, ALU.mult, ALU.mult, accum_out=s_sq[:, :]),
                inc="sVf")
            p.wait("vector", "sVf", f1)
            f2 = p.op("vector", lambda E: E.tensor_scalar(
                s_d[:, :], s_sum[:, :], s_sum[:, :], 1.0 / DM, ALU.mult, ALU.mult), inc="sVf")
            p.wait("vector", "sVf", f2)
            vstats = p.op("vector", lambda E: E.scalar_tensor_tensor(
                s_d[:, :], s_d[:, :], -1.0, s_sq[:, :], ALU.mult, ALU.add), inc="sV")
            p.wait("scalar", "sV", vstats)
            g1 = p.op("scalar", lambda E: E.activation(
                s_lnv[:, :], s_d[:, :], AF.Ln,
                bias=np.float32(EPS), scale=np.float32(1.0 / DM)), inc="sSf")
            p.wait("scalar", "sSf", g1)
            srst = p.op("scalar", lambda E: E.activation(s_rstd[:, :], s_lnv[:, :], AF.Exp,
                                                         scale=np.float32(-0.5)),
                        inc="sS")
            p.wait("vector", "sS", srst)
            vneg = p.op("vector", lambda E: E.tensor_scalar(
                s_negc[:, :], s_sum[:, :], s_rstd[:, :], -1.0 / DM,
                ALU.mult, ALU.mult), inc="sV")
            p.wait("scalar", "sV", vneg)
            nrm = p.op("scalar", lambda E: E.activation(
                out_ap, x_ap, AF.Identity, bias=s_negc[:, :], scale=s_rstd[:, :]), inc="sS")
            C[key][j] = nrm
            return nrm

        with (
            nc.sbuf_tensor("wqkvsb", [128, NB * FQKV], BF16) as wqkvsb,
            nc.sbuf_tensor("qkvt", [128, 2 * FQKV], BF16) as qkvt,
            nc.sbuf_tensor("junk", [128, DM], BF16) as junk,
            nc.sbuf_tensor("xn1tm", [128, 2 * DM], F32) as xn1tm,
        ):
            wgtsb = xn2s
            qkvt_sl = [qkvt[:, FQKV * k:FQKV * (k + 1)] for k in range(2)]
            qkvt3s = [qkvt[:, FQKV * k:FQKV * (k + 1)].rearrange("p (h c) -> p h c", h=NH)
                      for k in range(2)]
            xnT3 = xnT[:, :].rearrange("p (b t) -> p b t", b=NB)

            C["dx_x"] = {}
            C["dx_sem"] = {}
            for j in range(4):
                src = x_h[128 * j:128 * (j + 1), :]
                dst = xres[:, DM * j:DM * (j + 1)]
                sem = "dX" if j % 2 == 0 else "dX2"
                C["dx_x"][j] = p.dma_op("sync", sem, lambda E, s=src, d=dst: E.dma_start(d, s))
                C["dx_sem"][j] = sem
            for src, dst in ((i128f_h, i128f), (i128b_h, i128b), (ustr_h, ustr),
                             (ones1_h, ones1), (onesc_h, onesc), (iota_h, iota), (esel_h, esel)):
                p.dma_op("sync", "dX", lambda E, s=src, d=dst: E.dma_start(d[:, :], s))
            p.dma_op("sync", "dX", lambda E: E.dma_start(wgtsb[0:8, :], wgt_h))
            C["dx_consts"] = p.cnt["dX"]
            for j in range(4, 8):
                src = x_h[128 * j:128 * (j + 1), :]
                dst = xres[:, DM * j:DM * (j + 1)]
                sem = "dX" if j % 2 == 0 else "dX2"
                C["dx_x"][j] = p.dma_op("sync", sem, lambda E, s=src, d=dst: E.dma_start(d, s))
                C["dx_sem"][j] = sem
            for j in range(8, NT):
                src = x_h[128 * j:128 * (j + 1), :]
                dst = xres[:, DM * j:DM * (j + 1)]
                C["dx_x"][j] = p.dma_op("scalar", "dO", lambda E, s=src, d=dst: E.dma_start(d, s))
                C["dx_sem"][j] = "dO"

            wq_src = wqkv_h.rearrange("(b p) f -> p b f", p=128)
            wq_dst = wqkvsb[:, :].rearrange("p (b f) -> p b f", b=NB)
            C["dw_wqkv"] = {}
            for c in range(NQC):
                s = wq_src[:, :, 512 * c:512 * (c + 1)]
                d = wq_dst[:, :, 512 * c:512 * (c + 1)]
                C["dw_wqkv"][c] = p.dma_op("gpsimd", "dW", lambda E, s=s, d=d: E.dma_start(d, s))
            C["dw_w1"] = {}
            s0 = w1_h[0].rearrange("(b p) h -> p b h", p=128)[:, :, 0:1024]
            w1a3 = w1a[:, :].rearrange("p (b h) -> p b h", b=NB)
            C["dw_w1"][(0, 0)] = p.dma_op("gpsimd", "dW", lambda E: E.dma_start(w1a3, s0))

            p.wait("tensor", "dX", C["dx_consts"])
            for b in range(NB):
                i_ = wgtsb[0:8, 128 * b:128 * (b + 1)]
                o_ = pb[4][:, 8 * b:8 * (b + 1)]
                last = p.op("tensor", lambda E, o=o_, i2=i_: E.transpose(o, i2, i128f[0:8, 0:8]),
                            inc="sT" if b == NB - 1 else None)
            C["wgT"] = last
            p.wait("vector", "sT", C["wgT"])
            C["wg_evac"] = p.op("vector", lambda E: E.tensor_copy(wg_sb[:, :], pb[4][:, 0:64]),
                                inc="sV")

            for name in ("t1a", "t1b", "xn1T_evac", "qkv_mm", "qkv_evac", "ab", "cums",
                         "carryF", "combS", "combV1", "combV2", "prepTr", "xn2f32", "pb5free",
                         "xn2bf", "logits", "lgc"):
                C[name] = {}

            C["carry0"] = p.op("vector", lambda E: E.memset(carryb[:, :], 0), inc="sV")

            qkv_banks = [pb[0], pb[1]]

            C["l1negc"] = {}
            GRPS = [(0, 2), (2, 4), (4, 8), (8, 12), (12, 16)]
            C["tile2grp"] = {}
            for g, (glo, ghi) in enumerate(GRPS):
                for j in range(glo, ghi):
                    C["tile2grp"][j] = g
                fX = None
                for j in range(glo, ghi):
                    p.wait("vector", C["dx_sem"][j], C["dx_x"][j])
                    x_t = xres[:, DM * j:DM * (j + 1)]
                    p.op("vector", lambda E, x=x_t, jj=j: E.reduce_sum(
                        l1["sum"][:, jj:jj + 1], x, AX.X))
                    fX = p.op("vector", lambda E, x=x_t, jj=j: E.scalar_tensor_tensor(
                        junk[:, :], x, 1.0, x, ALU.mult, ALU.mult,
                        accum_out=l1["sumsq"][:, jj:jj + 1]), inc="sVf")
                p.wait("vector", "sVf", fX)
                sl0, sl1 = glo, ghi
                f2 = p.op("vector", lambda E, a=sl0, b=sl1: E.tensor_mul(
                    l1["d"][:, a:b], l1["sum"][:, a:b], l1["sum"][:, a:b]), inc="sVf")
                p.wait("vector", "sVf", f2)
                vst = p.op("vector", lambda E, a=sl0, b=sl1: E.scalar_tensor_tensor(
                    l1["d"][:, a:b], l1["d"][:, a:b], -1.0 / DM, l1["sumsq"][:, a:b],
                    ALU.mult, ALU.add), inc="sV")
                p.wait("scalar", "sV", vst)
                g1 = p.op("scalar", lambda E, a=sl0, b=sl1: E.activation(
                    l1["lnv"][:, a:b], l1["d"][:, a:b], AF.Ln,
                    bias=np.float32(EPS), scale=np.float32(1.0 / DM)), inc="sSf")
                p.wait("scalar", "sSf", g1)
                g2 = p.op("scalar", lambda E, a=sl0, b=sl1: E.activation(
                    l1["rstd"][:, a:b], l1["lnv"][:, a:b], AF.Exp,
                    scale=np.float32(-0.5)), inc="sS")
                p.wait("vector", "sS", g2)
                f3 = p.op("vector", lambda E, a=sl0, b=sl1: E.tensor_mul(
                    l1["negc"][:, a:b], l1["sum"][:, a:b], l1["rstd"][:, a:b]), inc="sVf")
                p.wait("vector", "sVf", f3)
                C["l1negc"][g] = p.op("vector", lambda E, a=sl0, b=sl1: E.tensor_scalar_mul(
                    l1["negc"][:, a:b], l1["negc"][:, a:b], -1.0 / DM), inc="sV")

            LIM = int(os.environ.get("KMK_NTI", str(NT)))
            ST = os.environ.get("KMK_ST", "I")
            def emit_gate(tlo, thi):
                for i in range(tlo, thi):
                    p.op("vector", lambda E, ii=i: E.reduce_max(
                        gm1a[:, ii:ii + 1], lgall[:, 8 * ii:8 * (ii + 1)], AX.X))
                fg = p.op("vector", lambda E: E.engine_nop(), inc="sVf")
                p.wait("vector", "sVf", fg)
                for i in range(tlo, thi):
                    p.op("vector", lambda E, ii=i: E.tensor_scalar(
                        gA[:, 8 * ii:8 * (ii + 1)], lgall[:, 8 * ii:8 * (ii + 1)],
                        gm1a[:, ii:ii + 1], None, ALU.is_equal))
                fg = p.op("vector", lambda E: E.engine_nop(), inc="sVf")
                p.wait("vector", "sVf", fg)
                for i in range(tlo, thi):
                    p.op("vector", lambda E, ii=i: E.scalar_tensor_tensor(
                        gA[:, 8 * ii:8 * (ii + 1)], gA[:, 8 * ii:8 * (ii + 1)], -1e30,
                        lgall[:, 8 * ii:8 * (ii + 1)], ALU.mult, ALU.add))
                fg = p.op("vector", lambda E: E.engine_nop(), inc="sVf")
                p.wait("vector", "sVf", fg)
                for i in range(tlo, thi):
                    p.op("vector", lambda E, ii=i: E.reduce_max(
                        gm2a[:, ii:ii + 1], gA[:, 8 * ii:8 * (ii + 1)], AX.X))
                fg = p.op("vector", lambda E: E.engine_nop(), inc="sVf")
                p.wait("vector", "sVf", fg)
                for i in range(tlo, thi):
                    p.op("vector", lambda E, ii=i: E.tensor_scalar(
                        keepall[:, 8 * ii:8 * (ii + 1)], lgall[:, 8 * ii:8 * (ii + 1)],
                        gm2a[:, ii:ii + 1], None, ALU.is_ge))
                g1 = p.op("vector", lambda E, a=tlo, b=thi: E.tensor_scalar_mul(
                    gnm1a[:, a:b], gm1a[:, a:b], -1.0), inc="sV")
                p.wait("scalar", "sV", g1)
                for i in range(tlo, thi):
                    cnt = p.op("scalar", lambda E, ii=i: E.activation(
                        exall[:, 8 * ii:8 * (ii + 1)], lgall[:, 8 * ii:8 * (ii + 1)],
                        AF.Exp, bias=gnm1a[:, ii:ii + 1]),
                        inc="sS" if i == thi - 1 else None)
                p.wait("vector", "sS", cnt)
                fg = p.op("vector", lambda E, a=tlo, b=thi: E.tensor_mul(
                    exall[:, 8 * a:8 * b], exall[:, 8 * a:8 * b], keepall[:, 8 * a:8 * b]),
                    inc="sVf")
                p.wait("vector", "sVf", fg)
                for i in range(tlo, thi):
                    p.op("vector", lambda E, ii=i: E.reduce_sum(
                        gzsa[:, ii:ii + 1], exall[:, 8 * ii:8 * (ii + 1)], AX.X))
                fg = p.op("vector", lambda E: E.engine_nop(), inc="sVf")
                p.wait("vector", "sVf", fg)
                fg = p.op("vector", lambda E, a=tlo, b=thi: E.reciprocal(
                    grza[:, a:b], gzsa[:, a:b]), inc="sVf")
                p.wait("vector", "sVf", fg)
                for i in range(tlo, thi):
                    cnt = p.op("vector", lambda E, ii=i: E.tensor_scalar(
                        gA[:, 8 * ii:8 * (ii + 1)], exall[:, 8 * ii:8 * (ii + 1)],
                        grza[:, ii:ii + 1], None, ALU.mult),
                        inc="sV" if i == thi - 1 else None)
                return cnt

            def emit_norm1(jj):
                x_t2 = xres[:, DM * jj:DM * (jj + 1)]
                if jj >= 2:
                    p.wait("scalar", "sT", C["t1b"][jj - 2])
                p.wait("scalar", "sV", C["l1negc"][C["tile2grp"][jj]])
                C.setdefault("norm1", {})[jj] = p.op(
                    "scalar", lambda E, x=x_t2, j3=jj:
                    E.activation(xn1tm[:, DM * (j3 % 2):DM * (j3 % 2 + 1)], x, AF.Identity,
                                 bias=l1["negc"][:, j3:j3 + 1],
                                 scale=l1["rstd"][:, j3:j3 + 1]), inc="sS")

            for it in range(LIM + 1):
                j = it
                i = it - 1
                if j < LIM:
                    x_t = xres[:, DM * j:DM * (j + 1)]
                    if j == 0:
                        emit_norm1(0)
                    if ST < "B":
                        continue
                    p.wait("tensor", "sS", C["norm1"][j])
                    for b in range(NB):
                        if b == 0 and i - 1 >= 0:
                            p.wait("tensor", "sS", C["pb5free"][i - 1])
                        if b == 4 and i - 1 >= 0:
                            p.wait("tensor", "sS", C["xn2bf"][i - 1])
                        bank = pb[5] if b < 4 else pb[6]
                        o_ = bank[:, 128 * (b % 4):128 * (b % 4 + 1)]
                        i_ = xn1tm[:, DM * (j % 2) + 128 * b:DM * (j % 2) + 128 * (b + 1)]
                        cnt = p.op("tensor", lambda E, o=o_, i2=i_: E.transpose(o, i2, i128f[:, :]),
                                   inc="sT" if b in (3, 7) else None)
                        if b == 3:
                            C["t1a"][j] = cnt
                        elif b == 7:
                            C["t1b"][j] = cnt
                    for half in range(2):
                        p.wait("scalar", "sT", C["t1a"][j] if half == 0 else C["t1b"][j])
                        bank = pb[5] if half == 0 else pb[6]
                        i_ = bank[:, 0:512].rearrange("p (b t) -> p b t", b=4)
                        o_ = xnT3[:, 4 * half:4 * (half + 1), 128 * j:128 * (j + 1)]
                        cnt = p.op("scalar", lambda E, o=o_, i2=i_: E.copy(o, i2),
                                   inc="sS" if half == 1 else None)
                    C["xn1T_evac"][j] = cnt
                    if j + 1 < LIM:
                        emit_norm1(j + 1)

                if i >= 0 and ST >= "D":
                    p.wait("vector", "sS", C["qkv_evac"][NQC * i + NQC - 1])
                    if i >= 1:
                        p.wait("vector", "sS", C["combS"][i - 1])
                    q3 = qkvt3s[i % 2]
                    p.op("vector", lambda E, q3=q3: E.scalar_tensor_tensor(
                        q3[:, :, 0:64], q3[:, :, 0:64], 1.0, q3[:, :, 64:128],
                        ALU.mult, ALU.mult))
                    fD = p.op("vector", lambda E, q3=q3: E.tensor_reduce(
                        sdot[:, :], q3[:, :, 0:64], AX.X, ALU.add), inc="sVf")
                    p.wait("vector", "sVf", fD)
                    p.op("vector", lambda E: E.tensor_scalar(
                        tmin[:, :], sdot[:, :], 0.125, 0.0, ALU.mult, ALU.min))
                    dmm = p.op("vector", lambda E: E.tensor_scalar(
                        tneg[:, :], sdot[:, :], -0.125, 0.0, ALU.mult, ALU.min), inc="sV")
                    p.wait("scalar", "sV", dmm)
                    p.op("scalar", lambda E: E.activation(e_sb[:, :], tmin[:, :], AF.Exp))
                    eS = p.op("scalar", lambda E: E.activation(z_sb[:, :], tneg[:, :], AF.Exp),
                              inc="sS")
                    p.wait("vector", "sS", eS)
                    fD2 = p.op("vector", lambda E, ii=i: E.scalar_tensor_tensor(
                        den[:, :], z_sb[:, :], iota[:, ii:ii + 1], e_sb[:, :], ALU.mult, ALU.add),
                        inc="sVf")
                    p.wait("vector", "sVf", fD2)
                    fD3 = p.op("vector", lambda E: E.reciprocal(rr[:, :], den[:, :]), inc="sVf")
                    p.wait("vector", "sVf", fD3)
                    p.op("vector", lambda E: E.tensor_mul(a_sb[:, :], z_sb[:, :], rr[:, :]))
                    C["ab"][i] = p.op("vector", lambda E: E.tensor_mul(
                        bb_sb[:, :], e_sb[:, :], rr[:, :]), inc="sV")
                    p.wait("tensor", "sS", C["qkv_evac"][NQC * i + NQC - 1])
                    p.wait("tensor", "sV", C["carryF"].get(i - 1, C["carry0"]))
                    if i >= 1:
                        p.wait("tensor", "sV", C["combV2"][i - 1])
                    for half in range(2):
                        bank = pb[2 + half]
                        v_ap = qkvt3s[i % 2][:, 8 * half:8 * (half + 1), 128:192]
                        c_ap = carryb[0:1, 512 * half:512 * (half + 1)]
                        p.op("tensor", lambda E, o=bank[:, :], v=v_ap:
                             E.matmul(o, ustr[:, :], v, start=True, stop=False))
                        cnt = p.op("tensor", lambda E, o=bank[:, :], cc=c_ap:
                                   E.matmul(o, ones1[:, :], cc, start=False, stop=True), inc="sT")
                    C["cums"][i] = cnt
                    if ST < "F":
                        continue
                    for half in range(2):
                        v_ap = qkvt3s[i % 2][:, 8 * half:8 * (half + 1), 128:192]
                        if half == 1:
                            p.wait("tensor", "sV", C["colA0"])
                        colT = p.op("tensor", lambda E, v=v_ap:
                                    E.matmul(pb[7][0:1, :], onesc[:, :], v, start=True, stop=True),
                                    inc="sT")
                        C.setdefault("colT", {})[i] = colT
                        p.wait("vector", "sT", colT)
                        c_ap = carryb[0:1, 512 * half:512 * (half + 1)]
                        cnt = p.op("vector", lambda E, cc=c_ap: E.tensor_add(cc, pb[7][0:1, :], cc),
                                   inc="sV")
                        if half == 0:
                            C["colA0"] = cnt
                    C["carryF"][i] = cnt
                    if ST < "G":
                        continue
                    p.wait("scalar", "sT", C["cums"][i])
                    p.wait("scalar", "sV", C["carryF"][i])
                    p.wait("scalar", "sV", C["ab"][i])
                    combs_cnt = {}
                    for h in range(NH):
                        bank = pb[2] if h < 8 else pb[3]
                        sl = bank[:, 64 * (h % 8):64 * (h % 8 + 1)]
                        combs_cnt[h] = p.op("scalar", lambda E, s=sl, hh=h: E.activation(
                            s, s, AF.Copy, scale=a_sb[:, hh:hh + 1]), inc="sS")
                    C["combS"][i] = combs_cnt[NH - 1]
                    for h in range(NH):
                        p.wait("vector", "sS", combs_cnt[7] if h < 8 else combs_cnt[15])
                        bank = pb[2] if h < 8 else pb[3]
                        sl = bank[:, 64 * (h % 8):64 * (h % 8 + 1)]
                        v_ap = qkvt_sl[i % 2][:, 192 * h + 128:192 * h + 192]
                        cnt = p.op("vector", lambda E, s=sl, v=v_ap, hh=h: E.scalar_tensor_tensor(
                            s, v, bb_sb[:, hh:hh + 1], s, ALU.mult, ALU.add),
                            inc="sV" if h == NH - 1 else None)
                    C["combV1"][i] = cnt
                    fC = p.op("vector", lambda E: E.engine_nop(), inc="sVf")
                    p.wait("vector", "sVf", fC)
                    for half in range(2):
                        bank = pb[2 + half]
                        xsl = xres[:, DM * i + 512 * half: DM * i + 512 * (half + 1)]
                        cnt = p.op("vector", lambda E, x=xsl, b2_=bank[:, :]:
                                   E.tensor_add(x, x, b2_), inc="sV" if half == 1 else None)
                    C["combV2"][i] = cnt

                if j < LIM and ST >= "C":
                    if j >= 2 and ST >= "G":
                        p.wait("scalar", "sT", C["colT"][j - 2])
                        p.wait("scalar", "sV", C["combV1"][j - 2])
                    for c in range(NQC):
                        g = NQC * j + c
                        bank = qkv_banks[g % 2]
                        p.wait("tensor", "dW", C["dw_wqkv"][c])
                        p.wait("tensor", "sS", C["xn1T_evac"][j])
                        if g - 2 >= 0:
                            p.wait("tensor", "sS", C["qkv_evac"][g - 2])
                        for b in range(NB):
                            l_ = xnT[:, SEQ * b + 128 * j: SEQ * b + 128 * (j + 1)]
                            r_ = wqkvsb[:, FQKV * b + 512 * c: FQKV * b + 512 * (c + 1)]
                            cnt = p.op("tensor", lambda E, o=bank[:, :], l=l_, r=r_,
                                       st=(b == 0), sp=(b == NB - 1):
                                       E.matmul(o, l, r, start=st, stop=sp),
                                       inc="sT" if b == NB - 1 else None)
                        C["qkv_mm"][(j, c)] = cnt
                        p.wait("scalar", "sT", cnt)
                        o_ = qkvt_sl[j % 2][:, 512 * c:512 * (c + 1)]
                        C["qkv_evac"][g] = p.op("scalar", lambda E, o=o_, i2=bank[:, :]:
                                                E.copy(o, i2), inc="sS")

                if i >= 0 and ST >= "H":
                    if i >= 1:
                        p.wait("scalar", "sT", C["prepTr"][i - 1])
                    x_i = xres[:, DM * i:DM * (i + 1)]
                    emit_ln(2, i, x_i, scr32[:, :], junk)
                    if ST < "I":
                        continue
                    p.wait("tensor", "sS", C["norm2"][i])
                    p.wait("tensor", "sS", C["xn1T_evac"].get(i + 1, C["xn1T_evac"][i]))
                    for b in range(NB):
                        bank = pb[5] if b < 4 else pb[6]
                        o_ = bank[:, 128 * (b % 4):128 * (b % 4 + 1)]
                        i_ = scr32[:, 128 * b:128 * (b + 1)]
                        cnt = p.op("tensor", lambda E, o=o_, i2=i_: E.transpose(o, i2, i128f[:, :]),
                                   inc="sT" if b == NB - 1 else None)
                    C["prepTr"][i] = cnt
                    p.wait("scalar", "sT", C["prepTr"][i])
                    if i >= 1:
                        p.wait("scalar", "sT", C["logits"][i - 1])
                    else:
                        p.wait("scalar", "sT", C["wgT"])
                    p.op("scalar", lambda E: E.copy(xn2s[:, 0:512], pb[5][:, 0:512]))
                    o_ = xnT3[:, 0:4, 128 * i:128 * (i + 1)]
                    i_ = pb[5][:, 0:512].rearrange("p (b t) -> p b t", b=4)
                    c5 = p.op("scalar", lambda E, o=o_, i2=i_: E.copy(o, i2), inc="sS")
                    cF = p.op("scalar", lambda E: E.copy(xn2s[:, 512:1024], pb[6][:, 0:512]),
                              inc="sS")
                    o_ = xnT3[:, 4:8, 128 * i:128 * (i + 1)]
                    i_ = pb[6][:, 0:512].rearrange("p (b t) -> p b t", b=4)
                    cnt = p.op("scalar", lambda E, o=o_, i2=i_: E.copy(o, i2), inc="sS")
                    C["pb5free"][i] = c5
                    C["xn2bf"][i] = cnt
                    p.wait("tensor", "sS", cF)
                    p.wait("tensor", "sV", C["lgc"].get(i - 1, C["wg_evac"]))
                    for b in range(NB):
                        l_ = xn2s[:, 128 * b:128 * (b + 1)]
                        r_ = wg_sb[:, 8 * b:8 * (b + 1)]
                        cnt = p.op("tensor", lambda E, l=l_, r=r_, st=(b == 0), sp=(b == NB - 1):
                                   E.matmul(pb[4][:, 0:8], l, r, start=st, stop=sp),
                                   inc="sT" if b == NB - 1 else None)
                    C["logits"][i] = cnt
                    p.wait("vector", "sT", C["logits"][i])
                    C["lgc"][i] = p.op("vector", lambda E, ii=i: E.tensor_copy(
                        lgall[:, 8 * ii:8 * (ii + 1)], pb[4][:, 0:8]), inc="sV")
                    if i == 7 and LIM == NT:
                        C["gate2a"] = emit_gate(0, 8)
                    if i == 11 and LIM == NT:
                        C["gate2b1"] = emit_gate(8, 12)

            if LIM == NT:
                gate2all = max(C["gate2a"], C["gate2b1"], emit_gate(12, NT))
            elif LIM >= 1:
                gate2all = emit_gate(0, LIM)
            else:
                gate2all = 0
            p.wait("tensor", "sV", gate2all)
            if LIM >= 1:
                p.wait("tensor", "sV", C["lgc"][LIM - 1])
            if LIM >= 1 and (LIM - 1) in C["xn2bf"]:
                p.wait("tensor", "sS", C["xn2bf"][LIM - 1])
                p.wait("tensor", "sV", C["carryF"][LIM - 1])
            sco_tr = {}
            for i in range(LIM):
                bank = pb[4 + i // 4]
                o_ = bank[0:8, 128 * (i % 4):128 * (i % 4 + 1)]
                i_ = gA[:, 8 * i:8 * (i + 1)]
                cnt = p.op("tensor", lambda E, o=o_, i2=i_: E.transpose(o, i2, i128f[:, :]),
                           inc="sT" if i % 4 == 3 else None)
                if i % 4 == 3:
                    sco_tr[i // 4] = cnt
            scoresT3 = scoresT[0:8, :].rearrange("e (t p) -> e t p", t=NT)
            for k in range(LIM // 4):
                p.wait("vector", "sT", sco_tr[k])
                o_ = scoresT3[0:8, 4 * k:4 * (k + 1), :]
                i_ = pb[4 + k][0:8, 0:512].rearrange("e (t p) -> e t p", t=4)
                p.op("vector", lambda E, o=o_, i2=i_: E.tensor_copy(o, i2), inc="sV")

            if os.environ.get("KMK_PH", "ABC") != "ABC":
                p.barrier_inc()
            else:
                C["ph_sT"] = p.cnt["sT"]
                C["ph_sV"] = p.cnt["sV"]
                C["ph_sS"] = p.cnt["sS"]
                C["sco_last"] = p.cnt["sV"]
            p.play_into(nc)

        if os.environ.get("KMK_DBG"):
            dbg_xnT = nc.dram_tensor("dbg_xnT", [128, NB * SEQ], F32, kind="ExternalOutput").ap()
            dbg_qkvt = nc.dram_tensor("dbg_qkvt", [128, FQKV], F32, kind="ExternalOutput").ap()
            dbg_misc = nc.dram_tensor("dbg_misc", [128, 128], F32, kind="ExternalOutput").ap()
            dbg_xn1 = nc.dram_tensor("dbg_xn1", [128, DM], F32, kind="ExternalOutput").ap()
            dbg_carry = nc.dram_tensor("dbg_carry", [1, DM], F32, kind="ExternalOutput").ap()
            misc_srcs = [sdot, tmin, tneg, e_sb, z_sb, den, rr, a_sb]
            p.wait("gpsimd", "sT", p.cnt["sT"])
            p.wait("gpsimd", "sV", p.cnt["sV"])
            p.wait("gpsimd", "sS", p.cnt["sS"])
            p.dma_op("gpsimd", "dW", lambda E: E.dma_start(dbg_xnT, xnT[:, :]))
            p.dma_op("gpsimd", "dW", lambda E: E.dma_start(dbg_qkvt, qkvt[:, 0:FQKV]))
            p.dma_op("gpsimd", "dW", lambda E: E.dma_start(dbg_xn1, xn1tm[:, :]))
            p.dma_op("gpsimd", "dW", lambda E: E.dma_start(dbg_carry, carryb[:, :]))
            for mi, msrc in enumerate(misc_srcs):
                p.dma_op("gpsimd", "dW",
                         lambda E, o=dbg_misc[0:128, 16 * mi:16 * (mi + 1)], s=msrc:
                         E.dma_start(o, s[:, :]))
            dbg_ln = nc.dram_tensor("dbg_ln", [128, 12], F32, kind="ExternalOutput").ap()
            ln_srcs = [ln[(t, s)] for s in (1, 2)
                       for t in ("sum", "sumsq", "var", "lnv", "rstd", "negc")]
            for mi, msrc in enumerate(ln_srcs):
                p.dma_op("gpsimd", "dW",
                         lambda E, o=dbg_ln[0:128, mi:mi + 1], s=msrc:
                         E.dma_start(o, s[:, :], allow_slow_non_contiguous=True))
            p.wait("gpsimd", "dW", p.cnt["dW"])

        _PH = os.environ.get("KMK_PH", "ABC")
        if _PH == "A":
            p.barrier_wait()
            for tau in range(NT):
                o_ = out_h[128 * tau:128 * (tau + 1), :]
                i_ = xres[:, DM * tau:DM * (tau + 1)]
                p.dma_op("scalar", "dO", lambda E, o=o_, i2=i_: E.dma_start(o, i2))
            p.wait("scalar", "dO", 16 * NT)
            p.play_into(nc)
            return nc

        with (
            nc.sbuf_tensor("w2sb", [128, NHB * DM], BF16) as w2sb,
            nc.sbuf_tensor("hs", [128, NHB * 512], BF16) as hs,
            nc.sbuf_tensor("sbc", [128, SEQ], BF16) as sbc,
            nc.sbuf_tensor("b2sb", [8, DM], BF16) as b2sb,
            nc.sbuf_tensor("w1b", [128, NB * 1024], BF16) as w1b,
        ):
            if os.environ.get("KMK_PH", "ABC") != "ABC":
                p.barrier_wait()
            else:
                LL = LIM - 1
                p.wait("gpsimd", "sT", C["logits"][LL])
                p.wait("gpsimd", "sV", C["combV1"][LL])
                p.wait("gpsimd", "sS", C["xn2bf"][LL])
                p.wait("sync", "sT", C["logits"][LL])
                p.wait("tensor", "sV", C["sco_last"])
                p.wait("tensor", "sS", C["xn2bf"][LL])
            for half in range(2):
                dxb = p.dma_op("sync", "dX", lambda E, h2=half: E.dma_start(
                    xn2s[0:8, :], b1_h[0:8, 1024 * h2:1024 * (h2 + 1)]))
                p.wait("tensor", "dX", dxb)
                for k in range(8):
                    kk = 8 * half + k
                    i_ = xn2s[0:8, 128 * k:128 * (k + 1)]
                    o_ = pb[5][:, 8 * kk:8 * (kk + 1)]
                    cnt = p.op("tensor", lambda E, o=o_, i2=i_: E.transpose(o, i2, i128f[0:8, 0:8]),
                               inc="sT")
                if half == 0:
                    p.wait("sync", "sT", cnt)
            C["b1Tr"] = cnt
            p.wait("vector", "sT", C["b1Tr"])
            C["b1Te"] = p.op("vector", lambda E: E.tensor_copy(b1T[:, :], pb[5][:, 0:128]), inc="sV")
            w1_dst = [w1a[:, :].rearrange("p (b h) -> p b h", b=NB),
                      w1b[:, :].rearrange("p (b h) -> p b h", b=NB)]
            s = w1_h[0].rearrange("(b p) h -> p b h", p=128)[:, :, 1024:2048]
            C["dw_w1"][(0, 1)] = p.dma_op("gpsimd", "dW", lambda E, s=s: E.dma_start(w1_dst[1], s))
            C["dw_b2"] = p.dma_op("gpsimd", "dW", lambda E: E.dma_start(b2sb[:, :], b2_h))
            w2_dst3 = w2sb[:, :].rearrange("p (k d) -> p k d", k=NHB)
            C["dw_w2"] = {}

            def emit_w2_dma(e, dc, wait_sT):
                s = w2_h[e].rearrange("(k p) d -> p k d", p=128)[:, :, 512 * dc:512 * (dc + 1)]
                d = w2_dst3[:, :, 512 * dc:512 * (dc + 1)]
                p.wait("gpsimd", "sT", wait_sT)
                C["dw_w2"][(e, dc)] = p.dma_op("gpsimd", "dW", lambda E, s=s, d=d: E.dma_start(d, s))

            def emit_w1_dma(e, half, wait_sT):
                s = w1_h[e].rearrange("(b p) h -> p b h", p=128)[:, :, 1024 * half:1024 * (half + 1)]
                p.wait("gpsimd", "sT", wait_sT)
                C["dw_w1"][(e, half)] = p.dma_op("gpsimd", "dW",
                                                 lambda E, s=s, h2=half: E.dma_start(w1_dst[h2], s))

            emit_w2_dma(0, 0, 0)
            emit_w2_dma(0, 1, 0)

            for name in ("relu", "hsmul", "w2mm", "oadd", "sbc_mm", "sbc_ev",
                         "h_last", "w2_last"):
                C[name] = {}

            NEXP = int(os.environ.get("KMK_NE", str(NE)))
            for e in range(NEXP):
                p.wait("tensor", "sV", C["b1Te"])
                for c4 in range(4):
                    bank = pb[4 + (c4 % 2)]
                    prior = (e, c4 - 2) if c4 >= 2 else ((e - 1, c4 + 2) if e >= 1 else None)
                    if prior is not None:
                        p.wait("tensor", "sV", C["sbc_ev"][prior])
                    l_ = esel[0:8, 128 * e:128 * (e + 1)]
                    r_ = scoresT[0:8, 512 * c4:512 * (c4 + 1)]
                    C["sbc_mm"][(e, c4)] = p.op("tensor", lambda E, o=bank[:, :], l=l_, r=r_:
                                                E.matmul(o, l, r, start=True, stop=True),
                                                inc="sT")
                    p.wait("vector", "sT", C["sbc_mm"][(e, c4)])
                    o_ = sbc[:, 512 * c4:512 * (c4 + 1)]
                    C["sbc_ev"][(e, c4)] = p.op("vector", lambda E, o=o_, i2=bank[:, :]:
                                                E.tensor_copy(o, i2), inc="sV")
                    if c4 == 3:
                        fsb = p.op("vector", lambda E: E.engine_nop(), inc="sVf")
                        p.wait("vector", "sVf", fsb)

                for c in range(NCH):
                    for k in range(NHB):
                        bank = pb[k % 2]
                        if k == 0:
                            p.wait("tensor", "dW", C["dw_w1"][(e, 0)])
                        if k == 8:
                            p.wait("tensor", "dW", C["dw_w1"][(e, 1)])
                        if k >= 2:
                            p.wait("tensor", "sS", C["relu"][(e, c, k - 2)])
                        elif c > 0:
                            p.wait("tensor", "sS", C["relu"][(e, c - 1, NHB - 2 + k)])
                        elif e > 0:
                            p.wait("tensor", "sS", C["relu"][(e - 1, NCH - 1, NHB - 2 + k)])
                        w1x = w1a if k < 8 else w1b
                        kk = k % 8
                        for b in range(NB):
                            l_ = w1x[:, 1024 * b + 128 * kk: 1024 * b + 128 * (kk + 1)]
                            r_ = xnT[:, SEQ * b + 512 * c: SEQ * b + 512 * (c + 1)]
                            cnt = p.op("tensor", lambda E, o=bank[:, :], l=l_, r=r_,
                                       st=(b == 0), sp=(b == NB - 1):
                                       E.matmul(o, l, r, start=st, stop=sp),
                                       inc="sT" if b == NB - 1 else None)
                        if c == NCH - 1 and k == 7:
                            C["h_last"][(e, 0)] = cnt
                        if c == NCH - 1 and k == NHB - 1:
                            C["h_last"][(e, 1)] = cnt
                        p.wait("scalar", "sT", cnt)
                        o_ = hs[:, 512 * k:512 * (k + 1)]
                        C["relu"][(e, c, k)] = p.op(
                            "scalar",
                            lambda E, o=o_, i2=bank[:, :], bi=b1T[:, 8 * k + e:8 * k + e + 1]:
                            E.activation(o, i2, AF.Relu, bias=bi), inc="sS")
                        p.wait("vector", "sS", C["relu"][(e, c, k)])
                        sb_ap = sbc[:, 512 * c:512 * (c + 1)]
                        C["hsmul"][(e, c, k)] = p.op(
                            "vector", lambda E, o=o_, s=sb_ap: E.tensor_tensor(o, o, s, ALU.mult),
                            inc="sV")
                    if e == 0 and c == 0:
                        p.wait("tensor", "dW", C["dw_b2"])
                    for dc in range(2):
                        p.wait("tensor", "dW", C["dw_w2"][(e, dc)])
                        for tt in range(4):
                            gidx = dc * 4 + tt
                            bank = pb[2 + (gidx % 2)]
                            tau = 4 * c + tt
                            if gidx >= 2:
                                p.wait("tensor", "sV", C["oadd"][(e, c, gidx - 2)])
                            elif c > 0:
                                p.wait("tensor", "sV", C["oadd"][(e, c - 1, 6 + gidx)])
                            elif e > 0:
                                p.wait("tensor", "sV", C["oadd"][(e - 1, NCH - 1, 6 + gidx)])
                            for k in range(NHB):
                                p.wait("tensor", "sV", C["hsmul"][(e, c, k)])
                                l_ = hs[:, 512 * k + 128 * tt: 512 * k + 128 * (tt + 1)]
                                r_ = w2sb[:, DM * k + 512 * dc: DM * k + 512 * (dc + 1)]
                                is_last = (k == NHB - 1 and e != 0)
                                cnt = p.op("tensor", lambda E, o=bank[:, :], l=l_, r=r_,
                                           st=(k == 0), sp=is_last:
                                           E.matmul(o, l, r, start=st, stop=sp),
                                           inc="sT" if is_last else None)
                            if e == 0:
                                l_ = scoresT[0:8, 128 * tau:128 * (tau + 1)]
                                r_ = b2sb[0:8, 512 * dc:512 * (dc + 1)]
                                cnt = p.op("tensor", lambda E, o=bank[:, :], l=l_, r=r_:
                                           E.matmul(o, l, r, start=False, stop=True), inc="sT")
                            C["w2mm"][(e, c, gidx)] = cnt
                            if c == NCH - 1 and gidx == 7:
                                C["w2_last"][e] = cnt
                            p.wait("vector", "sT", cnt)
                            xsl = xres[:, DM * tau + 512 * dc: DM * tau + 512 * (dc + 1)]
                            C["oadd"][(e, c, gidx)] = p.op(
                                "vector", lambda E, x=xsl, b2_=bank[:, :]: E.tensor_add(x, x, b2_),
                                inc="sV")
                            if e == NEXP - 1 and dc == 1:
                                w0 = C["oadd"][(e, c, tt)]
                                w1v = C["oadd"][(e, c, 4 + tt)]
                                p.wait("scalar", "sV", max(w0, w1v))
                                o_ = out_h[128 * tau:128 * (tau + 1), :]
                                i_ = xres[:, DM * tau:DM * (tau + 1)]
                                p.dma_op("scalar", "dO", lambda E, o=o_, i2=i_: E.dma_start(o, i2))
                if e + 1 < NEXP:
                    emit_w1_dma(e + 1, 0, C["h_last"][(e, 0)])
                    emit_w1_dma(e + 1, 1, C["h_last"][(e, 1)])
                    emit_w2_dma(e + 1, 0, C["w2_last"][e])
                    emit_w2_dma(e + 1, 1, C["w2_last"][e])

            p.wait("scalar", "dO", p.cnt["dO"])
            p.play_into(nc)

    return nc


_NC_CACHE = None


def kernel(L0_attn_x, L0_attn_wqkv, L0_ffn_Wg, L0_ffn_W1, L0_ffn_b1, L0_ffn_W2, L0_ffn_b2):
    global _NC_CACHE
    x = np.ascontiguousarray(np.asarray(L0_attn_x, dtype=np.float32))
    wqkv = np.ascontiguousarray(np.asarray(L0_attn_wqkv, dtype=np.float32))
    wg = np.asarray(L0_ffn_Wg, dtype=np.float32)
    wgt = np.ascontiguousarray(wg.T)
    w1 = np.ascontiguousarray(np.asarray(L0_ffn_W1, dtype=np.float32))
    b1 = np.ascontiguousarray(np.asarray(L0_ffn_b1, dtype=np.float32))
    w2 = np.ascontiguousarray(np.asarray(L0_ffn_W2, dtype=np.float32))
    b2 = np.ascontiguousarray(np.asarray(L0_ffn_b2, dtype=np.float32))

    i128f = np.eye(128, dtype=np.float32)
    i128b = np.eye(128, dtype=np.float32).astype(ml_dtypes.bfloat16)
    ustr = np.triu(np.ones((128, 128), dtype=np.float32), k=1).astype(ml_dtypes.bfloat16)
    ones1 = np.ones((1, 128), dtype=ml_dtypes.bfloat16)
    onesc = np.ones((128, 1), dtype=ml_dtypes.bfloat16)
    esel = np.zeros((NE, NE * 128), dtype=ml_dtypes.bfloat16)
    for e in range(NE):
        esel[e, 128 * e:128 * (e + 1)] = 1
    iota = (np.arange(NT, dtype=np.float32)[None, :] * 128.0
            + np.arange(128, dtype=np.float32)[:, None])

    if _NC_CACHE is None:
        _NC_CACHE = build_nc()
    nc = _NC_CACHE

    base = {
        "wqkv": wqkv, "wgt": wgt, "b1": b1, "b2": b2,
        "i128f": i128f, "i128b": i128b, "ustr": ustr, "ones1": ones1,
        "onesc": onesc, "esel": esel, "iota": np.ascontiguousarray(iota),
    }
    for e in range(NE):
        base["w1_%d" % e] = w1[e]
        base["w2_%d" % e] = w2[e]
    in_maps = [dict(base, x=x[bidx]) for bidx in range(NCORE)]
    res = run_bass_kernel_spmd(nc, in_maps, core_ids=list(range(NCORE)))
    global LAST_RES
    LAST_RES = res
    return np.stack([res.results[bidx]["out"] for bidx in range(NCORE)], axis=0)


# version 23
# speedup vs baseline: 1.0334x; 1.0004x over previous
import contextlib
import os

import numpy as np
import ml_dtypes
import concourse.bass as bass
import concourse.mybir as mybir
from concourse.bass_utils import run_bass_kernel_spmd

F32 = mybir.dt.float32
BF16 = mybir.dt.bfloat16
AF = mybir.ActivationFunctionType
ALU = mybir.AluOpType
AX = mybir.AxisListType

NCORE = 8
SEQ = 2048
DM = 1024
NT = 16
NB = 8
NH = 16
FQKV = 3072
NQC = 6
NE = 8
HID = 2048
NHB = 16
NCH = 4
EPS = 1e-5

_SEM_OWNER = {"sT": "tensor", "sV": "vector", "sS": "scalar"}


class Prog:
    ENG = ("tensor", "vector", "scalar", "gpsimd", "sync")

    def __init__(self):
        self.streams = {e: [] for e in self.ENG}
        self.sems = {}
        self.cnt = {}
        self.lastwait = {}

    def sem(self, name, handle):
        self.sems[name] = handle
        self.cnt[name] = 0

    def op(self, eng, fn, inc=None, inc_by=1):
        if inc is not None:
            self.cnt[inc] += inc_by
            h = self.sems[inc]
            self.streams[eng].append(lambda E, f=fn, h=h, k=inc_by: f(E).then_inc(h, k))
            return self.cnt[inc]
        self.streams[eng].append(fn)
        return None

    def wait(self, eng, sem, val):
        if val is None or val <= 0:
            return
        if _SEM_OWNER.get(sem) == eng:
            return
        assert val <= self.cnt[sem], (eng, sem, val, self.cnt[sem])
        key = (eng, sem)
        if self.lastwait.get(key, 0) >= val:
            return
        self.lastwait[key] = val
        h = self.sems[sem]
        self.streams[eng].append(lambda E, h=h, v=val: E.wait_ge(h, v))

    def dma_op(self, eng, sem, fn):
        self.wait(eng, sem, self.cnt[sem])
        return self.op(eng, fn, inc=sem, inc_by=16)

    def barrier_inc(self):
        h = self.sems["sB"]
        for e in self.ENG:
            if e == "sync":
                self.streams[e].append(lambda E, h=h: E.sem_inc(h, 1))
            else:
                self.streams[e].append(lambda E, h=h: E.drain().then_inc(h, 1))
        self.cnt["sB"] += 5

    def barrier_wait(self):
        h = self.sems["sB"]
        v = self.cnt["sB"]
        for e in self.ENG:
            self.streams[e].append(lambda E, h=h, v=v: E.wait_ge(h, v))

    def play_into(self, nc):
        with nc.Block() as block:
            streams = self.streams

            @block.tensor
            def _(E):
                for f in streams["tensor"]:
                    f(E)

            @block.vector
            def _(E):
                for f in streams["vector"]:
                    f(E)

            @block.scalar
            def _(E):
                for f in streams["scalar"]:
                    f(E)

            @block.gpsimd
            def _(E):
                for f in streams["gpsimd"]:
                    f(E)

            @block.sync
            def _(E):
                for f in streams["sync"]:
                    f(E)

        self.streams = {e: [] for e in self.ENG}
        self.lastwait = {}


def build_nc():
    nc = bass.Bass("TRN2", target_bir_lowering=False, detect_race_conditions=False)

    x_h = nc.dram_tensor("x", [SEQ, DM], F32, kind="ExternalInput").ap()
    wqkv_h = nc.dram_tensor("wqkv", [DM, FQKV], F32, kind="ExternalInput").ap()
    wgt_h = nc.dram_tensor("wgt", [NE, DM], F32, kind="ExternalInput").ap()
    b1_h = nc.dram_tensor("b1", [NE, HID], F32, kind="ExternalInput").ap()
    b2_h = nc.dram_tensor("b2", [NE, DM], F32, kind="ExternalInput").ap()
    w1_h = [nc.dram_tensor("w1_%d" % e, [DM, HID], F32, kind="ExternalInput").ap() for e in range(NE)]
    w2_h = [nc.dram_tensor("w2_%d" % e, [HID, DM], F32, kind="ExternalInput").ap() for e in range(NE)]
    i128f_h = nc.dram_tensor("i128f", [128, 128], F32, kind="ExternalInput").ap()
    i128b_h = nc.dram_tensor("i128b", [128, 128], BF16, kind="ExternalInput").ap()
    ustr_h = nc.dram_tensor("ustr", [128, 128], BF16, kind="ExternalInput").ap()
    ones1_h = nc.dram_tensor("ones1", [1, 128], BF16, kind="ExternalInput").ap()
    onesc_h = nc.dram_tensor("onesc", [128, 1], BF16, kind="ExternalInput").ap()
    esel_h = nc.dram_tensor("esel", [NE, NE * 128], BF16, kind="ExternalInput").ap()
    iota_h = nc.dram_tensor("iota", [128, NT], F32, kind="ExternalInput").ap()
    out_h = nc.dram_tensor("out", [SEQ, DM], F32, kind="ExternalOutput").ap()

    p = Prog()
    C = {}

    with contextlib.ExitStack() as stk:
        for s in ("sT", "sV", "sS", "sVf", "sSf", "dX", "dX2", "dW", "dO", "sB"):
            p.sem(s, stk.enter_context(nc.semaphore(s)))

        xres = stk.enter_context(nc.sbuf_tensor("xres", [128, NT * DM], F32))
        w1a = stk.enter_context(nc.sbuf_tensor("w1a", [128, NB * 1024], BF16))
        xnT = stk.enter_context(nc.sbuf_tensor("xnT", [128, NB * SEQ], BF16))
        scr32 = stk.enter_context(nc.sbuf_tensor("scr32", [128, DM], F32))
        xn2s = stk.enter_context(nc.sbuf_tensor("xn2s", [128, DM], F32))
        b1T = stk.enter_context(nc.sbuf_tensor("b1T", [128, 128], F32))
        wg_sb = stk.enter_context(nc.sbuf_tensor("wg_sb", [128, 64], F32))
        scoresT = stk.enter_context(nc.sbuf_tensor("scoresT", [8, SEQ], BF16))
        carryb = stk.enter_context(nc.sbuf_tensor("carryb", [1, DM], BF16))
        i128f = stk.enter_context(nc.sbuf_tensor("i128f_s", [128, 128], F32))
        i128b = stk.enter_context(nc.sbuf_tensor("i128b_s", [128, 128], BF16))
        ustr = stk.enter_context(nc.sbuf_tensor("ustr_s", [128, 128], BF16))
        ones1 = stk.enter_context(nc.sbuf_tensor("ones1_s", [1, 128], BF16))
        onesc = stk.enter_context(nc.sbuf_tensor("onesc_s", [128, 1], BF16))
        esel = stk.enter_context(nc.sbuf_tensor("esel_s", [NE, NE * 128], BF16))
        iota = stk.enter_context(nc.sbuf_tensor("iota_s", [128, NT], F32))
        sdot = stk.enter_context(nc.sbuf_tensor("sdot", [128, NH], F32))
        tmin = stk.enter_context(nc.sbuf_tensor("tmin", [128, NH], F32))
        tneg = stk.enter_context(nc.sbuf_tensor("tneg", [128, NH], F32))
        e_sb = stk.enter_context(nc.sbuf_tensor("e_sb", [128, NH], F32))
        z_sb = stk.enter_context(nc.sbuf_tensor("z_sb", [128, NH], F32))
        den = stk.enter_context(nc.sbuf_tensor("den", [128, NH], F32))
        rr = stk.enter_context(nc.sbuf_tensor("rr", [128, NH], F32))
        a_sb = stk.enter_context(nc.sbuf_tensor("a_sb", [128, NH], F32))
        bb_sb = stk.enter_context(nc.sbuf_tensor("bb_sb", [128, NH], F32))
        ln = {}
        for t in ("sum", "sumsq", "var", "lnv", "rstd", "negc"):
            for s in (2,):
                ln[(t, s)] = stk.enter_context(nc.sbuf_tensor("ln_%s_%d" % (t, s), [128, 1], F32))
        l1 = {}
        for t in ("sum", "sumsq", "d", "lnv", "rstd", "negc"):
            l1[t] = stk.enter_context(nc.sbuf_tensor("l1_%s" % t, [128, NT], F32))
        lgall = stk.enter_context(nc.sbuf_tensor("lgall", [128, 128], F32))
        gA = stk.enter_context(nc.sbuf_tensor("gA", [128, 128], F32))
        keepall = stk.enter_context(nc.sbuf_tensor("keepall", [128, 128], F32))
        exall = stk.enter_context(nc.sbuf_tensor("exall", [128, 128], F32))
        gm1a = stk.enter_context(nc.sbuf_tensor("gm1a", [128, NT], F32))
        gm2a = stk.enter_context(nc.sbuf_tensor("gm2a", [128, NT], F32))
        gnm1a = stk.enter_context(nc.sbuf_tensor("gnm1a", [128, NT], F32))
        gzsa = stk.enter_context(nc.sbuf_tensor("gzsa", [128, NT], F32))
        grza = stk.enter_context(nc.sbuf_tensor("grza", [128, NT], F32))

        pb = [stk.enter_context(nc.psum_tensor("pb%d" % i, [128, 512], F32)) for i in range(8)]

        def emit_ln(which, j, x_ap, out_ap, junk):
            key = "norm%d" % which
            s_sum = ln[("sum", which)]; s_sq = ln[("sumsq", which)]
            s_d = ln[("var", which)]
            s_lnv = ln[("lnv", which)]; s_rstd = ln[("rstd", which)]; s_negc = ln[("negc", which)]
            prev = C.setdefault(key, {}).get(j - 1)
            if prev:
                p.wait("vector", "sS", prev)
            p.op("vector", lambda E: E.reduce_sum(s_sum[:, :], x_ap, AX.X))
            f1 = p.op("vector", lambda E: E.scalar_tensor_tensor(
                junk[:, :], x_ap, 1.0, x_ap, ALU.mult, ALU.mult, accum_out=s_sq[:, :]),
                inc="sVf")
            p.wait("vector", "sVf", f1)
            f2 = p.op("vector", lambda E: E.tensor_scalar(
                s_d[:, :], s_sum[:, :], s_sum[:, :], 1.0 / DM, ALU.mult, ALU.mult), inc="sVf")
            p.wait("vector", "sVf", f2)
            vstats = p.op("vector", lambda E: E.scalar_tensor_tensor(
                s_d[:, :], s_d[:, :], -1.0, s_sq[:, :], ALU.mult, ALU.add), inc="sV")
            p.wait("scalar", "sV", vstats)
            g1 = p.op("scalar", lambda E: E.activation(
                s_lnv[:, :], s_d[:, :], AF.Ln,
                bias=np.float32(EPS), scale=np.float32(1.0 / DM)), inc="sSf")
            p.wait("scalar", "sSf", g1)
            srst = p.op("scalar", lambda E: E.activation(s_rstd[:, :], s_lnv[:, :], AF.Exp,
                                                         scale=np.float32(-0.5)),
                        inc="sS")
            p.wait("vector", "sS", srst)
            vneg = p.op("vector", lambda E: E.tensor_scalar(
                s_negc[:, :], s_sum[:, :], s_rstd[:, :], -1.0 / DM,
                ALU.mult, ALU.mult), inc="sV")
            p.wait("scalar", "sV", vneg)
            nrm = p.op("scalar", lambda E: E.activation(
                out_ap, x_ap, AF.Identity, bias=s_negc[:, :], scale=s_rstd[:, :]), inc="sS")
            C[key][j] = nrm
            return nrm

        with (
            nc.sbuf_tensor("wqkvsb", [128, NB * FQKV], BF16) as wqkvsb,
            nc.sbuf_tensor("qkvt", [128, 2 * FQKV], BF16) as qkvt,
            nc.sbuf_tensor("junk", [128, DM], BF16) as junk,
            nc.sbuf_tensor("xn1tm", [128, 2 * DM], F32) as xn1tm,
        ):
            wgtsb = xn2s
            qkvt_sl = [qkvt[:, FQKV * k:FQKV * (k + 1)] for k in range(2)]
            qkvt3s = [qkvt[:, FQKV * k:FQKV * (k + 1)].rearrange("p (h c) -> p h c", h=NH)
                      for k in range(2)]
            xnT3 = xnT[:, :].rearrange("p (b t) -> p b t", b=NB)

            C["dx_x"] = {}
            C["dx_sem"] = {}
            for j in range(4):
                src = x_h[128 * j:128 * (j + 1), :]
                dst = xres[:, DM * j:DM * (j + 1)]
                if j % 2 == 0:
                    eng, sem = "sync", "dX"
                else:
                    eng, sem = "gpsimd", "dW"
                C["dx_x"][j] = p.dma_op(eng, sem, lambda E, s=src, d=dst: E.dma_start(d, s))
                C["dx_sem"][j] = sem
            for src, dst in ((i128f_h, i128f), (i128b_h, i128b), (ustr_h, ustr),
                             (ones1_h, ones1), (onesc_h, onesc), (iota_h, iota), (esel_h, esel)):
                p.dma_op("sync", "dX", lambda E, s=src, d=dst: E.dma_start(d[:, :], s))
            p.dma_op("sync", "dX", lambda E: E.dma_start(wgtsb[0:8, :], wgt_h))
            C["dx_consts"] = p.cnt["dX"]
            for j in range(4, 8):
                src = x_h[128 * j:128 * (j + 1), :]
                dst = xres[:, DM * j:DM * (j + 1)]
                sem = "dX" if j % 2 == 0 else "dX2"
                C["dx_x"][j] = p.dma_op("sync", sem, lambda E, s=src, d=dst: E.dma_start(d, s))
                C["dx_sem"][j] = sem
            
            for j in range(8, NT):
                src = x_h[128 * j:128 * (j + 1), :]
                dst = xres[:, DM * j:DM * (j + 1)]
                C["dx_x"][j] = p.dma_op("scalar", "dO", lambda E, s=src, d=dst: E.dma_start(d, s))
                C["dx_sem"][j] = "dO"

            wq_src = wqkv_h.rearrange("(b p) f -> p b f", p=128)
            wq_dst = wqkvsb[:, :].rearrange("p (b f) -> p b f", b=NB)
            C["dw_wqkv"] = {}
            for c in range(NQC):
                s = wq_src[:, :, 512 * c:512 * (c + 1)]
                d = wq_dst[:, :, 512 * c:512 * (c + 1)]
                C["dw_wqkv"][c] = p.dma_op("gpsimd", "dW", lambda E, s=s, d=d: E.dma_start(d, s))
            C["dw_w1"] = {}
            s0 = w1_h[0].rearrange("(b p) h -> p b h", p=128)[:, :, 0:1024]
            w1a3 = w1a[:, :].rearrange("p (b h) -> p b h", b=NB)
            C["dw_w1"][(0, 0)] = p.dma_op("gpsimd", "dW", lambda E: E.dma_start(w1a3, s0))

            p.wait("tensor", "dX", C["dx_consts"])
            for b in range(NB):
                i_ = wgtsb[0:8, 128 * b:128 * (b + 1)]
                o_ = pb[4][:, 8 * b:8 * (b + 1)]
                last = p.op("tensor", lambda E, o=o_, i2=i_: E.transpose(o, i2, i128f[0:8, 0:8]),
                            inc="sT" if b == NB - 1 else None)
            C["wgT"] = last
            p.wait("vector", "sT", C["wgT"])
            C["wg_evac"] = p.op("vector", lambda E: E.tensor_copy(wg_sb[:, :], pb[4][:, 0:64]),
                                inc="sV")

            for name in ("t1a", "t1b", "xn1T_evac", "qkv_mm", "qkv_evac", "ab", "cums",
                         "carryF", "combS", "combV1", "combV2", "prepTr", "xn2f32", "pb5free",
                         "xn2bf", "logits", "lgc"):
                C[name] = {}

            C["carry0"] = p.op("vector", lambda E: E.memset(carryb[:, :], 0), inc="sV")

            qkv_banks = [pb[0], pb[1]]

            C["l1negc"] = {}
            GRPS = [(0, 2), (2, 4), (4, 8), (8, 12), (12, 16)]
            C["tile2grp"] = {}
            for g, (glo, ghi) in enumerate(GRPS):
                for j in range(glo, ghi):
                    C["tile2grp"][j] = g
                fX = None
                for j in range(glo, ghi):
                    p.wait("vector", C["dx_sem"][j], C["dx_x"][j])
                    x_t = xres[:, DM * j:DM * (j + 1)]
                    p.op("vector", lambda E, x=x_t, jj=j: E.reduce_sum(
                        l1["sum"][:, jj:jj + 1], x, AX.X))
                    fX = p.op("vector", lambda E, x=x_t, jj=j: E.scalar_tensor_tensor(
                        junk[:, :], x, 1.0, x, ALU.mult, ALU.mult,
                        accum_out=l1["sumsq"][:, jj:jj + 1]), inc="sVf")
                p.wait("vector", "sVf", fX)
                sl0, sl1 = glo, ghi
                f2 = p.op("vector", lambda E, a=sl0, b=sl1: E.tensor_mul(
                    l1["d"][:, a:b], l1["sum"][:, a:b], l1["sum"][:, a:b]), inc="sVf")
                p.wait("vector", "sVf", f2)
                vst = p.op("vector", lambda E, a=sl0, b=sl1: E.scalar_tensor_tensor(
                    l1["d"][:, a:b], l1["d"][:, a:b], -1.0 / DM, l1["sumsq"][:, a:b],
                    ALU.mult, ALU.add), inc="sV")
                p.wait("scalar", "sV", vst)
                g1 = p.op("scalar", lambda E, a=sl0, b=sl1: E.activation(
                    l1["lnv"][:, a:b], l1["d"][:, a:b], AF.Ln,
                    bias=np.float32(EPS), scale=np.float32(1.0 / DM)), inc="sSf")
                p.wait("scalar", "sSf", g1)
                g2 = p.op("scalar", lambda E, a=sl0, b=sl1: E.activation(
                    l1["rstd"][:, a:b], l1["lnv"][:, a:b], AF.Exp,
                    scale=np.float32(-0.5)), inc="sS")
                p.wait("vector", "sS", g2)
                f3 = p.op("vector", lambda E, a=sl0, b=sl1: E.tensor_mul(
                    l1["negc"][:, a:b], l1["sum"][:, a:b], l1["rstd"][:, a:b]), inc="sVf")
                p.wait("vector", "sVf", f3)
                C["l1negc"][g] = p.op("vector", lambda E, a=sl0, b=sl1: E.tensor_scalar_mul(
                    l1["negc"][:, a:b], l1["negc"][:, a:b], -1.0 / DM), inc="sV")

            LIM = int(os.environ.get("KMK_NTI", str(NT)))
            ST = os.environ.get("KMK_ST", "I")
            def emit_gate(tlo, thi):
                for i in range(tlo, thi):
                    p.op("vector", lambda E, ii=i: E.reduce_max(
                        gm1a[:, ii:ii + 1], lgall[:, 8 * ii:8 * (ii + 1)], AX.X))
                fg = p.op("vector", lambda E: E.engine_nop(), inc="sVf")
                p.wait("vector", "sVf", fg)
                for i in range(tlo, thi):
                    p.op("vector", lambda E, ii=i: E.tensor_scalar(
                        gA[:, 8 * ii:8 * (ii + 1)], lgall[:, 8 * ii:8 * (ii + 1)],
                        gm1a[:, ii:ii + 1], None, ALU.is_equal))
                fg = p.op("vector", lambda E: E.engine_nop(), inc="sVf")
                p.wait("vector", "sVf", fg)
                for i in range(tlo, thi):
                    p.op("vector", lambda E, ii=i: E.scalar_tensor_tensor(
                        gA[:, 8 * ii:8 * (ii + 1)], gA[:, 8 * ii:8 * (ii + 1)], -1e30,
                        lgall[:, 8 * ii:8 * (ii + 1)], ALU.mult, ALU.add))
                fg = p.op("vector", lambda E: E.engine_nop(), inc="sVf")
                p.wait("vector", "sVf", fg)
                for i in range(tlo, thi):
                    p.op("vector", lambda E, ii=i: E.reduce_max(
                        gm2a[:, ii:ii + 1], gA[:, 8 * ii:8 * (ii + 1)], AX.X))
                fg = p.op("vector", lambda E: E.engine_nop(), inc="sVf")
                p.wait("vector", "sVf", fg)
                for i in range(tlo, thi):
                    p.op("vector", lambda E, ii=i: E.tensor_scalar(
                        keepall[:, 8 * ii:8 * (ii + 1)], lgall[:, 8 * ii:8 * (ii + 1)],
                        gm2a[:, ii:ii + 1], None, ALU.is_ge))
                g1 = p.op("vector", lambda E, a=tlo, b=thi: E.tensor_scalar_mul(
                    gnm1a[:, a:b], gm1a[:, a:b], -1.0), inc="sV")
                p.wait("scalar", "sV", g1)
                for i in range(tlo, thi):
                    cnt = p.op("scalar", lambda E, ii=i: E.activation(
                        exall[:, 8 * ii:8 * (ii + 1)], lgall[:, 8 * ii:8 * (ii + 1)],
                        AF.Exp, bias=gnm1a[:, ii:ii + 1]),
                        inc="sS" if i == thi - 1 else None)
                p.wait("vector", "sS", cnt)
                fg = p.op("vector", lambda E, a=tlo, b=thi: E.tensor_mul(
                    exall[:, 8 * a:8 * b], exall[:, 8 * a:8 * b], keepall[:, 8 * a:8 * b]),
                    inc="sVf")
                p.wait("vector", "sVf", fg)
                for i in range(tlo, thi):
                    p.op("vector", lambda E, ii=i: E.reduce_sum(
                        gzsa[:, ii:ii + 1], exall[:, 8 * ii:8 * (ii + 1)], AX.X))
                fg = p.op("vector", lambda E: E.engine_nop(), inc="sVf")
                p.wait("vector", "sVf", fg)
                fg = p.op("vector", lambda E, a=tlo, b=thi: E.reciprocal(
                    grza[:, a:b], gzsa[:, a:b]), inc="sVf")
                p.wait("vector", "sVf", fg)
                for i in range(tlo, thi):
                    cnt = p.op("vector", lambda E, ii=i: E.tensor_scalar(
                        gA[:, 8 * ii:8 * (ii + 1)], exall[:, 8 * ii:8 * (ii + 1)],
                        grza[:, ii:ii + 1], None, ALU.mult),
                        inc="sV" if i == thi - 1 else None)
                return cnt

            def emit_norm1(jj):
                x_t2 = xres[:, DM * jj:DM * (jj + 1)]
                if jj >= 2:
                    p.wait("scalar", "sT", C["t1b"][jj - 2])
                p.wait("scalar", "sV", C["l1negc"][C["tile2grp"][jj]])
                C.setdefault("norm1", {})[jj] = p.op(
                    "scalar", lambda E, x=x_t2, j3=jj:
                    E.activation(xn1tm[:, DM * (j3 % 2):DM * (j3 % 2 + 1)], x, AF.Identity,
                                 bias=l1["negc"][:, j3:j3 + 1],
                                 scale=l1["rstd"][:, j3:j3 + 1]), inc="sS")

            for it in range(LIM + 1):
                j = it
                i = it - 1
                if j < LIM:
                    x_t = xres[:, DM * j:DM * (j + 1)]
                    if j == 0:
                        emit_norm1(0)
                    if ST < "B":
                        continue
                    p.wait("tensor", "sS", C["norm1"][j])
                    for b in range(NB):
                        if b == 0 and i - 1 >= 0:
                            p.wait("tensor", "sS", C["pb5free"][i - 1])
                        if b == 4 and i - 1 >= 0:
                            p.wait("tensor", "sS", C["xn2bf"][i - 1])
                        bank = pb[5] if b < 4 else pb[6]
                        o_ = bank[:, 128 * (b % 4):128 * (b % 4 + 1)]
                        i_ = xn1tm[:, DM * (j % 2) + 128 * b:DM * (j % 2) + 128 * (b + 1)]
                        cnt = p.op("tensor", lambda E, o=o_, i2=i_: E.transpose(o, i2, i128f[:, :]),
                                   inc="sT" if b in (3, 7) else None)
                        if b == 3:
                            C["t1a"][j] = cnt
                        elif b == 7:
                            C["t1b"][j] = cnt
                    for half in range(2):
                        p.wait("scalar", "sT", C["t1a"][j] if half == 0 else C["t1b"][j])
                        bank = pb[5] if half == 0 else pb[6]
                        i_ = bank[:, 0:512].rearrange("p (b t) -> p b t", b=4)
                        o_ = xnT3[:, 4 * half:4 * (half + 1), 128 * j:128 * (j + 1)]
                        cnt = p.op("scalar", lambda E, o=o_, i2=i_: E.copy(o, i2),
                                   inc="sS" if half == 1 else None)
                    C["xn1T_evac"][j] = cnt
                    if j + 1 < LIM:
                        emit_norm1(j + 1)

                if i >= 0 and ST >= "D":
                    p.wait("vector", "sS", C["qkv_evac"][NQC * i + NQC - 1])
                    if i >= 1:
                        p.wait("vector", "sS", C["combS"][i - 1])
                    q3 = qkvt3s[i % 2]
                    p.op("vector", lambda E, q3=q3: E.scalar_tensor_tensor(
                        q3[:, :, 0:64], q3[:, :, 0:64], 1.0, q3[:, :, 64:128],
                        ALU.mult, ALU.mult))
                    fD = p.op("vector", lambda E, q3=q3: E.tensor_reduce(
                        sdot[:, :], q3[:, :, 0:64], AX.X, ALU.add), inc="sVf")
                    p.wait("vector", "sVf", fD)
                    p.op("vector", lambda E: E.tensor_scalar(
                        tmin[:, :], sdot[:, :], 0.125, 0.0, ALU.mult, ALU.min))
                    dmm = p.op("vector", lambda E: E.tensor_scalar(
                        tneg[:, :], sdot[:, :], -0.125, 0.0, ALU.mult, ALU.min), inc="sV")
                    p.wait("scalar", "sV", dmm)
                    p.op("scalar", lambda E: E.activation(e_sb[:, :], tmin[:, :], AF.Exp))
                    eS = p.op("scalar", lambda E: E.activation(z_sb[:, :], tneg[:, :], AF.Exp),
                              inc="sS")
                    p.wait("vector", "sS", eS)
                    fD2 = p.op("vector", lambda E, ii=i: E.scalar_tensor_tensor(
                        den[:, :], z_sb[:, :], iota[:, ii:ii + 1], e_sb[:, :], ALU.mult, ALU.add),
                        inc="sVf")
                    p.wait("vector", "sVf", fD2)
                    fD3 = p.op("vector", lambda E: E.reciprocal(rr[:, :], den[:, :]), inc="sVf")
                    p.wait("vector", "sVf", fD3)
                    p.op("vector", lambda E: E.tensor_mul(a_sb[:, :], z_sb[:, :], rr[:, :]))
                    C["ab"][i] = p.op("vector", lambda E: E.tensor_mul(
                        bb_sb[:, :], e_sb[:, :], rr[:, :]), inc="sV")
                    p.wait("tensor", "sS", C["qkv_evac"][NQC * i + NQC - 1])
                    p.wait("tensor", "sV", C["carryF"].get(i - 1, C["carry0"]))
                    if i >= 1:
                        p.wait("tensor", "sV", C["combV2"][i - 1])
                    for half in range(2):
                        bank = pb[2 + half]
                        v_ap = qkvt3s[i % 2][:, 8 * half:8 * (half + 1), 128:192]
                        c_ap = carryb[0:1, 512 * half:512 * (half + 1)]
                        p.op("tensor", lambda E, o=bank[:, :], v=v_ap:
                             E.matmul(o, ustr[:, :], v, start=True, stop=False))
                        cnt = p.op("tensor", lambda E, o=bank[:, :], cc=c_ap:
                                   E.matmul(o, ones1[:, :], cc, start=False, stop=True), inc="sT")
                    C["cums"][i] = cnt
                    if ST < "F":
                        continue
                    for half in range(2):
                        v_ap = qkvt3s[i % 2][:, 8 * half:8 * (half + 1), 128:192]
                        if half == 1:
                            p.wait("tensor", "sV", C["colA0"])
                        colT = p.op("tensor", lambda E, v=v_ap:
                                    E.matmul(pb[7][0:1, :], onesc[:, :], v, start=True, stop=True),
                                    inc="sT")
                        C.setdefault("colT", {})[i] = colT
                        p.wait("vector", "sT", colT)
                        c_ap = carryb[0:1, 512 * half:512 * (half + 1)]
                        cnt = p.op("vector", lambda E, cc=c_ap: E.tensor_add(cc, pb[7][0:1, :], cc),
                                   inc="sV")
                        if half == 0:
                            C["colA0"] = cnt
                    C["carryF"][i] = cnt
                    if ST < "G":
                        continue
                    p.wait("scalar", "sT", C["cums"][i])
                    p.wait("scalar", "sV", C["carryF"][i])
                    p.wait("scalar", "sV", C["ab"][i])
                    combs_cnt = {}
                    for h in range(NH):
                        bank = pb[2] if h < 8 else pb[3]
                        sl = bank[:, 64 * (h % 8):64 * (h % 8 + 1)]
                        combs_cnt[h] = p.op("scalar", lambda E, s=sl, hh=h: E.activation(
                            s, s, AF.Copy, scale=a_sb[:, hh:hh + 1]), inc="sS")
                    C["combS"][i] = combs_cnt[NH - 1]
                    for h in range(NH):
                        p.wait("vector", "sS", combs_cnt[7] if h < 8 else combs_cnt[15])
                        bank = pb[2] if h < 8 else pb[3]
                        sl = bank[:, 64 * (h % 8):64 * (h % 8 + 1)]
                        v_ap = qkvt_sl[i % 2][:, 192 * h + 128:192 * h + 192]
                        cnt = p.op("vector", lambda E, s=sl, v=v_ap, hh=h: E.scalar_tensor_tensor(
                            s, v, bb_sb[:, hh:hh + 1], s, ALU.mult, ALU.add),
                            inc="sV" if h == NH - 1 else None)
                    C["combV1"][i] = cnt
                    fC = p.op("vector", lambda E: E.engine_nop(), inc="sVf")
                    p.wait("vector", "sVf", fC)
                    for half in range(2):
                        bank = pb[2 + half]
                        xsl = xres[:, DM * i + 512 * half: DM * i + 512 * (half + 1)]
                        cnt = p.op("vector", lambda E, x=xsl, b2_=bank[:, :]:
                                   E.tensor_add(x, x, b2_), inc="sV" if half == 1 else None)
                    C["combV2"][i] = cnt

                if j < LIM and ST >= "C":
                    if j >= 2 and ST >= "G":
                        p.wait("scalar", "sT", C["colT"][j - 2])
                        p.wait("scalar", "sV", C["combV1"][j - 2])
                    for c in range(NQC):
                        g = NQC * j + c
                        bank = qkv_banks[g % 2]
                        p.wait("tensor", "dW", C["dw_wqkv"][c])
                        p.wait("tensor", "sS", C["xn1T_evac"][j])
                        if g - 2 >= 0:
                            p.wait("tensor", "sS", C["qkv_evac"][g - 2])
                        for b in range(NB):
                            l_ = xnT[:, SEQ * b + 128 * j: SEQ * b + 128 * (j + 1)]
                            r_ = wqkvsb[:, FQKV * b + 512 * c: FQKV * b + 512 * (c + 1)]
                            cnt = p.op("tensor", lambda E, o=bank[:, :], l=l_, r=r_,
                                       st=(b == 0), sp=(b == NB - 1):
                                       E.matmul(o, l, r, start=st, stop=sp),
                                       inc="sT" if b == NB - 1 else None)
                        C["qkv_mm"][(j, c)] = cnt
                        p.wait("scalar", "sT", cnt)
                        o_ = qkvt_sl[j % 2][:, 512 * c:512 * (c + 1)]
                        C["qkv_evac"][g] = p.op("scalar", lambda E, o=o_, i2=bank[:, :]:
                                                E.copy(o, i2), inc="sS")

                if i >= 0 and ST >= "H":
                    if i >= 1:
                        p.wait("scalar", "sT", C["prepTr"][i - 1])
                    x_i = xres[:, DM * i:DM * (i + 1)]
                    emit_ln(2, i, x_i, scr32[:, :], junk)
                    if ST < "I":
                        continue
                    p.wait("tensor", "sS", C["norm2"][i])
                    p.wait("tensor", "sS", C["xn1T_evac"].get(i + 1, C["xn1T_evac"][i]))
                    for b in range(NB):
                        bank = pb[5] if b < 4 else pb[6]
                        o_ = bank[:, 128 * (b % 4):128 * (b % 4 + 1)]
                        i_ = scr32[:, 128 * b:128 * (b + 1)]
                        cnt = p.op("tensor", lambda E, o=o_, i2=i_: E.transpose(o, i2, i128f[:, :]),
                                   inc="sT" if b == NB - 1 else None)
                    C["prepTr"][i] = cnt
                    p.wait("scalar", "sT", C["prepTr"][i])
                    if i >= 1:
                        p.wait("scalar", "sT", C["logits"][i - 1])
                    else:
                        p.wait("scalar", "sT", C["wgT"])
                    p.op("scalar", lambda E: E.copy(xn2s[:, 0:512], pb[5][:, 0:512]))
                    o_ = xnT3[:, 0:4, 128 * i:128 * (i + 1)]
                    i_ = pb[5][:, 0:512].rearrange("p (b t) -> p b t", b=4)
                    c5 = p.op("scalar", lambda E, o=o_, i2=i_: E.copy(o, i2), inc="sS")
                    cF = p.op("scalar", lambda E: E.copy(xn2s[:, 512:1024], pb[6][:, 0:512]),
                              inc="sS")
                    o_ = xnT3[:, 4:8, 128 * i:128 * (i + 1)]
                    i_ = pb[6][:, 0:512].rearrange("p (b t) -> p b t", b=4)
                    cnt = p.op("scalar", lambda E, o=o_, i2=i_: E.copy(o, i2), inc="sS")
                    C["pb5free"][i] = c5
                    C["xn2bf"][i] = cnt
                    p.wait("tensor", "sS", cF)
                    p.wait("tensor", "sV", C["lgc"].get(i - 1, C["wg_evac"]))
                    for b in range(NB):
                        l_ = xn2s[:, 128 * b:128 * (b + 1)]
                        r_ = wg_sb[:, 8 * b:8 * (b + 1)]
                        cnt = p.op("tensor", lambda E, l=l_, r=r_, st=(b == 0), sp=(b == NB - 1):
                                   E.matmul(pb[4][:, 0:8], l, r, start=st, stop=sp),
                                   inc="sT" if b == NB - 1 else None)
                    C["logits"][i] = cnt
                    p.wait("vector", "sT", C["logits"][i])
                    C["lgc"][i] = p.op("vector", lambda E, ii=i: E.tensor_copy(
                        lgall[:, 8 * ii:8 * (ii + 1)], pb[4][:, 0:8]), inc="sV")
                    if i == 7 and LIM == NT:
                        C["gate2a"] = emit_gate(0, 8)
                    if i == 11 and LIM == NT:
                        C["gate2b1"] = emit_gate(8, 12)

            if LIM == NT:
                gate2all = max(C["gate2a"], C["gate2b1"], emit_gate(12, NT))
            elif LIM >= 1:
                gate2all = emit_gate(0, LIM)
            else:
                gate2all = 0
            p.wait("tensor", "sV", gate2all)
            if LIM >= 1:
                p.wait("tensor", "sV", C["lgc"][LIM - 1])
            if LIM >= 1 and (LIM - 1) in C["xn2bf"]:
                p.wait("tensor", "sS", C["xn2bf"][LIM - 1])
                p.wait("tensor", "sV", C["carryF"][LIM - 1])
            sco_tr = {}
            for i in range(LIM):
                bank = pb[4 + i // 4]
                o_ = bank[0:8, 128 * (i % 4):128 * (i % 4 + 1)]
                i_ = gA[:, 8 * i:8 * (i + 1)]
                cnt = p.op("tensor", lambda E, o=o_, i2=i_: E.transpose(o, i2, i128f[:, :]),
                           inc="sT" if i % 4 == 3 else None)
                if i % 4 == 3:
                    sco_tr[i // 4] = cnt
            scoresT3 = scoresT[0:8, :].rearrange("e (t p) -> e t p", t=NT)
            for k in range(LIM // 4):
                p.wait("vector", "sT", sco_tr[k])
                o_ = scoresT3[0:8, 4 * k:4 * (k + 1), :]
                i_ = pb[4 + k][0:8, 0:512].rearrange("e (t p) -> e t p", t=4)
                p.op("vector", lambda E, o=o_, i2=i_: E.tensor_copy(o, i2), inc="sV")

            if os.environ.get("KMK_PH", "ABC") != "ABC":
                p.barrier_inc()
            else:
                C["ph_sT"] = p.cnt["sT"]
                C["ph_sV"] = p.cnt["sV"]
                C["ph_sS"] = p.cnt["sS"]
                C["sco_last"] = p.cnt["sV"]
            p.play_into(nc)

        if os.environ.get("KMK_DBG"):
            dbg_xnT = nc.dram_tensor("dbg_xnT", [128, NB * SEQ], F32, kind="ExternalOutput").ap()
            dbg_qkvt = nc.dram_tensor("dbg_qkvt", [128, FQKV], F32, kind="ExternalOutput").ap()
            dbg_misc = nc.dram_tensor("dbg_misc", [128, 128], F32, kind="ExternalOutput").ap()
            dbg_xn1 = nc.dram_tensor("dbg_xn1", [128, DM], F32, kind="ExternalOutput").ap()
            dbg_carry = nc.dram_tensor("dbg_carry", [1, DM], F32, kind="ExternalOutput").ap()
            misc_srcs = [sdot, tmin, tneg, e_sb, z_sb, den, rr, a_sb]
            p.wait("gpsimd", "sT", p.cnt["sT"])
            p.wait("gpsimd", "sV", p.cnt["sV"])
            p.wait("gpsimd", "sS", p.cnt["sS"])
            p.dma_op("gpsimd", "dW", lambda E: E.dma_start(dbg_xnT, xnT[:, :]))
            p.dma_op("gpsimd", "dW", lambda E: E.dma_start(dbg_qkvt, qkvt[:, 0:FQKV]))
            p.dma_op("gpsimd", "dW", lambda E: E.dma_start(dbg_xn1, xn1tm[:, :]))
            p.dma_op("gpsimd", "dW", lambda E: E.dma_start(dbg_carry, carryb[:, :]))
            for mi, msrc in enumerate(misc_srcs):
                p.dma_op("gpsimd", "dW",
                         lambda E, o=dbg_misc[0:128, 16 * mi:16 * (mi + 1)], s=msrc:
                         E.dma_start(o, s[:, :]))
            dbg_ln = nc.dram_tensor("dbg_ln", [128, 12], F32, kind="ExternalOutput").ap()
            ln_srcs = [ln[(t, s)] for s in (1, 2)
                       for t in ("sum", "sumsq", "var", "lnv", "rstd", "negc")]
            for mi, msrc in enumerate(ln_srcs):
                p.dma_op("gpsimd", "dW",
                         lambda E, o=dbg_ln[0:128, mi:mi + 1], s=msrc:
                         E.dma_start(o, s[:, :], allow_slow_non_contiguous=True))
            p.wait("gpsimd", "dW", p.cnt["dW"])

        _PH = os.environ.get("KMK_PH", "ABC")
        if _PH == "A":
            p.barrier_wait()
            for tau in range(NT):
                o_ = out_h[128 * tau:128 * (tau + 1), :]
                i_ = xres[:, DM * tau:DM * (tau + 1)]
                p.dma_op("scalar", "dO", lambda E, o=o_, i2=i_: E.dma_start(o, i2))
            p.wait("scalar", "dO", 16 * NT)
            p.play_into(nc)
            return nc

        with (
            nc.sbuf_tensor("w2sb", [128, NHB * DM], BF16) as w2sb,
            nc.sbuf_tensor("hs", [128, NHB * 512], BF16) as hs,
            nc.sbuf_tensor("sbc", [128, SEQ], BF16) as sbc,
            nc.sbuf_tensor("b2sb", [8, DM], BF16) as b2sb,
            nc.sbuf_tensor("w1b", [128, NB * 1024], BF16) as w1b,
        ):
            if os.environ.get("KMK_PH", "ABC") != "ABC":
                p.barrier_wait()
            else:
                LL = LIM - 1
                p.wait("gpsimd", "sT", C["logits"][LL])
                p.wait("gpsimd", "sV", C["combV1"][LL])
                p.wait("gpsimd", "sS", C["xn2bf"][LL])
                p.wait("sync", "sT", C["logits"][LL])
                p.wait("tensor", "sV", C["sco_last"])
                p.wait("tensor", "sS", C["xn2bf"][LL])
            for half in range(2):
                dxb = p.dma_op("sync", "dX", lambda E, h2=half: E.dma_start(
                    xn2s[0:8, :], b1_h[0:8, 1024 * h2:1024 * (h2 + 1)]))
                p.wait("tensor", "dX", dxb)
                for k in range(8):
                    kk = 8 * half + k
                    i_ = xn2s[0:8, 128 * k:128 * (k + 1)]
                    o_ = pb[5][:, 8 * kk:8 * (kk + 1)]
                    cnt = p.op("tensor", lambda E, o=o_, i2=i_: E.transpose(o, i2, i128f[0:8, 0:8]),
                               inc="sT")
                if half == 0:
                    p.wait("sync", "sT", cnt)
            C["b1Tr"] = cnt
            p.wait("vector", "sT", C["b1Tr"])
            C["b1Te"] = p.op("vector", lambda E: E.tensor_copy(b1T[:, :], pb[5][:, 0:128]), inc="sV")
            w1_dst = [w1a[:, :].rearrange("p (b h) -> p b h", b=NB),
                      w1b[:, :].rearrange("p (b h) -> p b h", b=NB)]
            s = w1_h[0].rearrange("(b p) h -> p b h", p=128)[:, :, 1024:2048]
            C["dw_w1"][(0, 1)] = p.dma_op("gpsimd", "dW", lambda E, s=s: E.dma_start(w1_dst[1], s))
            C["dw_b2"] = p.dma_op("gpsimd", "dW", lambda E: E.dma_start(b2sb[:, :], b2_h))
            w2_dst3 = w2sb[:, :].rearrange("p (k d) -> p k d", k=NHB)
            C["dw_w2"] = {}

            def emit_w2_dma(e, dc, wait_sT):
                s = w2_h[e].rearrange("(k p) d -> p k d", p=128)[:, :, 512 * dc:512 * (dc + 1)]
                d = w2_dst3[:, :, 512 * dc:512 * (dc + 1)]
                p.wait("gpsimd", "sT", wait_sT)
                C["dw_w2"][(e, dc)] = p.dma_op("gpsimd", "dW", lambda E, s=s, d=d: E.dma_start(d, s))

            def emit_w1_dma(e, half, wait_sT):
                s = w1_h[e].rearrange("(b p) h -> p b h", p=128)[:, :, 1024 * half:1024 * (half + 1)]
                p.wait("gpsimd", "sT", wait_sT)
                C["dw_w1"][(e, half)] = p.dma_op("gpsimd", "dW",
                                                 lambda E, s=s, h2=half: E.dma_start(w1_dst[h2], s))

            emit_w2_dma(0, 0, 0)
            emit_w2_dma(0, 1, 0)

            for name in ("relu", "hsmul", "w2mm", "oadd", "sbc_mm", "sbc_ev",
                         "h_last", "w2_last"):
                C[name] = {}

            NEXP = int(os.environ.get("KMK_NE", str(NE)))
            for e in range(NEXP):
                p.wait("tensor", "sV", C["b1Te"])
                for c4 in range(4):
                    bank = pb[4 + (c4 % 2)]
                    prior = (e, c4 - 2) if c4 >= 2 else ((e - 1, c4 + 2) if e >= 1 else None)
                    if prior is not None:
                        p.wait("tensor", "sV", C["sbc_ev"][prior])
                    l_ = esel[0:8, 128 * e:128 * (e + 1)]
                    r_ = scoresT[0:8, 512 * c4:512 * (c4 + 1)]
                    C["sbc_mm"][(e, c4)] = p.op("tensor", lambda E, o=bank[:, :], l=l_, r=r_:
                                                E.matmul(o, l, r, start=True, stop=True),
                                                inc="sT")
                    p.wait("vector", "sT", C["sbc_mm"][(e, c4)])
                    o_ = sbc[:, 512 * c4:512 * (c4 + 1)]
                    C["sbc_ev"][(e, c4)] = p.op("vector", lambda E, o=o_, i2=bank[:, :]:
                                                E.tensor_copy(o, i2), inc="sV")
                    if c4 == 3:
                        fsb = p.op("vector", lambda E: E.engine_nop(), inc="sVf")
                        p.wait("vector", "sVf", fsb)

                for c in range(NCH):
                    for k in range(NHB):
                        bank = pb[k % 2]
                        if k == 0:
                            p.wait("tensor", "dW", C["dw_w1"][(e, 0)])
                        if k == 8:
                            p.wait("tensor", "dW", C["dw_w1"][(e, 1)])
                        if k >= 2:
                            p.wait("tensor", "sS", C["relu"][(e, c, k - 2)])
                        elif c > 0:
                            p.wait("tensor", "sS", C["relu"][(e, c - 1, NHB - 2 + k)])
                        elif e > 0:
                            p.wait("tensor", "sS", C["relu"][(e - 1, NCH - 1, NHB - 2 + k)])
                        w1x = w1a if k < 8 else w1b
                        kk = k % 8
                        for b in range(NB):
                            l_ = w1x[:, 1024 * b + 128 * kk: 1024 * b + 128 * (kk + 1)]
                            r_ = xnT[:, SEQ * b + 512 * c: SEQ * b + 512 * (c + 1)]
                            cnt = p.op("tensor", lambda E, o=bank[:, :], l=l_, r=r_,
                                       st=(b == 0), sp=(b == NB - 1):
                                       E.matmul(o, l, r, start=st, stop=sp),
                                       inc="sT" if b == NB - 1 else None)
                        if c == NCH - 1 and k == 7:
                            C["h_last"][(e, 0)] = cnt
                        if c == NCH - 1 and k == NHB - 1:
                            C["h_last"][(e, 1)] = cnt
                        p.wait("scalar", "sT", cnt)
                        o_ = hs[:, 512 * k:512 * (k + 1)]
                        C["relu"][(e, c, k)] = p.op(
                            "scalar",
                            lambda E, o=o_, i2=bank[:, :], bi=b1T[:, 8 * k + e:8 * k + e + 1]:
                            E.activation(o, i2, AF.Relu, bias=bi), inc="sS")
                        p.wait("vector", "sS", C["relu"][(e, c, k)])
                        sb_ap = sbc[:, 512 * c:512 * (c + 1)]
                        C["hsmul"][(e, c, k)] = p.op(
                            "vector", lambda E, o=o_, s=sb_ap: E.tensor_tensor(o, o, s, ALU.mult),
                            inc="sV")
                    if e == 0 and c == 0:
                        p.wait("tensor", "dW", C["dw_b2"])
                    for dc in range(2):
                        p.wait("tensor", "dW", C["dw_w2"][(e, dc)])
                        for tt in range(4):
                            gidx = dc * 4 + tt
                            bank = pb[2 + (gidx % 2)]
                            tau = 4 * c + tt
                            if gidx >= 2:
                                p.wait("tensor", "sV", C["oadd"][(e, c, gidx - 2)])
                            elif c > 0:
                                p.wait("tensor", "sV", C["oadd"][(e, c - 1, 6 + gidx)])
                            elif e > 0:
                                p.wait("tensor", "sV", C["oadd"][(e - 1, NCH - 1, 6 + gidx)])
                            for k in range(NHB):
                                p.wait("tensor", "sV", C["hsmul"][(e, c, k)])
                                l_ = hs[:, 512 * k + 128 * tt: 512 * k + 128 * (tt + 1)]
                                r_ = w2sb[:, DM * k + 512 * dc: DM * k + 512 * (dc + 1)]
                                is_last = (k == NHB - 1 and e != 0)
                                cnt = p.op("tensor", lambda E, o=bank[:, :], l=l_, r=r_,
                                           st=(k == 0), sp=is_last:
                                           E.matmul(o, l, r, start=st, stop=sp),
                                           inc="sT" if is_last else None)
                            if e == 0:
                                l_ = scoresT[0:8, 128 * tau:128 * (tau + 1)]
                                r_ = b2sb[0:8, 512 * dc:512 * (dc + 1)]
                                cnt = p.op("tensor", lambda E, o=bank[:, :], l=l_, r=r_:
                                           E.matmul(o, l, r, start=False, stop=True), inc="sT")
                            C["w2mm"][(e, c, gidx)] = cnt
                            if c == NCH - 1 and gidx == 7:
                                C["w2_last"][e] = cnt
                            p.wait("vector", "sT", cnt)
                            xsl = xres[:, DM * tau + 512 * dc: DM * tau + 512 * (dc + 1)]
                            C["oadd"][(e, c, gidx)] = p.op(
                                "vector", lambda E, x=xsl, b2_=bank[:, :]: E.tensor_add(x, x, b2_),
                                inc="sV")
                            if e == NEXP - 1 and dc == 1:
                                w0 = C["oadd"][(e, c, tt)]
                                w1v = C["oadd"][(e, c, 4 + tt)]
                                p.wait("scalar", "sV", max(w0, w1v))
                                o_ = out_h[128 * tau:128 * (tau + 1), :]
                                i_ = xres[:, DM * tau:DM * (tau + 1)]
                                p.dma_op("scalar", "dO", lambda E, o=o_, i2=i_: E.dma_start(o, i2))
                if e + 1 < NEXP:
                    emit_w1_dma(e + 1, 0, C["h_last"][(e, 0)])
                    emit_w1_dma(e + 1, 1, C["h_last"][(e, 1)])
                    emit_w2_dma(e + 1, 0, C["w2_last"][e])
                    emit_w2_dma(e + 1, 1, C["w2_last"][e])

            p.wait("scalar", "dO", p.cnt["dO"])
            p.play_into(nc)

    return nc


_NC_CACHE = None


def kernel(L0_attn_x, L0_attn_wqkv, L0_ffn_Wg, L0_ffn_W1, L0_ffn_b1, L0_ffn_W2, L0_ffn_b2):
    global _NC_CACHE
    x = np.ascontiguousarray(np.asarray(L0_attn_x, dtype=np.float32))
    wqkv = np.ascontiguousarray(np.asarray(L0_attn_wqkv, dtype=np.float32))
    wg = np.asarray(L0_ffn_Wg, dtype=np.float32)
    wgt = np.ascontiguousarray(wg.T)
    w1 = np.ascontiguousarray(np.asarray(L0_ffn_W1, dtype=np.float32))
    b1 = np.ascontiguousarray(np.asarray(L0_ffn_b1, dtype=np.float32))
    w2 = np.ascontiguousarray(np.asarray(L0_ffn_W2, dtype=np.float32))
    b2 = np.ascontiguousarray(np.asarray(L0_ffn_b2, dtype=np.float32))

    i128f = np.eye(128, dtype=np.float32)
    i128b = np.eye(128, dtype=np.float32).astype(ml_dtypes.bfloat16)
    ustr = np.triu(np.ones((128, 128), dtype=np.float32), k=1).astype(ml_dtypes.bfloat16)
    ones1 = np.ones((1, 128), dtype=ml_dtypes.bfloat16)
    onesc = np.ones((128, 1), dtype=ml_dtypes.bfloat16)
    esel = np.zeros((NE, NE * 128), dtype=ml_dtypes.bfloat16)
    for e in range(NE):
        esel[e, 128 * e:128 * (e + 1)] = 1
    iota = (np.arange(NT, dtype=np.float32)[None, :] * 128.0
            + np.arange(128, dtype=np.float32)[:, None])

    if _NC_CACHE is None:
        _NC_CACHE = build_nc()
    nc = _NC_CACHE

    base = {
        "wqkv": wqkv, "wgt": wgt, "b1": b1, "b2": b2,
        "i128f": i128f, "i128b": i128b, "ustr": ustr, "ones1": ones1,
        "onesc": onesc, "esel": esel, "iota": np.ascontiguousarray(iota),
    }
    for e in range(NE):
        base["w1_%d" % e] = w1[e]
        base["w2_%d" % e] = w2[e]
    in_maps = [dict(base, x=x[bidx]) for bidx in range(NCORE)]
    res = run_bass_kernel_spmd(nc, in_maps, core_ids=list(range(NCORE)))
    global LAST_RES
    LAST_RES = res
    return np.stack([res.results[bidx]["out"] for bidx in range(NCORE)], axis=0)


# version 24
# speedup vs baseline: 1.0385x; 1.0050x over previous
import contextlib
import os

import numpy as np
import ml_dtypes
import concourse.bass as bass
import concourse.mybir as mybir
from concourse.bass_utils import run_bass_kernel_spmd

F32 = mybir.dt.float32
BF16 = mybir.dt.bfloat16
AF = mybir.ActivationFunctionType
ALU = mybir.AluOpType
AX = mybir.AxisListType

NCORE = 8
SEQ = 2048
DM = 1024
NT = 16
NB = 8
NH = 16
FQKV = 3072
NQC = 6
NE = 8
HID = 2048
NHB = 16
NCH = 4
EPS = 1e-5

_SEM_OWNER = {"sT": "tensor", "sV": "vector", "sS": "scalar"}


class Prog:
    ENG = ("tensor", "vector", "scalar", "gpsimd", "sync")

    def __init__(self):
        self.streams = {e: [] for e in self.ENG}
        self.sems = {}
        self.cnt = {}
        self.lastwait = {}

    def sem(self, name, handle):
        self.sems[name] = handle
        self.cnt[name] = 0

    def op(self, eng, fn, inc=None, inc_by=1):
        if inc is not None:
            self.cnt[inc] += inc_by
            h = self.sems[inc]
            self.streams[eng].append(lambda E, f=fn, h=h, k=inc_by: f(E).then_inc(h, k))
            return self.cnt[inc]
        self.streams[eng].append(fn)
        return None

    def wait(self, eng, sem, val):
        if val is None or val <= 0:
            return
        if _SEM_OWNER.get(sem) == eng:
            return
        assert val <= self.cnt[sem], (eng, sem, val, self.cnt[sem])
        key = (eng, sem)
        if self.lastwait.get(key, 0) >= val:
            return
        self.lastwait[key] = val
        h = self.sems[sem]
        self.streams[eng].append(lambda E, h=h, v=val: E.wait_ge(h, v))

    def dma_op(self, eng, sem, fn):
        self.wait(eng, sem, self.cnt[sem])
        return self.op(eng, fn, inc=sem, inc_by=16)

    def barrier_inc(self):
        h = self.sems["sB"]
        for e in self.ENG:
            if e == "sync":
                self.streams[e].append(lambda E, h=h: E.sem_inc(h, 1))
            else:
                self.streams[e].append(lambda E, h=h: E.drain().then_inc(h, 1))
        self.cnt["sB"] += 5

    def barrier_wait(self):
        h = self.sems["sB"]
        v = self.cnt["sB"]
        for e in self.ENG:
            self.streams[e].append(lambda E, h=h, v=v: E.wait_ge(h, v))

    def play_into(self, nc):
        with nc.Block() as block:
            streams = self.streams

            @block.tensor
            def _(E):
                for f in streams["tensor"]:
                    f(E)

            @block.vector
            def _(E):
                for f in streams["vector"]:
                    f(E)

            @block.scalar
            def _(E):
                for f in streams["scalar"]:
                    f(E)

            @block.gpsimd
            def _(E):
                for f in streams["gpsimd"]:
                    f(E)

            @block.sync
            def _(E):
                for f in streams["sync"]:
                    f(E)

        self.streams = {e: [] for e in self.ENG}
        self.lastwait = {}


def build_nc():
    nc = bass.Bass("TRN2", target_bir_lowering=False, detect_race_conditions=False)

    x_h = nc.dram_tensor("x", [SEQ, DM], F32, kind="ExternalInput").ap()
    wqkv_h = nc.dram_tensor("wqkv", [DM, FQKV], F32, kind="ExternalInput").ap()
    wgt_h = nc.dram_tensor("wgt", [NE, DM], F32, kind="ExternalInput").ap()
    b1_h = nc.dram_tensor("b1", [NE, HID], F32, kind="ExternalInput").ap()
    b2_h = nc.dram_tensor("b2", [NE, DM], F32, kind="ExternalInput").ap()
    w1_h = [nc.dram_tensor("w1_%d" % e, [DM, HID], F32, kind="ExternalInput").ap() for e in range(NE)]
    w2_h = [nc.dram_tensor("w2_%d" % e, [HID, DM], F32, kind="ExternalInput").ap() for e in range(NE)]
    i128f_h = nc.dram_tensor("i128f", [128, 128], F32, kind="ExternalInput").ap()
    i128b_h = nc.dram_tensor("i128b", [128, 128], BF16, kind="ExternalInput").ap()
    ustr_h = nc.dram_tensor("ustr", [128, 128], BF16, kind="ExternalInput").ap()
    ones1_h = nc.dram_tensor("ones1", [1, 128], BF16, kind="ExternalInput").ap()
    onesc_h = nc.dram_tensor("onesc", [128, 1], BF16, kind="ExternalInput").ap()
    esel_h = nc.dram_tensor("esel", [NE, NE * 128], BF16, kind="ExternalInput").ap()
    iota_h = nc.dram_tensor("iota", [128, NT], F32, kind="ExternalInput").ap()
    out_h = nc.dram_tensor("out", [SEQ, DM], F32, kind="ExternalOutput").ap()

    p = Prog()
    C = {}

    with contextlib.ExitStack() as stk:
        for s in ("sT", "sV", "sS", "sVf", "sSf", "dX", "dX2", "dW", "dO", "sB"):
            p.sem(s, stk.enter_context(nc.semaphore(s)))

        xres = stk.enter_context(nc.sbuf_tensor("xres", [128, NT * DM], F32))
        w1a = stk.enter_context(nc.sbuf_tensor("w1a", [128, NB * 1024], BF16))
        xnT = stk.enter_context(nc.sbuf_tensor("xnT", [128, NB * SEQ], BF16))
        scr32 = stk.enter_context(nc.sbuf_tensor("scr32", [128, DM], F32))
        xn2s = stk.enter_context(nc.sbuf_tensor("xn2s", [128, DM], F32))
        b1T = stk.enter_context(nc.sbuf_tensor("b1T", [128, 128], F32))
        wg_sb = stk.enter_context(nc.sbuf_tensor("wg_sb", [128, 64], F32))
        scoresT = stk.enter_context(nc.sbuf_tensor("scoresT", [8, SEQ], BF16))
        carryb = stk.enter_context(nc.sbuf_tensor("carryb", [1, DM], BF16))
        i128f = stk.enter_context(nc.sbuf_tensor("i128f_s", [128, 128], F32))
        i128b = stk.enter_context(nc.sbuf_tensor("i128b_s", [128, 128], BF16))
        ustr = stk.enter_context(nc.sbuf_tensor("ustr_s", [128, 128], BF16))
        ones1 = stk.enter_context(nc.sbuf_tensor("ones1_s", [1, 128], BF16))
        onesc = stk.enter_context(nc.sbuf_tensor("onesc_s", [128, 1], BF16))
        esel = stk.enter_context(nc.sbuf_tensor("esel_s", [NE, NE * 128], BF16))
        iota = stk.enter_context(nc.sbuf_tensor("iota_s", [128, NT], F32))
        sdot = stk.enter_context(nc.sbuf_tensor("sdot", [128, NH], F32))
        tmin = stk.enter_context(nc.sbuf_tensor("tmin", [128, NH], F32))
        tneg = stk.enter_context(nc.sbuf_tensor("tneg", [128, NH], F32))
        e_sb = stk.enter_context(nc.sbuf_tensor("e_sb", [128, NH], F32))
        z_sb = stk.enter_context(nc.sbuf_tensor("z_sb", [128, NH], F32))
        den = stk.enter_context(nc.sbuf_tensor("den", [128, NH], F32))
        rr = stk.enter_context(nc.sbuf_tensor("rr", [128, NH], F32))
        a_sb = stk.enter_context(nc.sbuf_tensor("a_sb", [128, NH], F32))
        bb_sb = stk.enter_context(nc.sbuf_tensor("bb_sb", [128, NH], F32))
        ln = {}
        for t in ("sum", "sumsq", "var", "lnv", "rstd", "negc"):
            for s in (2,):
                ln[(t, s)] = stk.enter_context(nc.sbuf_tensor("ln_%s_%d" % (t, s), [128, 1], F32))
        l1 = {}
        for t in ("sum", "sumsq", "d", "lnv", "rstd", "negc"):
            l1[t] = stk.enter_context(nc.sbuf_tensor("l1_%s" % t, [128, NT], F32))
        lgall = stk.enter_context(nc.sbuf_tensor("lgall", [128, 128], F32))
        gA = stk.enter_context(nc.sbuf_tensor("gA", [128, 128], F32))
        keepall = stk.enter_context(nc.sbuf_tensor("keepall", [128, 128], F32))
        exall = stk.enter_context(nc.sbuf_tensor("exall", [128, 128], F32))
        gm1a = stk.enter_context(nc.sbuf_tensor("gm1a", [128, NT], F32))
        gm2a = stk.enter_context(nc.sbuf_tensor("gm2a", [128, NT], F32))
        gnm1a = stk.enter_context(nc.sbuf_tensor("gnm1a", [128, NT], F32))
        gzsa = stk.enter_context(nc.sbuf_tensor("gzsa", [128, NT], F32))
        grza = stk.enter_context(nc.sbuf_tensor("grza", [128, NT], F32))

        pb = [stk.enter_context(nc.psum_tensor("pb%d" % i, [128, 512], F32)) for i in range(8)]

        def emit_ln(which, j, x_ap, out_ap, junk):
            key = "norm%d" % which
            s_sum = ln[("sum", which)]; s_sq = ln[("sumsq", which)]
            s_d = ln[("var", which)]
            s_lnv = ln[("lnv", which)]; s_rstd = ln[("rstd", which)]; s_negc = ln[("negc", which)]
            prev = C.setdefault(key, {}).get(j - 1)
            if prev:
                p.wait("vector", "sS", prev)
            p.op("vector", lambda E: E.reduce_sum(s_sum[:, :], x_ap, AX.X))
            f1 = p.op("vector", lambda E: E.scalar_tensor_tensor(
                junk[:, :], x_ap, 1.0, x_ap, ALU.mult, ALU.mult, accum_out=s_sq[:, :]),
                inc="sVf")
            p.wait("vector", "sVf", f1)
            f2 = p.op("vector", lambda E: E.tensor_scalar(
                s_d[:, :], s_sum[:, :], s_sum[:, :], 1.0 / DM, ALU.mult, ALU.mult), inc="sVf")
            p.wait("vector", "sVf", f2)
            vstats = p.op("vector", lambda E: E.scalar_tensor_tensor(
                s_d[:, :], s_d[:, :], -1.0, s_sq[:, :], ALU.mult, ALU.add), inc="sV")
            p.wait("scalar", "sV", vstats)
            g1 = p.op("scalar", lambda E: E.activation(
                s_lnv[:, :], s_d[:, :], AF.Ln,
                bias=np.float32(EPS), scale=np.float32(1.0 / DM)), inc="sSf")
            p.wait("scalar", "sSf", g1)
            srst = p.op("scalar", lambda E: E.activation(s_rstd[:, :], s_lnv[:, :], AF.Exp,
                                                         scale=np.float32(-0.5)),
                        inc="sS")
            p.wait("vector", "sS", srst)
            vneg = p.op("vector", lambda E: E.tensor_scalar(
                s_negc[:, :], s_sum[:, :], s_rstd[:, :], -1.0 / DM,
                ALU.mult, ALU.mult), inc="sV")
            p.wait("scalar", "sV", vneg)
            nrm = p.op("scalar", lambda E: E.activation(
                out_ap, x_ap, AF.Identity, bias=s_negc[:, :], scale=s_rstd[:, :]), inc="sS")
            C[key][j] = nrm
            return nrm

        with (
            nc.sbuf_tensor("wqkvsb", [128, NB * FQKV], BF16) as wqkvsb,
            nc.sbuf_tensor("qkvt", [128, 2 * FQKV], BF16) as qkvt,
            nc.sbuf_tensor("junk", [128, DM], BF16) as junk,
            nc.sbuf_tensor("xn1tm", [128, 2 * DM], F32) as xn1tm,
        ):
            wgtsb = xn2s
            qkvt_sl = [qkvt[:, FQKV * k:FQKV * (k + 1)] for k in range(2)]
            qkvt3s = [qkvt[:, FQKV * k:FQKV * (k + 1)].rearrange("p (h c) -> p h c", h=NH)
                      for k in range(2)]
            xnT3 = xnT[:, :].rearrange("p (b t) -> p b t", b=NB)

            C["dx_x"] = {}
            C["dx_sem"] = {}
            for j in range(4):
                src = x_h[128 * j:128 * (j + 1), :]
                dst = xres[:, DM * j:DM * (j + 1)]
                if j % 2 == 0:
                    eng, sem = "sync", "dX"
                else:
                    eng, sem = "gpsimd", "dW"
                C["dx_x"][j] = p.dma_op(eng, sem, lambda E, s=src, d=dst: E.dma_start(d, s))
                C["dx_sem"][j] = sem
            C["dx_i128f"] = p.dma_op("sync", "dX", lambda E: E.dma_start(i128f[:, :], i128f_h))
            for src, dst in ((i128b_h, i128b), (ustr_h, ustr),
                             (ones1_h, ones1), (onesc_h, onesc), (iota_h, iota), (esel_h, esel)):
                p.dma_op("sync", "dX", lambda E, s=src, d=dst: E.dma_start(d[:, :], s))
            p.dma_op("sync", "dX", lambda E: E.dma_start(wgtsb[0:8, :], wgt_h))
            C["dx_consts"] = p.cnt["dX"]
            for j in range(4, 8):
                src = x_h[128 * j:128 * (j + 1), :]
                dst = xres[:, DM * j:DM * (j + 1)]
                sem = "dX" if j % 2 == 0 else "dX2"
                C["dx_x"][j] = p.dma_op("sync", sem, lambda E, s=src, d=dst: E.dma_start(d, s))
                C["dx_sem"][j] = sem
            
            for j in range(8, NT):
                src = x_h[128 * j:128 * (j + 1), :]
                dst = xres[:, DM * j:DM * (j + 1)]
                C["dx_x"][j] = p.dma_op("scalar", "dO", lambda E, s=src, d=dst: E.dma_start(d, s))
                C["dx_sem"][j] = "dO"

            wq_src = wqkv_h.rearrange("(b p) f -> p b f", p=128)
            wq_dst = wqkvsb[:, :].rearrange("p (b f) -> p b f", b=NB)
            C["dw_wqkv"] = {}
            for c in range(NQC):
                s = wq_src[:, :, 512 * c:512 * (c + 1)]
                d = wq_dst[:, :, 512 * c:512 * (c + 1)]
                C["dw_wqkv"][c] = p.dma_op("gpsimd", "dW", lambda E, s=s, d=d: E.dma_start(d, s))
            C["dw_w1"] = {}
            s0 = w1_h[0].rearrange("(b p) h -> p b h", p=128)[:, :, 0:1024]
            w1a3 = w1a[:, :].rearrange("p (b h) -> p b h", b=NB)
            C["dw_w1"][(0, 0)] = p.dma_op("gpsimd", "dW", lambda E: E.dma_start(w1a3, s0))

            def emit_wg():
                p.wait("tensor", "dX", C["dx_consts"])
                for b in range(NB):
                    i_ = wgtsb[0:8, 128 * b:128 * (b + 1)]
                    o_ = pb[4][:, 8 * b:8 * (b + 1)]
                    last = p.op("tensor", lambda E, o=o_, i2=i_: E.transpose(o, i2, i128f[0:8, 0:8]),
                                inc="sT" if b == NB - 1 else None)
                C["wgT"] = last
                p.wait("vector", "sT", C["wgT"])
                C["wg_evac"] = p.op("vector", lambda E: E.tensor_copy(wg_sb[:, :], pb[4][:, 0:64]),
                                    inc="sV")

            for name in ("t1a", "t1b", "xn1T_evac", "qkv_mm", "qkv_evac", "ab", "cums",
                         "carryF", "combS", "combV1", "combV2", "prepTr", "xn2f32", "pb5free",
                         "xn2bf", "logits", "lgc"):
                C[name] = {}

            C["carry0"] = p.op("vector", lambda E: E.memset(carryb[:, :], 0), inc="sV")

            qkv_banks = [pb[0], pb[1]]

            C["l1negc"] = {}
            GRPS = [(0, 2), (2, 4), (4, 8), (8, 12), (12, 16)]
            C["tile2grp"] = {}
            for g, (glo, ghi) in enumerate(GRPS):
                for j in range(glo, ghi):
                    C["tile2grp"][j] = g
                fX = None
                for j in range(glo, ghi):
                    p.wait("vector", C["dx_sem"][j], C["dx_x"][j])
                    x_t = xres[:, DM * j:DM * (j + 1)]
                    p.op("vector", lambda E, x=x_t, jj=j: E.reduce_sum(
                        l1["sum"][:, jj:jj + 1], x, AX.X))
                    fX = p.op("vector", lambda E, x=x_t, jj=j: E.scalar_tensor_tensor(
                        junk[:, :], x, 1.0, x, ALU.mult, ALU.mult,
                        accum_out=l1["sumsq"][:, jj:jj + 1]), inc="sVf")
                p.wait("vector", "sVf", fX)
                sl0, sl1 = glo, ghi
                f2 = p.op("vector", lambda E, a=sl0, b=sl1: E.tensor_mul(
                    l1["d"][:, a:b], l1["sum"][:, a:b], l1["sum"][:, a:b]), inc="sVf")
                p.wait("vector", "sVf", f2)
                vst = p.op("vector", lambda E, a=sl0, b=sl1: E.scalar_tensor_tensor(
                    l1["d"][:, a:b], l1["d"][:, a:b], -1.0 / DM, l1["sumsq"][:, a:b],
                    ALU.mult, ALU.add), inc="sV")
                p.wait("scalar", "sV", vst)
                g1 = p.op("scalar", lambda E, a=sl0, b=sl1: E.activation(
                    l1["lnv"][:, a:b], l1["d"][:, a:b], AF.Ln,
                    bias=np.float32(EPS), scale=np.float32(1.0 / DM)), inc="sSf")
                p.wait("scalar", "sSf", g1)
                g2 = p.op("scalar", lambda E, a=sl0, b=sl1: E.activation(
                    l1["rstd"][:, a:b], l1["lnv"][:, a:b], AF.Exp,
                    scale=np.float32(-0.5)), inc="sS")
                p.wait("vector", "sS", g2)
                f3 = p.op("vector", lambda E, a=sl0, b=sl1: E.tensor_mul(
                    l1["negc"][:, a:b], l1["sum"][:, a:b], l1["rstd"][:, a:b]), inc="sVf")
                p.wait("vector", "sVf", f3)
                C["l1negc"][g] = p.op("vector", lambda E, a=sl0, b=sl1: E.tensor_scalar_mul(
                    l1["negc"][:, a:b], l1["negc"][:, a:b], -1.0 / DM), inc="sV")

            LIM = int(os.environ.get("KMK_NTI", str(NT)))
            ST = os.environ.get("KMK_ST", "I")
            def emit_gate(tlo, thi):
                for i in range(tlo, thi):
                    p.op("vector", lambda E, ii=i: E.reduce_max(
                        gm1a[:, ii:ii + 1], lgall[:, 8 * ii:8 * (ii + 1)], AX.X))
                fg = p.op("vector", lambda E: E.engine_nop(), inc="sVf")
                p.wait("vector", "sVf", fg)
                for i in range(tlo, thi):
                    p.op("vector", lambda E, ii=i: E.tensor_scalar(
                        gA[:, 8 * ii:8 * (ii + 1)], lgall[:, 8 * ii:8 * (ii + 1)],
                        gm1a[:, ii:ii + 1], None, ALU.is_equal))
                fg = p.op("vector", lambda E: E.engine_nop(), inc="sVf")
                p.wait("vector", "sVf", fg)
                for i in range(tlo, thi):
                    p.op("vector", lambda E, ii=i: E.scalar_tensor_tensor(
                        gA[:, 8 * ii:8 * (ii + 1)], gA[:, 8 * ii:8 * (ii + 1)], -1e30,
                        lgall[:, 8 * ii:8 * (ii + 1)], ALU.mult, ALU.add))
                fg = p.op("vector", lambda E: E.engine_nop(), inc="sVf")
                p.wait("vector", "sVf", fg)
                for i in range(tlo, thi):
                    p.op("vector", lambda E, ii=i: E.reduce_max(
                        gm2a[:, ii:ii + 1], gA[:, 8 * ii:8 * (ii + 1)], AX.X))
                fg = p.op("vector", lambda E: E.engine_nop(), inc="sVf")
                p.wait("vector", "sVf", fg)
                for i in range(tlo, thi):
                    p.op("vector", lambda E, ii=i: E.tensor_scalar(
                        keepall[:, 8 * ii:8 * (ii + 1)], lgall[:, 8 * ii:8 * (ii + 1)],
                        gm2a[:, ii:ii + 1], None, ALU.is_ge))
                g1 = p.op("vector", lambda E, a=tlo, b=thi: E.tensor_scalar_mul(
                    gnm1a[:, a:b], gm1a[:, a:b], -1.0), inc="sV")
                p.wait("scalar", "sV", g1)
                for i in range(tlo, thi):
                    cnt = p.op("scalar", lambda E, ii=i: E.activation(
                        exall[:, 8 * ii:8 * (ii + 1)], lgall[:, 8 * ii:8 * (ii + 1)],
                        AF.Exp, bias=gnm1a[:, ii:ii + 1]),
                        inc="sS" if i == thi - 1 else None)
                p.wait("vector", "sS", cnt)
                fg = p.op("vector", lambda E, a=tlo, b=thi: E.tensor_mul(
                    exall[:, 8 * a:8 * b], exall[:, 8 * a:8 * b], keepall[:, 8 * a:8 * b]),
                    inc="sVf")
                p.wait("vector", "sVf", fg)
                for i in range(tlo, thi):
                    p.op("vector", lambda E, ii=i: E.reduce_sum(
                        gzsa[:, ii:ii + 1], exall[:, 8 * ii:8 * (ii + 1)], AX.X))
                fg = p.op("vector", lambda E: E.engine_nop(), inc="sVf")
                p.wait("vector", "sVf", fg)
                fg = p.op("vector", lambda E, a=tlo, b=thi: E.reciprocal(
                    grza[:, a:b], gzsa[:, a:b]), inc="sVf")
                p.wait("vector", "sVf", fg)
                for i in range(tlo, thi):
                    cnt = p.op("vector", lambda E, ii=i: E.tensor_scalar(
                        gA[:, 8 * ii:8 * (ii + 1)], exall[:, 8 * ii:8 * (ii + 1)],
                        grza[:, ii:ii + 1], None, ALU.mult),
                        inc="sV" if i == thi - 1 else None)
                return cnt

            def emit_norm1(jj):
                x_t2 = xres[:, DM * jj:DM * (jj + 1)]
                if jj >= 2:
                    p.wait("scalar", "sT", C["t1b"][jj - 2])
                p.wait("scalar", "sV", C["l1negc"][C["tile2grp"][jj]])
                C.setdefault("norm1", {})[jj] = p.op(
                    "scalar", lambda E, x=x_t2, j3=jj:
                    E.activation(xn1tm[:, DM * (j3 % 2):DM * (j3 % 2 + 1)], x, AF.Identity,
                                 bias=l1["negc"][:, j3:j3 + 1],
                                 scale=l1["rstd"][:, j3:j3 + 1]), inc="sS")

            for it in range(LIM + 1):
                j = it
                i = it - 1
                if j < LIM:
                    x_t = xres[:, DM * j:DM * (j + 1)]
                    if j == 0:
                        emit_norm1(0)
                    if ST < "B":
                        continue
                    p.wait("tensor", "dX", C["dx_i128f"])
                    p.wait("tensor", "sS", C["norm1"][j])
                    for b in range(NB):
                        if b == 0 and i - 1 >= 0:
                            p.wait("tensor", "sS", C["pb5free"][i - 1])
                        if b == 4 and i - 1 >= 0:
                            p.wait("tensor", "sS", C["xn2bf"][i - 1])
                        bank = pb[5] if b < 4 else pb[6]
                        o_ = bank[:, 128 * (b % 4):128 * (b % 4 + 1)]
                        i_ = xn1tm[:, DM * (j % 2) + 128 * b:DM * (j % 2) + 128 * (b + 1)]
                        cnt = p.op("tensor", lambda E, o=o_, i2=i_: E.transpose(o, i2, i128f[:, :]),
                                   inc="sT" if b in (3, 7) else None)
                        if b == 3:
                            C["t1a"][j] = cnt
                        elif b == 7:
                            C["t1b"][j] = cnt
                    for half in range(2):
                        p.wait("scalar", "sT", C["t1a"][j] if half == 0 else C["t1b"][j])
                        bank = pb[5] if half == 0 else pb[6]
                        i_ = bank[:, 0:512].rearrange("p (b t) -> p b t", b=4)
                        o_ = xnT3[:, 4 * half:4 * (half + 1), 128 * j:128 * (j + 1)]
                        cnt = p.op("scalar", lambda E, o=o_, i2=i_: E.copy(o, i2),
                                   inc="sS" if half == 1 else None)
                    C["xn1T_evac"][j] = cnt
                    if j + 1 < LIM:
                        emit_norm1(j + 1)

                if i >= 0 and ST >= "D":
                    p.wait("vector", "sS", C["qkv_evac"][NQC * i + NQC - 1])
                    if i >= 1:
                        p.wait("vector", "sS", C["combS"][i - 1])
                    q3 = qkvt3s[i % 2]
                    p.op("vector", lambda E, q3=q3: E.scalar_tensor_tensor(
                        q3[:, :, 0:64], q3[:, :, 0:64], 1.0, q3[:, :, 64:128],
                        ALU.mult, ALU.mult))
                    fD = p.op("vector", lambda E, q3=q3: E.tensor_reduce(
                        sdot[:, :], q3[:, :, 0:64], AX.X, ALU.add), inc="sVf")
                    p.wait("vector", "sVf", fD)
                    p.op("vector", lambda E: E.tensor_scalar(
                        tmin[:, :], sdot[:, :], 0.125, 0.0, ALU.mult, ALU.min))
                    dmm = p.op("vector", lambda E: E.tensor_scalar(
                        tneg[:, :], sdot[:, :], -0.125, 0.0, ALU.mult, ALU.min), inc="sV")
                    p.wait("scalar", "sV", dmm)
                    p.op("scalar", lambda E: E.activation(e_sb[:, :], tmin[:, :], AF.Exp))
                    eS = p.op("scalar", lambda E: E.activation(z_sb[:, :], tneg[:, :], AF.Exp),
                              inc="sS")
                    p.wait("vector", "sS", eS)
                    fD2 = p.op("vector", lambda E, ii=i: E.scalar_tensor_tensor(
                        den[:, :], z_sb[:, :], iota[:, ii:ii + 1], e_sb[:, :], ALU.mult, ALU.add),
                        inc="sVf")
                    p.wait("vector", "sVf", fD2)
                    fD3 = p.op("vector", lambda E: E.reciprocal(rr[:, :], den[:, :]), inc="sVf")
                    p.wait("vector", "sVf", fD3)
                    p.op("vector", lambda E: E.tensor_mul(a_sb[:, :], z_sb[:, :], rr[:, :]))
                    C["ab"][i] = p.op("vector", lambda E: E.tensor_mul(
                        bb_sb[:, :], e_sb[:, :], rr[:, :]), inc="sV")
                    p.wait("tensor", "sS", C["qkv_evac"][NQC * i + NQC - 1])
                    p.wait("tensor", "sV", C["carryF"].get(i - 1, C["carry0"]))
                    if i >= 1:
                        p.wait("tensor", "sV", C["combV2"][i - 1])
                    for half in range(2):
                        bank = pb[2 + half]
                        v_ap = qkvt3s[i % 2][:, 8 * half:8 * (half + 1), 128:192]
                        c_ap = carryb[0:1, 512 * half:512 * (half + 1)]
                        p.op("tensor", lambda E, o=bank[:, :], v=v_ap:
                             E.matmul(o, ustr[:, :], v, start=True, stop=False))
                        cnt = p.op("tensor", lambda E, o=bank[:, :], cc=c_ap:
                                   E.matmul(o, ones1[:, :], cc, start=False, stop=True), inc="sT")
                    C["cums"][i] = cnt
                    if ST < "F":
                        continue
                    for half in range(2):
                        v_ap = qkvt3s[i % 2][:, 8 * half:8 * (half + 1), 128:192]
                        if half == 1:
                            p.wait("tensor", "sV", C["colA0"])
                        colT = p.op("tensor", lambda E, v=v_ap:
                                    E.matmul(pb[7][0:1, :], onesc[:, :], v, start=True, stop=True),
                                    inc="sT")
                        C.setdefault("colT", {})[i] = colT
                        p.wait("vector", "sT", colT)
                        c_ap = carryb[0:1, 512 * half:512 * (half + 1)]
                        cnt = p.op("vector", lambda E, cc=c_ap: E.tensor_add(cc, pb[7][0:1, :], cc),
                                   inc="sV")
                        if half == 0:
                            C["colA0"] = cnt
                    C["carryF"][i] = cnt
                    if ST < "G":
                        continue
                    p.wait("scalar", "sT", C["cums"][i])
                    p.wait("scalar", "sV", C["carryF"][i])
                    p.wait("scalar", "sV", C["ab"][i])
                    combs_cnt = {}
                    for h in range(NH):
                        bank = pb[2] if h < 8 else pb[3]
                        sl = bank[:, 64 * (h % 8):64 * (h % 8 + 1)]
                        combs_cnt[h] = p.op("scalar", lambda E, s=sl, hh=h: E.activation(
                            s, s, AF.Copy, scale=a_sb[:, hh:hh + 1]), inc="sS")
                    C["combS"][i] = combs_cnt[NH - 1]
                    for h in range(NH):
                        p.wait("vector", "sS", combs_cnt[7] if h < 8 else combs_cnt[15])
                        bank = pb[2] if h < 8 else pb[3]
                        sl = bank[:, 64 * (h % 8):64 * (h % 8 + 1)]
                        v_ap = qkvt_sl[i % 2][:, 192 * h + 128:192 * h + 192]
                        cnt = p.op("vector", lambda E, s=sl, v=v_ap, hh=h: E.scalar_tensor_tensor(
                            s, v, bb_sb[:, hh:hh + 1], s, ALU.mult, ALU.add),
                            inc="sV" if h == NH - 1 else None)
                    C["combV1"][i] = cnt
                    fC = p.op("vector", lambda E: E.engine_nop(), inc="sVf")
                    p.wait("vector", "sVf", fC)
                    for half in range(2):
                        bank = pb[2 + half]
                        xsl = xres[:, DM * i + 512 * half: DM * i + 512 * (half + 1)]
                        cnt = p.op("vector", lambda E, x=xsl, b2_=bank[:, :]:
                                   E.tensor_add(x, x, b2_), inc="sV" if half == 1 else None)
                    C["combV2"][i] = cnt

                if j < LIM and ST >= "C":
                    if j >= 2 and ST >= "G":
                        p.wait("scalar", "sT", C["colT"][j - 2])
                        p.wait("scalar", "sV", C["combV1"][j - 2])
                    for c in range(NQC):
                        g = NQC * j + c
                        bank = qkv_banks[g % 2]
                        p.wait("tensor", "dW", C["dw_wqkv"][c])
                        p.wait("tensor", "sS", C["xn1T_evac"][j])
                        if g - 2 >= 0:
                            p.wait("tensor", "sS", C["qkv_evac"][g - 2])
                        for b in range(NB):
                            l_ = xnT[:, SEQ * b + 128 * j: SEQ * b + 128 * (j + 1)]
                            r_ = wqkvsb[:, FQKV * b + 512 * c: FQKV * b + 512 * (c + 1)]
                            cnt = p.op("tensor", lambda E, o=bank[:, :], l=l_, r=r_,
                                       st=(b == 0), sp=(b == NB - 1):
                                       E.matmul(o, l, r, start=st, stop=sp),
                                       inc="sT" if b == NB - 1 else None)
                        C["qkv_mm"][(j, c)] = cnt
                        p.wait("scalar", "sT", cnt)
                        o_ = qkvt_sl[j % 2][:, 512 * c:512 * (c + 1)]
                        C["qkv_evac"][g] = p.op("scalar", lambda E, o=o_, i2=bank[:, :]:
                                                E.copy(o, i2), inc="sS")

                if it == 0:
                    emit_wg()

                if i >= 0 and ST >= "H":
                    if i >= 1:
                        p.wait("scalar", "sT", C["prepTr"][i - 1])
                    x_i = xres[:, DM * i:DM * (i + 1)]
                    emit_ln(2, i, x_i, scr32[:, :], junk)
                    if ST < "I":
                        continue
                    p.wait("tensor", "sS", C["norm2"][i])
                    p.wait("tensor", "sS", C["xn1T_evac"].get(i + 1, C["xn1T_evac"][i]))
                    for b in range(NB):
                        bank = pb[5] if b < 4 else pb[6]
                        o_ = bank[:, 128 * (b % 4):128 * (b % 4 + 1)]
                        i_ = scr32[:, 128 * b:128 * (b + 1)]
                        cnt = p.op("tensor", lambda E, o=o_, i2=i_: E.transpose(o, i2, i128f[:, :]),
                                   inc="sT" if b == NB - 1 else None)
                    C["prepTr"][i] = cnt
                    p.wait("scalar", "sT", C["prepTr"][i])
                    if i >= 1:
                        p.wait("scalar", "sT", C["logits"][i - 1])
                    else:
                        p.wait("scalar", "sT", C["wgT"])
                    p.op("scalar", lambda E: E.copy(xn2s[:, 0:512], pb[5][:, 0:512]))
                    o_ = xnT3[:, 0:4, 128 * i:128 * (i + 1)]
                    i_ = pb[5][:, 0:512].rearrange("p (b t) -> p b t", b=4)
                    c5 = p.op("scalar", lambda E, o=o_, i2=i_: E.copy(o, i2), inc="sS")
                    cF = p.op("scalar", lambda E: E.copy(xn2s[:, 512:1024], pb[6][:, 0:512]),
                              inc="sS")
                    o_ = xnT3[:, 4:8, 128 * i:128 * (i + 1)]
                    i_ = pb[6][:, 0:512].rearrange("p (b t) -> p b t", b=4)
                    cnt = p.op("scalar", lambda E, o=o_, i2=i_: E.copy(o, i2), inc="sS")
                    C["pb5free"][i] = c5
                    C["xn2bf"][i] = cnt
                    p.wait("tensor", "sS", cF)
                    p.wait("tensor", "sV", C["lgc"].get(i - 1, C["wg_evac"]))
                    for b in range(NB):
                        l_ = xn2s[:, 128 * b:128 * (b + 1)]
                        r_ = wg_sb[:, 8 * b:8 * (b + 1)]
                        cnt = p.op("tensor", lambda E, l=l_, r=r_, st=(b == 0), sp=(b == NB - 1):
                                   E.matmul(pb[4][:, 0:8], l, r, start=st, stop=sp),
                                   inc="sT" if b == NB - 1 else None)
                    C["logits"][i] = cnt
                    p.wait("vector", "sT", C["logits"][i])
                    C["lgc"][i] = p.op("vector", lambda E, ii=i: E.tensor_copy(
                        lgall[:, 8 * ii:8 * (ii + 1)], pb[4][:, 0:8]), inc="sV")
                    if i == 7 and LIM == NT:
                        C["gate2a"] = emit_gate(0, 8)
                    if i == 11 and LIM == NT:
                        C["gate2b1"] = emit_gate(8, 12)

            if LIM == NT:
                gate2all = max(C["gate2a"], C["gate2b1"], emit_gate(12, NT))
            elif LIM >= 1:
                gate2all = emit_gate(0, LIM)
            else:
                gate2all = 0
            p.wait("tensor", "sV", gate2all)
            if LIM >= 1:
                p.wait("tensor", "sV", C["lgc"][LIM - 1])
            if LIM >= 1 and (LIM - 1) in C["xn2bf"]:
                p.wait("tensor", "sS", C["xn2bf"][LIM - 1])
                p.wait("tensor", "sV", C["carryF"][LIM - 1])
            sco_tr = {}
            for i in range(LIM):
                bank = pb[4 + i // 4]
                o_ = bank[0:8, 128 * (i % 4):128 * (i % 4 + 1)]
                i_ = gA[:, 8 * i:8 * (i + 1)]
                cnt = p.op("tensor", lambda E, o=o_, i2=i_: E.transpose(o, i2, i128f[:, :]),
                           inc="sT" if i % 4 == 3 else None)
                if i % 4 == 3:
                    sco_tr[i // 4] = cnt
            scoresT3 = scoresT[0:8, :].rearrange("e (t p) -> e t p", t=NT)
            for k in range(LIM // 4):
                p.wait("vector", "sT", sco_tr[k])
                o_ = scoresT3[0:8, 4 * k:4 * (k + 1), :]
                i_ = pb[4 + k][0:8, 0:512].rearrange("e (t p) -> e t p", t=4)
                p.op("vector", lambda E, o=o_, i2=i_: E.tensor_copy(o, i2), inc="sV")

            if os.environ.get("KMK_PH", "ABC") != "ABC":
                p.barrier_inc()
            else:
                C["ph_sT"] = p.cnt["sT"]
                C["ph_sV"] = p.cnt["sV"]
                C["ph_sS"] = p.cnt["sS"]
                C["sco_last"] = p.cnt["sV"]
            p.play_into(nc)

        if os.environ.get("KMK_DBG"):
            dbg_xnT = nc.dram_tensor("dbg_xnT", [128, NB * SEQ], F32, kind="ExternalOutput").ap()
            dbg_qkvt = nc.dram_tensor("dbg_qkvt", [128, FQKV], F32, kind="ExternalOutput").ap()
            dbg_misc = nc.dram_tensor("dbg_misc", [128, 128], F32, kind="ExternalOutput").ap()
            dbg_xn1 = nc.dram_tensor("dbg_xn1", [128, DM], F32, kind="ExternalOutput").ap()
            dbg_carry = nc.dram_tensor("dbg_carry", [1, DM], F32, kind="ExternalOutput").ap()
            misc_srcs = [sdot, tmin, tneg, e_sb, z_sb, den, rr, a_sb]
            p.wait("gpsimd", "sT", p.cnt["sT"])
            p.wait("gpsimd", "sV", p.cnt["sV"])
            p.wait("gpsimd", "sS", p.cnt["sS"])
            p.dma_op("gpsimd", "dW", lambda E: E.dma_start(dbg_xnT, xnT[:, :]))
            p.dma_op("gpsimd", "dW", lambda E: E.dma_start(dbg_qkvt, qkvt[:, 0:FQKV]))
            p.dma_op("gpsimd", "dW", lambda E: E.dma_start(dbg_xn1, xn1tm[:, :]))
            p.dma_op("gpsimd", "dW", lambda E: E.dma_start(dbg_carry, carryb[:, :]))
            for mi, msrc in enumerate(misc_srcs):
                p.dma_op("gpsimd", "dW",
                         lambda E, o=dbg_misc[0:128, 16 * mi:16 * (mi + 1)], s=msrc:
                         E.dma_start(o, s[:, :]))
            dbg_ln = nc.dram_tensor("dbg_ln", [128, 12], F32, kind="ExternalOutput").ap()
            ln_srcs = [ln[(t, s)] for s in (1, 2)
                       for t in ("sum", "sumsq", "var", "lnv", "rstd", "negc")]
            for mi, msrc in enumerate(ln_srcs):
                p.dma_op("gpsimd", "dW",
                         lambda E, o=dbg_ln[0:128, mi:mi + 1], s=msrc:
                         E.dma_start(o, s[:, :], allow_slow_non_contiguous=True))
            p.wait("gpsimd", "dW", p.cnt["dW"])

        _PH = os.environ.get("KMK_PH", "ABC")
        if _PH == "A":
            p.barrier_wait()
            for tau in range(NT):
                o_ = out_h[128 * tau:128 * (tau + 1), :]
                i_ = xres[:, DM * tau:DM * (tau + 1)]
                p.dma_op("scalar", "dO", lambda E, o=o_, i2=i_: E.dma_start(o, i2))
            p.wait("scalar", "dO", 16 * NT)
            p.play_into(nc)
            return nc

        with (
            nc.sbuf_tensor("w2sb", [128, NHB * DM], BF16) as w2sb,
            nc.sbuf_tensor("hs", [128, NHB * 512], BF16) as hs,
            nc.sbuf_tensor("sbc", [128, SEQ], BF16) as sbc,
            nc.sbuf_tensor("b2sb", [8, DM], BF16) as b2sb,
            nc.sbuf_tensor("w1b", [128, NB * 1024], BF16) as w1b,
        ):
            if os.environ.get("KMK_PH", "ABC") != "ABC":
                p.barrier_wait()
            else:
                LL = LIM - 1
                p.wait("gpsimd", "sT", C["logits"][LL])
                p.wait("gpsimd", "sV", C["combV1"][LL])
                p.wait("gpsimd", "sS", C["xn2bf"][LL])
                p.wait("sync", "sT", C["logits"][LL])
                p.wait("tensor", "sV", C["sco_last"])
                p.wait("tensor", "sS", C["xn2bf"][LL])
            for half in range(2):
                dxb = p.dma_op("sync", "dX", lambda E, h2=half: E.dma_start(
                    xn2s[0:8, :], b1_h[0:8, 1024 * h2:1024 * (h2 + 1)]))
                p.wait("tensor", "dX", dxb)
                for k in range(8):
                    kk = 8 * half + k
                    i_ = xn2s[0:8, 128 * k:128 * (k + 1)]
                    o_ = pb[5][:, 8 * kk:8 * (kk + 1)]
                    cnt = p.op("tensor", lambda E, o=o_, i2=i_: E.transpose(o, i2, i128f[0:8, 0:8]),
                               inc="sT")
                if half == 0:
                    p.wait("sync", "sT", cnt)
            C["b1Tr"] = cnt
            p.wait("vector", "sT", C["b1Tr"])
            C["b1Te"] = p.op("vector", lambda E: E.tensor_copy(b1T[:, :], pb[5][:, 0:128]), inc="sV")
            w1_dst = [w1a[:, :].rearrange("p (b h) -> p b h", b=NB),
                      w1b[:, :].rearrange("p (b h) -> p b h", b=NB)]
            s = w1_h[0].rearrange("(b p) h -> p b h", p=128)[:, :, 1024:2048]
            C["dw_w1"][(0, 1)] = p.dma_op("gpsimd", "dW", lambda E, s=s: E.dma_start(w1_dst[1], s))
            C["dw_b2"] = p.dma_op("gpsimd", "dW", lambda E: E.dma_start(b2sb[:, :], b2_h))
            w2_dst3 = w2sb[:, :].rearrange("p (k d) -> p k d", k=NHB)
            C["dw_w2"] = {}

            def emit_w2_dma(e, dc, wait_sT):
                s = w2_h[e].rearrange("(k p) d -> p k d", p=128)[:, :, 512 * dc:512 * (dc + 1)]
                d = w2_dst3[:, :, 512 * dc:512 * (dc + 1)]
                p.wait("gpsimd", "sT", wait_sT)
                C["dw_w2"][(e, dc)] = p.dma_op("gpsimd", "dW", lambda E, s=s, d=d: E.dma_start(d, s))

            def emit_w1_dma(e, half, wait_sT):
                s = w1_h[e].rearrange("(b p) h -> p b h", p=128)[:, :, 1024 * half:1024 * (half + 1)]
                p.wait("gpsimd", "sT", wait_sT)
                C["dw_w1"][(e, half)] = p.dma_op("gpsimd", "dW",
                                                 lambda E, s=s, h2=half: E.dma_start(w1_dst[h2], s))

            emit_w2_dma(0, 0, 0)
            emit_w2_dma(0, 1, 0)

            for name in ("relu", "hsmul", "w2mm", "oadd", "sbc_mm", "sbc_ev",
                         "h_last", "w2_last"):
                C[name] = {}

            NEXP = int(os.environ.get("KMK_NE", str(NE)))
            for e in range(NEXP):
                p.wait("tensor", "sV", C["b1Te"])
                for c4 in range(4):
                    bank = pb[4 + (c4 % 2)]
                    prior = (e, c4 - 2) if c4 >= 2 else ((e - 1, c4 + 2) if e >= 1 else None)
                    if prior is not None:
                        p.wait("tensor", "sV", C["sbc_ev"][prior])
                    l_ = esel[0:8, 128 * e:128 * (e + 1)]
                    r_ = scoresT[0:8, 512 * c4:512 * (c4 + 1)]
                    C["sbc_mm"][(e, c4)] = p.op("tensor", lambda E, o=bank[:, :], l=l_, r=r_:
                                                E.matmul(o, l, r, start=True, stop=True),
                                                inc="sT")
                    p.wait("vector", "sT", C["sbc_mm"][(e, c4)])
                    o_ = sbc[:, 512 * c4:512 * (c4 + 1)]
                    C["sbc_ev"][(e, c4)] = p.op("vector", lambda E, o=o_, i2=bank[:, :]:
                                                E.tensor_copy(o, i2), inc="sV")
                    if c4 == 3:
                        fsb = p.op("vector", lambda E: E.engine_nop(), inc="sVf")
                        p.wait("vector", "sVf", fsb)

                for c in range(NCH):
                    for k in range(NHB):
                        bank = pb[k % 2]
                        if k == 0:
                            p.wait("tensor", "dW", C["dw_w1"][(e, 0)])
                        if k == 8:
                            p.wait("tensor", "dW", C["dw_w1"][(e, 1)])
                        if k >= 2:
                            p.wait("tensor", "sS", C["relu"][(e, c, k - 2)])
                        elif c > 0:
                            p.wait("tensor", "sS", C["relu"][(e, c - 1, NHB - 2 + k)])
                        elif e > 0:
                            p.wait("tensor", "sS", C["relu"][(e - 1, NCH - 1, NHB - 2 + k)])
                        w1x = w1a if k < 8 else w1b
                        kk = k % 8
                        for b in range(NB):
                            l_ = w1x[:, 1024 * b + 128 * kk: 1024 * b + 128 * (kk + 1)]
                            r_ = xnT[:, SEQ * b + 512 * c: SEQ * b + 512 * (c + 1)]
                            cnt = p.op("tensor", lambda E, o=bank[:, :], l=l_, r=r_,
                                       st=(b == 0), sp=(b == NB - 1):
                                       E.matmul(o, l, r, start=st, stop=sp),
                                       inc="sT" if b == NB - 1 else None)
                        if c == NCH - 1 and k == 7:
                            C["h_last"][(e, 0)] = cnt
                        if c == NCH - 1 and k == NHB - 1:
                            C["h_last"][(e, 1)] = cnt
                        p.wait("scalar", "sT", cnt)
                        o_ = hs[:, 512 * k:512 * (k + 1)]
                        C["relu"][(e, c, k)] = p.op(
                            "scalar",
                            lambda E, o=o_, i2=bank[:, :], bi=b1T[:, 8 * k + e:8 * k + e + 1]:
                            E.activation(o, i2, AF.Relu, bias=bi), inc="sS")
                        p.wait("vector", "sS", C["relu"][(e, c, k)])
                        sb_ap = sbc[:, 512 * c:512 * (c + 1)]
                        C["hsmul"][(e, c, k)] = p.op(
                            "vector", lambda E, o=o_, s=sb_ap: E.tensor_tensor(o, o, s, ALU.mult),
                            inc="sV")
                    if e == 0 and c == 0:
                        p.wait("tensor", "dW", C["dw_b2"])
                    for dc in range(2):
                        p.wait("tensor", "dW", C["dw_w2"][(e, dc)])
                        for tt in range(4):
                            gidx = dc * 4 + tt
                            bank = pb[2 + (gidx % 2)]
                            tau = 4 * c + tt
                            if gidx >= 2:
                                p.wait("tensor", "sV", C["oadd"][(e, c, gidx - 2)])
                            elif c > 0:
                                p.wait("tensor", "sV", C["oadd"][(e, c - 1, 6 + gidx)])
                            elif e > 0:
                                p.wait("tensor", "sV", C["oadd"][(e - 1, NCH - 1, 6 + gidx)])
                            for k in range(NHB):
                                p.wait("tensor", "sV", C["hsmul"][(e, c, k)])
                                l_ = hs[:, 512 * k + 128 * tt: 512 * k + 128 * (tt + 1)]
                                r_ = w2sb[:, DM * k + 512 * dc: DM * k + 512 * (dc + 1)]
                                is_last = (k == NHB - 1 and e != 0)
                                cnt = p.op("tensor", lambda E, o=bank[:, :], l=l_, r=r_,
                                           st=(k == 0), sp=is_last:
                                           E.matmul(o, l, r, start=st, stop=sp),
                                           inc="sT" if is_last else None)
                            if e == 0:
                                l_ = scoresT[0:8, 128 * tau:128 * (tau + 1)]
                                r_ = b2sb[0:8, 512 * dc:512 * (dc + 1)]
                                cnt = p.op("tensor", lambda E, o=bank[:, :], l=l_, r=r_:
                                           E.matmul(o, l, r, start=False, stop=True), inc="sT")
                            C["w2mm"][(e, c, gidx)] = cnt
                            if c == NCH - 1 and gidx == 7:
                                C["w2_last"][e] = cnt
                            p.wait("vector", "sT", cnt)
                            xsl = xres[:, DM * tau + 512 * dc: DM * tau + 512 * (dc + 1)]
                            C["oadd"][(e, c, gidx)] = p.op(
                                "vector", lambda E, x=xsl, b2_=bank[:, :]: E.tensor_add(x, x, b2_),
                                inc="sV")
                            if e == NEXP - 1 and dc == 1:
                                w0 = C["oadd"][(e, c, tt)]
                                w1v = C["oadd"][(e, c, 4 + tt)]
                                p.wait("scalar", "sV", max(w0, w1v))
                                o_ = out_h[128 * tau:128 * (tau + 1), :]
                                i_ = xres[:, DM * tau:DM * (tau + 1)]
                                p.dma_op("scalar", "dO", lambda E, o=o_, i2=i_: E.dma_start(o, i2))
                if e + 1 < NEXP:
                    emit_w1_dma(e + 1, 0, C["h_last"][(e, 0)])
                    emit_w1_dma(e + 1, 1, C["h_last"][(e, 1)])
                    emit_w2_dma(e + 1, 0, C["w2_last"][e])
                    emit_w2_dma(e + 1, 1, C["w2_last"][e])

            p.wait("scalar", "dO", p.cnt["dO"])
            p.play_into(nc)

    return nc


_NC_CACHE = None


def kernel(L0_attn_x, L0_attn_wqkv, L0_ffn_Wg, L0_ffn_W1, L0_ffn_b1, L0_ffn_W2, L0_ffn_b2):
    global _NC_CACHE
    x = np.ascontiguousarray(np.asarray(L0_attn_x, dtype=np.float32))
    wqkv = np.ascontiguousarray(np.asarray(L0_attn_wqkv, dtype=np.float32))
    wg = np.asarray(L0_ffn_Wg, dtype=np.float32)
    wgt = np.ascontiguousarray(wg.T)
    w1 = np.ascontiguousarray(np.asarray(L0_ffn_W1, dtype=np.float32))
    b1 = np.ascontiguousarray(np.asarray(L0_ffn_b1, dtype=np.float32))
    w2 = np.ascontiguousarray(np.asarray(L0_ffn_W2, dtype=np.float32))
    b2 = np.ascontiguousarray(np.asarray(L0_ffn_b2, dtype=np.float32))

    i128f = np.eye(128, dtype=np.float32)
    i128b = np.eye(128, dtype=np.float32).astype(ml_dtypes.bfloat16)
    ustr = np.triu(np.ones((128, 128), dtype=np.float32), k=1).astype(ml_dtypes.bfloat16)
    ones1 = np.ones((1, 128), dtype=ml_dtypes.bfloat16)
    onesc = np.ones((128, 1), dtype=ml_dtypes.bfloat16)
    esel = np.zeros((NE, NE * 128), dtype=ml_dtypes.bfloat16)
    for e in range(NE):
        esel[e, 128 * e:128 * (e + 1)] = 1
    iota = (np.arange(NT, dtype=np.float32)[None, :] * 128.0
            + np.arange(128, dtype=np.float32)[:, None])

    if _NC_CACHE is None:
        _NC_CACHE = build_nc()
    nc = _NC_CACHE

    base = {
        "wqkv": wqkv, "wgt": wgt, "b1": b1, "b2": b2,
        "i128f": i128f, "i128b": i128b, "ustr": ustr, "ones1": ones1,
        "onesc": onesc, "esel": esel, "iota": np.ascontiguousarray(iota),
    }
    for e in range(NE):
        base["w1_%d" % e] = w1[e]
        base["w2_%d" % e] = w2[e]
    in_maps = [dict(base, x=x[bidx]) for bidx in range(NCORE)]
    res = run_bass_kernel_spmd(nc, in_maps, core_ids=list(range(NCORE)))
    global LAST_RES
    LAST_RES = res
    return np.stack([res.results[bidx]["out"] for bidx in range(NCORE)], axis=0)


# version 25 (final)
# speedup vs baseline: 1.0387x; 1.0002x over previous
import contextlib
import os

import numpy as np
import ml_dtypes
import concourse.bass as bass
import concourse.mybir as mybir
from concourse.bass_utils import run_bass_kernel_spmd

F32 = mybir.dt.float32
BF16 = mybir.dt.bfloat16
AF = mybir.ActivationFunctionType
ALU = mybir.AluOpType
AX = mybir.AxisListType

NCORE = 8
SEQ = 2048
DM = 1024
NT = 16
NB = 8
NH = 16
FQKV = 3072
NQC = 6
NE = 8
HID = 2048
NHB = 16
NCH = 4
EPS = 1e-5

_SEM_OWNER = {"sT": "tensor", "sV": "vector", "sS": "scalar"}


class Prog:
    ENG = ("tensor", "vector", "scalar", "gpsimd", "sync")

    def __init__(self):
        self.streams = {e: [] for e in self.ENG}
        self.sems = {}
        self.cnt = {}
        self.lastwait = {}

    def sem(self, name, handle):
        self.sems[name] = handle
        self.cnt[name] = 0

    def op(self, eng, fn, inc=None, inc_by=1):
        if inc is not None:
            self.cnt[inc] += inc_by
            h = self.sems[inc]
            self.streams[eng].append(lambda E, f=fn, h=h, k=inc_by: f(E).then_inc(h, k))
            return self.cnt[inc]
        self.streams[eng].append(fn)
        return None

    def wait(self, eng, sem, val):
        if val is None or val <= 0:
            return
        if _SEM_OWNER.get(sem) == eng:
            return
        assert val <= self.cnt[sem], (eng, sem, val, self.cnt[sem])
        key = (eng, sem)
        if self.lastwait.get(key, 0) >= val:
            return
        self.lastwait[key] = val
        h = self.sems[sem]
        self.streams[eng].append(lambda E, h=h, v=val: E.wait_ge(h, v))

    def dma_op(self, eng, sem, fn):
        self.wait(eng, sem, self.cnt[sem])
        return self.op(eng, fn, inc=sem, inc_by=16)

    def barrier_inc(self):
        h = self.sems["sB"]
        for e in self.ENG:
            if e == "sync":
                self.streams[e].append(lambda E, h=h: E.sem_inc(h, 1))
            else:
                self.streams[e].append(lambda E, h=h: E.drain().then_inc(h, 1))
        self.cnt["sB"] += 5

    def barrier_wait(self):
        h = self.sems["sB"]
        v = self.cnt["sB"]
        for e in self.ENG:
            self.streams[e].append(lambda E, h=h, v=v: E.wait_ge(h, v))

    def play_into(self, nc):
        with nc.Block() as block:
            streams = self.streams

            @block.tensor
            def _(E):
                for f in streams["tensor"]:
                    f(E)

            @block.vector
            def _(E):
                for f in streams["vector"]:
                    f(E)

            @block.scalar
            def _(E):
                for f in streams["scalar"]:
                    f(E)

            @block.gpsimd
            def _(E):
                for f in streams["gpsimd"]:
                    f(E)

            @block.sync
            def _(E):
                for f in streams["sync"]:
                    f(E)

        self.streams = {e: [] for e in self.ENG}
        self.lastwait = {}


def build_nc():
    nc = bass.Bass("TRN2", target_bir_lowering=False, detect_race_conditions=False)

    x_h = nc.dram_tensor("x", [SEQ, DM], F32, kind="ExternalInput").ap()
    wqkv_h = nc.dram_tensor("wqkv", [DM, FQKV], F32, kind="ExternalInput").ap()
    wgt_h = nc.dram_tensor("wgt", [NE, DM], F32, kind="ExternalInput").ap()
    b1_h = nc.dram_tensor("b1", [NE, HID], F32, kind="ExternalInput").ap()
    b2_h = nc.dram_tensor("b2", [NE, DM], F32, kind="ExternalInput").ap()
    w1_h = [nc.dram_tensor("w1_%d" % e, [DM, HID], F32, kind="ExternalInput").ap() for e in range(NE)]
    w2_h = [nc.dram_tensor("w2_%d" % e, [HID, DM], F32, kind="ExternalInput").ap() for e in range(NE)]
    i128f_h = nc.dram_tensor("i128f", [128, 128], F32, kind="ExternalInput").ap()
    i128b_h = nc.dram_tensor("i128b", [128, 128], BF16, kind="ExternalInput").ap()
    ustr_h = nc.dram_tensor("ustr", [128, 128], BF16, kind="ExternalInput").ap()
    ones1_h = nc.dram_tensor("ones1", [1, 128], BF16, kind="ExternalInput").ap()
    onesc_h = nc.dram_tensor("onesc", [128, 1], BF16, kind="ExternalInput").ap()
    esel_h = nc.dram_tensor("esel", [NE, NE * 128], BF16, kind="ExternalInput").ap()
    iota_h = nc.dram_tensor("iota", [128, NT], F32, kind="ExternalInput").ap()
    out_h = nc.dram_tensor("out", [SEQ, DM], F32, kind="ExternalOutput").ap()

    p = Prog()
    C = {}

    with contextlib.ExitStack() as stk:
        for s in ("sT", "sV", "sS", "sVf", "sSf", "dX", "dX2", "dW", "dO", "sB"):
            p.sem(s, stk.enter_context(nc.semaphore(s)))

        xres = stk.enter_context(nc.sbuf_tensor("xres", [128, NT * DM], F32))
        w1a = stk.enter_context(nc.sbuf_tensor("w1a", [128, NB * 1024], BF16))
        xnT = stk.enter_context(nc.sbuf_tensor("xnT", [128, NB * SEQ], BF16))
        scr32 = stk.enter_context(nc.sbuf_tensor("scr32", [128, DM], F32))
        xn2s = stk.enter_context(nc.sbuf_tensor("xn2s", [128, DM], F32))
        b1T = stk.enter_context(nc.sbuf_tensor("b1T", [128, 128], F32))
        wg_sb = stk.enter_context(nc.sbuf_tensor("wg_sb", [128, 64], F32))
        scoresT = stk.enter_context(nc.sbuf_tensor("scoresT", [8, SEQ], BF16))
        carryb = stk.enter_context(nc.sbuf_tensor("carryb", [1, DM], BF16))
        i128f = stk.enter_context(nc.sbuf_tensor("i128f_s", [128, 128], F32))
        i128b = stk.enter_context(nc.sbuf_tensor("i128b_s", [128, 128], BF16))
        ustr = stk.enter_context(nc.sbuf_tensor("ustr_s", [128, 128], BF16))
        ones1 = stk.enter_context(nc.sbuf_tensor("ones1_s", [1, 128], BF16))
        onesc = stk.enter_context(nc.sbuf_tensor("onesc_s", [128, 1], BF16))
        esel = stk.enter_context(nc.sbuf_tensor("esel_s", [NE, NE * 128], BF16))
        iota = stk.enter_context(nc.sbuf_tensor("iota_s", [128, NT], F32))
        sdot = stk.enter_context(nc.sbuf_tensor("sdot", [128, NH], F32))
        tmin = stk.enter_context(nc.sbuf_tensor("tmin", [128, NH], F32))
        tneg = stk.enter_context(nc.sbuf_tensor("tneg", [128, NH], F32))
        e_sb = stk.enter_context(nc.sbuf_tensor("e_sb", [128, NH], F32))
        z_sb = stk.enter_context(nc.sbuf_tensor("z_sb", [128, NH], F32))
        den = stk.enter_context(nc.sbuf_tensor("den", [128, NH], F32))
        rr = stk.enter_context(nc.sbuf_tensor("rr", [128, NH], F32))
        a_sb = stk.enter_context(nc.sbuf_tensor("a_sb", [128, NH], F32))
        bb_sb = stk.enter_context(nc.sbuf_tensor("bb_sb", [128, NH], F32))
        ln = {}
        for t in ("sum", "sumsq", "var", "lnv", "rstd", "negc"):
            for s in (2,):
                ln[(t, s)] = stk.enter_context(nc.sbuf_tensor("ln_%s_%d" % (t, s), [128, 1], F32))
        l1 = {}
        for t in ("sum", "sumsq", "d", "lnv", "rstd", "negc"):
            l1[t] = stk.enter_context(nc.sbuf_tensor("l1_%s" % t, [128, NT], F32))
        lgall = stk.enter_context(nc.sbuf_tensor("lgall", [128, 128], F32))
        gA = stk.enter_context(nc.sbuf_tensor("gA", [128, 128], F32))
        keepall = stk.enter_context(nc.sbuf_tensor("keepall", [128, 128], F32))
        exall = stk.enter_context(nc.sbuf_tensor("exall", [128, 128], F32))
        gm1a = stk.enter_context(nc.sbuf_tensor("gm1a", [128, NT], F32))
        gm2a = stk.enter_context(nc.sbuf_tensor("gm2a", [128, NT], F32))
        gnm1a = stk.enter_context(nc.sbuf_tensor("gnm1a", [128, NT], F32))
        gzsa = stk.enter_context(nc.sbuf_tensor("gzsa", [128, NT], F32))
        grza = stk.enter_context(nc.sbuf_tensor("grza", [128, NT], F32))

        pb = [stk.enter_context(nc.psum_tensor("pb%d" % i, [128, 512], F32)) for i in range(8)]

        def emit_ln(which, j, x_ap, out_ap, junk):
            key = "norm%d" % which
            s_sum = ln[("sum", which)]; s_sq = ln[("sumsq", which)]
            s_d = ln[("var", which)]
            s_lnv = ln[("lnv", which)]; s_rstd = ln[("rstd", which)]; s_negc = ln[("negc", which)]
            prev = C.setdefault(key, {}).get(j - 1)
            if prev:
                p.wait("vector", "sS", prev)
            p.op("vector", lambda E: E.reduce_sum(s_sum[:, :], x_ap, AX.X))
            f1 = p.op("vector", lambda E: E.scalar_tensor_tensor(
                junk[:, :], x_ap, 1.0, x_ap, ALU.mult, ALU.mult, accum_out=s_sq[:, :]),
                inc="sVf")
            p.wait("vector", "sVf", f1)
            f2 = p.op("vector", lambda E: E.tensor_scalar(
                s_d[:, :], s_sum[:, :], s_sum[:, :], 1.0 / DM, ALU.mult, ALU.mult), inc="sVf")
            p.wait("vector", "sVf", f2)
            vstats = p.op("vector", lambda E: E.scalar_tensor_tensor(
                s_d[:, :], s_d[:, :], -1.0, s_sq[:, :], ALU.mult, ALU.add), inc="sV")
            p.wait("scalar", "sV", vstats)
            g1 = p.op("scalar", lambda E: E.activation(
                s_lnv[:, :], s_d[:, :], AF.Ln,
                bias=np.float32(EPS), scale=np.float32(1.0 / DM)), inc="sSf")
            p.wait("scalar", "sSf", g1)
            srst = p.op("scalar", lambda E: E.activation(s_rstd[:, :], s_lnv[:, :], AF.Exp,
                                                         scale=np.float32(-0.5)),
                        inc="sS")
            p.wait("vector", "sS", srst)
            vneg = p.op("vector", lambda E: E.tensor_scalar(
                s_negc[:, :], s_sum[:, :], s_rstd[:, :], -1.0 / DM,
                ALU.mult, ALU.mult), inc="sV")
            p.wait("scalar", "sV", vneg)
            nrm = p.op("scalar", lambda E: E.activation(
                out_ap, x_ap, AF.Identity, bias=s_negc[:, :], scale=s_rstd[:, :]), inc="sS")
            C[key][j] = nrm
            return nrm

        with (
            nc.sbuf_tensor("wqkvsb", [128, NB * FQKV], BF16) as wqkvsb,
            nc.sbuf_tensor("qkvt", [128, 2 * FQKV], BF16) as qkvt,
            nc.sbuf_tensor("junk", [128, DM], BF16) as junk,
            nc.sbuf_tensor("xn1tm", [128, 2 * DM], F32) as xn1tm,
        ):
            wgtsb = xn2s
            qkvt_sl = [qkvt[:, FQKV * k:FQKV * (k + 1)] for k in range(2)]
            qkvt3s = [qkvt[:, FQKV * k:FQKV * (k + 1)].rearrange("p (h c) -> p h c", h=NH)
                      for k in range(2)]
            xnT3 = xnT[:, :].rearrange("p (b t) -> p b t", b=NB)

            C["dx_x"] = {}
            C["dx_sem"] = {}
            for j in range(4):
                src = x_h[128 * j:128 * (j + 1), :]
                dst = xres[:, DM * j:DM * (j + 1)]
                if j % 2 == 0:
                    eng, sem = "sync", "dX"
                else:
                    eng, sem = "gpsimd", "dW"
                C["dx_x"][j] = p.dma_op(eng, sem, lambda E, s=src, d=dst: E.dma_start(d, s))
                C["dx_sem"][j] = sem
            C["dx_i128f"] = p.dma_op("sync", "dX", lambda E: E.dma_start(i128f[:, :], i128f_h))
            for src, dst in ((i128b_h, i128b), (ustr_h, ustr),
                             (ones1_h, ones1), (onesc_h, onesc), (iota_h, iota), (esel_h, esel)):
                p.dma_op("sync", "dX", lambda E, s=src, d=dst: E.dma_start(d[:, :], s))
            p.dma_op("sync", "dX", lambda E: E.dma_start(wgtsb[0:8, :], wgt_h))
            C["dx_consts"] = p.cnt["dX"]
            for j in range(4, 8):
                src = x_h[128 * j:128 * (j + 1), :]
                dst = xres[:, DM * j:DM * (j + 1)]
                sem = "dX" if j % 2 == 0 else "dX2"
                C["dx_x"][j] = p.dma_op("sync", sem, lambda E, s=src, d=dst: E.dma_start(d, s))
                C["dx_sem"][j] = sem
            
            for j in range(8, NT):
                src = x_h[128 * j:128 * (j + 1), :]
                dst = xres[:, DM * j:DM * (j + 1)]
                C["dx_x"][j] = p.dma_op("scalar", "dO", lambda E, s=src, d=dst: E.dma_start(d, s))
                C["dx_sem"][j] = "dO"

            wq_src = wqkv_h.rearrange("(b p) f -> p b f", p=128)
            wq_dst = wqkvsb[:, :].rearrange("p (b f) -> p b f", b=NB)
            C["dw_wqkv"] = {}
            for c in range(NQC):
                s = wq_src[:, :, 512 * c:512 * (c + 1)]
                d = wq_dst[:, :, 512 * c:512 * (c + 1)]
                C["dw_wqkv"][c] = p.dma_op("gpsimd", "dW", lambda E, s=s, d=d: E.dma_start(d, s))
            C["dw_w1"] = {}
            s0 = w1_h[0].rearrange("(b p) h -> p b h", p=128)[:, :, 0:1024]
            w1a3 = w1a[:, :].rearrange("p (b h) -> p b h", b=NB)
            C["dw_w1"][(0, 0)] = p.dma_op("gpsimd", "dW", lambda E: E.dma_start(w1a3, s0))

            def emit_wg():
                p.wait("tensor", "dX", C["dx_consts"])
                for b in range(NB):
                    i_ = wgtsb[0:8, 128 * b:128 * (b + 1)]
                    o_ = pb[4][:, 8 * b:8 * (b + 1)]
                    last = p.op("tensor", lambda E, o=o_, i2=i_: E.transpose(o, i2, i128f[0:8, 0:8]),
                                inc="sT" if b == NB - 1 else None)
                C["wgT"] = last
                p.wait("vector", "sT", C["wgT"])
                C["wg_evac"] = p.op("vector", lambda E: E.tensor_copy(wg_sb[:, :], pb[4][:, 0:64]),
                                    inc="sV")

            for name in ("t1a", "t1b", "xn1T_evac", "qkv_mm", "qkv_evac", "ab", "cums",
                         "carryF", "combS", "combV1", "combV2", "prepTr", "xn2f32", "pb5free",
                         "xn2bf", "logits", "lgc"):
                C[name] = {}

            C["carry0"] = p.op("vector", lambda E: E.memset(carryb[:, :], 0), inc="sV")

            qkv_banks = [pb[0], pb[1]]

            C["l1negc"] = {}
            GRPS = [(0, 2), (2, 4), (4, 8), (8, 12), (12, 16)]
            C["tile2grp"] = {}
            for g, (glo, ghi) in enumerate(GRPS):
                for j in range(glo, ghi):
                    C["tile2grp"][j] = g
                fX = None
                for j in range(glo, ghi):
                    p.wait("vector", C["dx_sem"][j], C["dx_x"][j])
                    x_t = xres[:, DM * j:DM * (j + 1)]
                    p.op("vector", lambda E, x=x_t, jj=j: E.reduce_sum(
                        l1["sum"][:, jj:jj + 1], x, AX.X))
                    fX = p.op("vector", lambda E, x=x_t, jj=j: E.scalar_tensor_tensor(
                        junk[:, :], x, 1.0, x, ALU.mult, ALU.mult,
                        accum_out=l1["sumsq"][:, jj:jj + 1]), inc="sVf")
                p.wait("vector", "sVf", fX)
                sl0, sl1 = glo, ghi
                f2 = p.op("vector", lambda E, a=sl0, b=sl1: E.tensor_mul(
                    l1["d"][:, a:b], l1["sum"][:, a:b], l1["sum"][:, a:b]), inc="sVf")
                p.wait("vector", "sVf", f2)
                vst = p.op("vector", lambda E, a=sl0, b=sl1: E.scalar_tensor_tensor(
                    l1["d"][:, a:b], l1["d"][:, a:b], -1.0 / DM, l1["sumsq"][:, a:b],
                    ALU.mult, ALU.add), inc="sV")
                p.wait("scalar", "sV", vst)
                g1 = p.op("scalar", lambda E, a=sl0, b=sl1: E.activation(
                    l1["lnv"][:, a:b], l1["d"][:, a:b], AF.Ln,
                    bias=np.float32(EPS), scale=np.float32(1.0 / DM)), inc="sSf")
                p.wait("scalar", "sSf", g1)
                g2 = p.op("scalar", lambda E, a=sl0, b=sl1: E.activation(
                    l1["rstd"][:, a:b], l1["lnv"][:, a:b], AF.Exp,
                    scale=np.float32(-0.5)), inc="sS")
                p.wait("vector", "sS", g2)
                f3 = p.op("vector", lambda E, a=sl0, b=sl1: E.tensor_mul(
                    l1["negc"][:, a:b], l1["sum"][:, a:b], l1["rstd"][:, a:b]), inc="sVf")
                p.wait("vector", "sVf", f3)
                C["l1negc"][g] = p.op("vector", lambda E, a=sl0, b=sl1: E.tensor_scalar_mul(
                    l1["negc"][:, a:b], l1["negc"][:, a:b], -1.0 / DM), inc="sV")

            LIM = int(os.environ.get("KMK_NTI", str(NT)))
            ST = os.environ.get("KMK_ST", "I")
            def emit_gate(tlo, thi):
                for i in range(tlo, thi):
                    p.op("vector", lambda E, ii=i: E.reduce_max(
                        gm1a[:, ii:ii + 1], lgall[:, 8 * ii:8 * (ii + 1)], AX.X))
                fg = p.op("vector", lambda E: E.engine_nop(), inc="sVf")
                p.wait("vector", "sVf", fg)
                for i in range(tlo, thi):
                    p.op("vector", lambda E, ii=i: E.tensor_scalar(
                        gA[:, 8 * ii:8 * (ii + 1)], lgall[:, 8 * ii:8 * (ii + 1)],
                        gm1a[:, ii:ii + 1], None, ALU.is_equal))
                fg = p.op("vector", lambda E: E.engine_nop(), inc="sVf")
                p.wait("vector", "sVf", fg)
                for i in range(tlo, thi):
                    p.op("vector", lambda E, ii=i: E.scalar_tensor_tensor(
                        gA[:, 8 * ii:8 * (ii + 1)], gA[:, 8 * ii:8 * (ii + 1)], -1e30,
                        lgall[:, 8 * ii:8 * (ii + 1)], ALU.mult, ALU.add))
                fg = p.op("vector", lambda E: E.engine_nop(), inc="sVf")
                p.wait("vector", "sVf", fg)
                for i in range(tlo, thi):
                    p.op("vector", lambda E, ii=i: E.reduce_max(
                        gm2a[:, ii:ii + 1], gA[:, 8 * ii:8 * (ii + 1)], AX.X))
                fg = p.op("vector", lambda E: E.engine_nop(), inc="sVf")
                p.wait("vector", "sVf", fg)
                for i in range(tlo, thi):
                    p.op("vector", lambda E, ii=i: E.tensor_scalar(
                        keepall[:, 8 * ii:8 * (ii + 1)], lgall[:, 8 * ii:8 * (ii + 1)],
                        gm2a[:, ii:ii + 1], None, ALU.is_ge))
                g1 = p.op("vector", lambda E, a=tlo, b=thi: E.tensor_scalar_mul(
                    gnm1a[:, a:b], gm1a[:, a:b], -1.0), inc="sV")
                p.wait("scalar", "sV", g1)
                for i in range(tlo, thi):
                    cnt = p.op("scalar", lambda E, ii=i: E.activation(
                        exall[:, 8 * ii:8 * (ii + 1)], lgall[:, 8 * ii:8 * (ii + 1)],
                        AF.Exp, bias=gnm1a[:, ii:ii + 1]),
                        inc="sS" if i == thi - 1 else None)
                p.wait("vector", "sS", cnt)
                fg = p.op("vector", lambda E, a=tlo, b=thi: E.tensor_mul(
                    exall[:, 8 * a:8 * b], exall[:, 8 * a:8 * b], keepall[:, 8 * a:8 * b]),
                    inc="sVf")
                p.wait("vector", "sVf", fg)
                for i in range(tlo, thi):
                    p.op("vector", lambda E, ii=i: E.reduce_sum(
                        gzsa[:, ii:ii + 1], exall[:, 8 * ii:8 * (ii + 1)], AX.X))
                fg = p.op("vector", lambda E: E.engine_nop(), inc="sVf")
                p.wait("vector", "sVf", fg)
                fg = p.op("vector", lambda E, a=tlo, b=thi: E.reciprocal(
                    grza[:, a:b], gzsa[:, a:b]), inc="sVf")
                p.wait("vector", "sVf", fg)
                for i in range(tlo, thi):
                    cnt = p.op("vector", lambda E, ii=i: E.tensor_scalar(
                        gA[:, 8 * ii:8 * (ii + 1)], exall[:, 8 * ii:8 * (ii + 1)],
                        grza[:, ii:ii + 1], None, ALU.mult),
                        inc="sV" if i == thi - 1 else None)
                return cnt

            def emit_norm1(jj):
                x_t2 = xres[:, DM * jj:DM * (jj + 1)]
                if jj >= 2:
                    p.wait("scalar", "sT", C["t1b"][jj - 2])
                p.wait("scalar", "sV", C["l1negc"][C["tile2grp"][jj]])
                C.setdefault("norm1", {})[jj] = p.op(
                    "scalar", lambda E, x=x_t2, j3=jj:
                    E.activation(xn1tm[:, DM * (j3 % 2):DM * (j3 % 2 + 1)], x, AF.Identity,
                                 bias=l1["negc"][:, j3:j3 + 1],
                                 scale=l1["rstd"][:, j3:j3 + 1]), inc="sS")

            for it in range(LIM + 1):
                j = it
                i = it - 1
                if j < LIM:
                    x_t = xres[:, DM * j:DM * (j + 1)]
                    if j == 0:
                        emit_norm1(0)
                    if ST < "B":
                        continue
                    p.wait("tensor", "dX", C["dx_i128f"])
                    p.wait("tensor", "sS", C["norm1"][j])
                    for b in range(NB):
                        if b == 0 and i - 1 >= 0:
                            p.wait("tensor", "sS", C["pb5free"][i - 1])
                        if b == 4 and i - 1 >= 0:
                            p.wait("tensor", "sS", C["xn2bf"][i - 1])
                        bank = pb[5] if b < 4 else pb[6]
                        o_ = bank[:, 128 * (b % 4):128 * (b % 4 + 1)]
                        i_ = xn1tm[:, DM * (j % 2) + 128 * b:DM * (j % 2) + 128 * (b + 1)]
                        cnt = p.op("tensor", lambda E, o=o_, i2=i_: E.transpose(o, i2, i128f[:, :]),
                                   inc="sT" if b in (3, 7) else None)
                        if b == 3:
                            C["t1a"][j] = cnt
                        elif b == 7:
                            C["t1b"][j] = cnt
                    for half in range(2):
                        p.wait("scalar", "sT", C["t1a"][j] if half == 0 else C["t1b"][j])
                        bank = pb[5] if half == 0 else pb[6]
                        i_ = bank[:, 0:512].rearrange("p (b t) -> p b t", b=4)
                        o_ = xnT3[:, 4 * half:4 * (half + 1), 128 * j:128 * (j + 1)]
                        cnt = p.op("scalar", lambda E, o=o_, i2=i_: E.copy(o, i2),
                                   inc="sS" if half == 1 else None)
                    C["xn1T_evac"][j] = cnt
                    if j + 1 < LIM:
                        emit_norm1(j + 1)

                if i >= 0 and ST >= "D":
                    p.wait("vector", "sS", C["qkv_evac"][NQC * i + NQC - 1])
                    if i >= 1:
                        p.wait("vector", "sS", C["combS"][i - 1])
                    q3 = qkvt3s[i % 2]
                    p.op("vector", lambda E, q3=q3: E.scalar_tensor_tensor(
                        q3[:, :, 0:64], q3[:, :, 0:64], 1.0, q3[:, :, 64:128],
                        ALU.mult, ALU.mult))
                    fD = p.op("vector", lambda E, q3=q3: E.tensor_reduce(
                        sdot[:, :], q3[:, :, 0:64], AX.X, ALU.add), inc="sVf")
                    p.wait("vector", "sVf", fD)
                    p.op("vector", lambda E: E.tensor_scalar(
                        tmin[:, :], sdot[:, :], 0.125, 0.0, ALU.mult, ALU.min))
                    dmm = p.op("vector", lambda E: E.tensor_scalar(
                        tneg[:, :], sdot[:, :], -0.125, 0.0, ALU.mult, ALU.min), inc="sV")
                    p.wait("scalar", "sV", dmm)
                    p.op("scalar", lambda E: E.activation(e_sb[:, :], tmin[:, :], AF.Exp))
                    eS = p.op("scalar", lambda E: E.activation(z_sb[:, :], tneg[:, :], AF.Exp),
                              inc="sS")
                    p.wait("vector", "sS", eS)
                    fD2 = p.op("vector", lambda E, ii=i: E.scalar_tensor_tensor(
                        den[:, :], z_sb[:, :], iota[:, ii:ii + 1], e_sb[:, :], ALU.mult, ALU.add),
                        inc="sVf")
                    p.wait("vector", "sVf", fD2)
                    fD3 = p.op("vector", lambda E: E.reciprocal(rr[:, :], den[:, :]), inc="sVf")
                    p.wait("vector", "sVf", fD3)
                    p.op("vector", lambda E: E.tensor_mul(a_sb[:, :], z_sb[:, :], rr[:, :]))
                    C["ab"][i] = p.op("vector", lambda E: E.tensor_mul(
                        bb_sb[:, :], e_sb[:, :], rr[:, :]), inc="sV")
                    p.wait("tensor", "sS", C["qkv_evac"][NQC * i + NQC - 1])
                    p.wait("tensor", "sV", C["carryF"].get(i - 1, C["carry0"]))
                    if i >= 1:
                        p.wait("tensor", "sV", C["combV2"][i - 1])
                    for half in range(2):
                        bank = pb[2 + half]
                        v_ap = qkvt3s[i % 2][:, 8 * half:8 * (half + 1), 128:192]
                        c_ap = carryb[0:1, 512 * half:512 * (half + 1)]
                        p.op("tensor", lambda E, o=bank[:, :], v=v_ap:
                             E.matmul(o, ustr[:, :], v, start=True, stop=False))
                        cnt = p.op("tensor", lambda E, o=bank[:, :], cc=c_ap:
                                   E.matmul(o, ones1[:, :], cc, start=False, stop=True), inc="sT")
                    C["cums"][i] = cnt
                    if ST < "F":
                        continue
                    for half in range(2):
                        v_ap = qkvt3s[i % 2][:, 8 * half:8 * (half + 1), 128:192]
                        if half == 1:
                            p.wait("tensor", "sV", C["colA0"])
                        colT = p.op("tensor", lambda E, v=v_ap:
                                    E.matmul(pb[7][0:1, :], onesc[:, :], v, start=True, stop=True),
                                    inc="sT")
                        C.setdefault("colT", {})[i] = colT
                        p.wait("vector", "sT", colT)
                        c_ap = carryb[0:1, 512 * half:512 * (half + 1)]
                        cnt = p.op("vector", lambda E, cc=c_ap: E.tensor_add(cc, pb[7][0:1, :], cc),
                                   inc="sV")
                        if half == 0:
                            C["colA0"] = cnt
                    C["carryF"][i] = cnt
                    if ST < "G":
                        continue
                    p.wait("scalar", "sT", C["cums"][i])
                    p.wait("scalar", "sV", C["carryF"][i])
                    p.wait("scalar", "sV", C["ab"][i])
                    combs_cnt = {}
                    for h in range(NH):
                        bank = pb[2] if h < 8 else pb[3]
                        sl = bank[:, 64 * (h % 8):64 * (h % 8 + 1)]
                        combs_cnt[h] = p.op("scalar", lambda E, s=sl, hh=h: E.activation(
                            s, s, AF.Copy, scale=a_sb[:, hh:hh + 1]), inc="sS")
                    C["combS"][i] = combs_cnt[NH - 1]
                    for h in range(NH):
                        p.wait("vector", "sS", combs_cnt[7] if h < 8 else combs_cnt[15])
                        bank = pb[2] if h < 8 else pb[3]
                        sl = bank[:, 64 * (h % 8):64 * (h % 8 + 1)]
                        v_ap = qkvt_sl[i % 2][:, 192 * h + 128:192 * h + 192]
                        cnt = p.op("vector", lambda E, s=sl, v=v_ap, hh=h: E.scalar_tensor_tensor(
                            s, v, bb_sb[:, hh:hh + 1], s, ALU.mult, ALU.add),
                            inc="sV" if h == NH - 1 else None)
                    C["combV1"][i] = cnt
                    fC = p.op("vector", lambda E: E.engine_nop(), inc="sVf")
                    p.wait("vector", "sVf", fC)
                    for half in range(2):
                        bank = pb[2 + half]
                        xsl = xres[:, DM * i + 512 * half: DM * i + 512 * (half + 1)]
                        cnt = p.op("vector", lambda E, x=xsl, b2_=bank[:, :]:
                                   E.tensor_add(x, x, b2_), inc="sV" if half == 1 else None)
                    C["combV2"][i] = cnt

                if j < LIM and ST >= "C":
                    if j >= 2 and ST >= "G":
                        p.wait("scalar", "sT", C["colT"][j - 2])
                        p.wait("scalar", "sV", C["combV1"][j - 2])
                    for c in range(NQC):
                        g = NQC * j + c
                        bank = qkv_banks[g % 2]
                        p.wait("tensor", "dW", C["dw_wqkv"][c])
                        p.wait("tensor", "sS", C["xn1T_evac"][j])
                        if g - 2 >= 0:
                            p.wait("tensor", "sS", C["qkv_evac"][g - 2])
                        for b in range(NB):
                            l_ = xnT[:, SEQ * b + 128 * j: SEQ * b + 128 * (j + 1)]
                            r_ = wqkvsb[:, FQKV * b + 512 * c: FQKV * b + 512 * (c + 1)]
                            cnt = p.op("tensor", lambda E, o=bank[:, :], l=l_, r=r_,
                                       st=(b == 0), sp=(b == NB - 1):
                                       E.matmul(o, l, r, start=st, stop=sp),
                                       inc="sT" if b == NB - 1 else None)
                        C["qkv_mm"][(j, c)] = cnt
                        p.wait("scalar", "sT", cnt)
                        o_ = qkvt_sl[j % 2][:, 512 * c:512 * (c + 1)]
                        C["qkv_evac"][g] = p.op("scalar", lambda E, o=o_, i2=bank[:, :]:
                                                E.copy(o, i2), inc="sS")

                if it == 0:
                    emit_wg()

                if i >= 0 and ST >= "H":
                    if i >= 1:
                        p.wait("scalar", "sT", C["prepTr"][i - 1])
                    x_i = xres[:, DM * i:DM * (i + 1)]
                    emit_ln(2, i, x_i, scr32[:, :], junk)
                    if ST < "I":
                        continue
                    p.wait("tensor", "sS", C["norm2"][i])
                    p.wait("tensor", "sS", C["xn1T_evac"].get(i + 1, C["xn1T_evac"][i]))
                    for b in range(NB):
                        bank = pb[5] if b < 4 else pb[6]
                        o_ = bank[:, 128 * (b % 4):128 * (b % 4 + 1)]
                        i_ = scr32[:, 128 * b:128 * (b + 1)]
                        cnt = p.op("tensor", lambda E, o=o_, i2=i_: E.transpose(o, i2, i128f[:, :]),
                                   inc="sT" if b == NB - 1 else None)
                    C["prepTr"][i] = cnt
                    p.wait("scalar", "sT", C["prepTr"][i])
                    if i >= 1:
                        p.wait("scalar", "sT", C["logits"][i - 1])
                    else:
                        p.wait("scalar", "sT", C["wgT"])
                    p.op("scalar", lambda E: E.copy(xn2s[:, 0:512], pb[5][:, 0:512]))
                    o_ = xnT3[:, 0:4, 128 * i:128 * (i + 1)]
                    i_ = pb[5][:, 0:512].rearrange("p (b t) -> p b t", b=4)
                    c5 = p.op("scalar", lambda E, o=o_, i2=i_: E.copy(o, i2), inc="sS")
                    cF = p.op("scalar", lambda E: E.copy(xn2s[:, 512:1024], pb[6][:, 0:512]),
                              inc="sS")
                    o_ = xnT3[:, 4:8, 128 * i:128 * (i + 1)]
                    i_ = pb[6][:, 0:512].rearrange("p (b t) -> p b t", b=4)
                    cnt = p.op("scalar", lambda E, o=o_, i2=i_: E.copy(o, i2), inc="sS")
                    C["pb5free"][i] = c5
                    C["xn2bf"][i] = cnt
                    p.wait("tensor", "sS", cF)
                    p.wait("tensor", "sV", C["lgc"].get(i - 1, C["wg_evac"]))
                    for b in range(NB):
                        l_ = xn2s[:, 128 * b:128 * (b + 1)]
                        r_ = wg_sb[:, 8 * b:8 * (b + 1)]
                        cnt = p.op("tensor", lambda E, l=l_, r=r_, st=(b == 0), sp=(b == NB - 1):
                                   E.matmul(pb[4][:, 0:8], l, r, start=st, stop=sp),
                                   inc="sT" if b == NB - 1 else None)
                    C["logits"][i] = cnt
                    p.wait("vector", "sT", C["logits"][i])
                    C["lgc"][i] = p.op("vector", lambda E, ii=i: E.tensor_copy(
                        lgall[:, 8 * ii:8 * (ii + 1)], pb[4][:, 0:8]), inc="sV")
                    if i == 7 and LIM == NT:
                        C["gate2a"] = emit_gate(0, 8)
                    if i == 11 and LIM == NT:
                        C["gate2b1"] = emit_gate(8, 12)

            if LIM == NT:
                gate2all = max(C["gate2a"], C["gate2b1"], emit_gate(12, NT))
            elif LIM >= 1:
                gate2all = emit_gate(0, LIM)
            else:
                gate2all = 0
            p.wait("tensor", "sV", gate2all)
            if LIM >= 1:
                p.wait("tensor", "sV", C["lgc"][LIM - 1])
            if LIM >= 1 and (LIM - 1) in C["xn2bf"]:
                p.wait("tensor", "sS", C["xn2bf"][LIM - 1])
                p.wait("tensor", "sV", C["carryF"][LIM - 1])
            sco_tr = {}
            for i in range(LIM):
                bank = pb[4 + i // 4]
                o_ = bank[0:8, 128 * (i % 4):128 * (i % 4 + 1)]
                i_ = gA[:, 8 * i:8 * (i + 1)]
                cnt = p.op("tensor", lambda E, o=o_, i2=i_: E.transpose(o, i2, i128f[:, :]),
                           inc="sT" if i % 4 == 3 else None)
                if i % 4 == 3:
                    sco_tr[i // 4] = cnt
            scoresT3 = scoresT[0:8, :].rearrange("e (t p) -> e t p", t=NT)
            for k in range(LIM // 4):
                p.wait("vector", "sT", sco_tr[k])
                o_ = scoresT3[0:8, 4 * k:4 * (k + 1), :]
                i_ = pb[4 + k][0:8, 0:512].rearrange("e (t p) -> e t p", t=4)
                p.op("vector", lambda E, o=o_, i2=i_: E.tensor_copy(o, i2), inc="sV")

            if os.environ.get("KMK_PH", "ABC") != "ABC":
                p.barrier_inc()
            else:
                C["ph_sT"] = p.cnt["sT"]
                C["ph_sV"] = p.cnt["sV"]
                C["ph_sS"] = p.cnt["sS"]
                C["sco_last"] = p.cnt["sV"]
            p.play_into(nc)

        if os.environ.get("KMK_DBG"):
            dbg_xnT = nc.dram_tensor("dbg_xnT", [128, NB * SEQ], F32, kind="ExternalOutput").ap()
            dbg_qkvt = nc.dram_tensor("dbg_qkvt", [128, FQKV], F32, kind="ExternalOutput").ap()
            dbg_misc = nc.dram_tensor("dbg_misc", [128, 128], F32, kind="ExternalOutput").ap()
            dbg_xn1 = nc.dram_tensor("dbg_xn1", [128, DM], F32, kind="ExternalOutput").ap()
            dbg_carry = nc.dram_tensor("dbg_carry", [1, DM], F32, kind="ExternalOutput").ap()
            misc_srcs = [sdot, tmin, tneg, e_sb, z_sb, den, rr, a_sb]
            p.wait("gpsimd", "sT", p.cnt["sT"])
            p.wait("gpsimd", "sV", p.cnt["sV"])
            p.wait("gpsimd", "sS", p.cnt["sS"])
            p.dma_op("gpsimd", "dW", lambda E: E.dma_start(dbg_xnT, xnT[:, :]))
            p.dma_op("gpsimd", "dW", lambda E: E.dma_start(dbg_qkvt, qkvt[:, 0:FQKV]))
            p.dma_op("gpsimd", "dW", lambda E: E.dma_start(dbg_xn1, xn1tm[:, :]))
            p.dma_op("gpsimd", "dW", lambda E: E.dma_start(dbg_carry, carryb[:, :]))
            for mi, msrc in enumerate(misc_srcs):
                p.dma_op("gpsimd", "dW",
                         lambda E, o=dbg_misc[0:128, 16 * mi:16 * (mi + 1)], s=msrc:
                         E.dma_start(o, s[:, :]))
            dbg_ln = nc.dram_tensor("dbg_ln", [128, 12], F32, kind="ExternalOutput").ap()
            ln_srcs = [ln[(t, s)] for s in (1, 2)
                       for t in ("sum", "sumsq", "var", "lnv", "rstd", "negc")]
            for mi, msrc in enumerate(ln_srcs):
                p.dma_op("gpsimd", "dW",
                         lambda E, o=dbg_ln[0:128, mi:mi + 1], s=msrc:
                         E.dma_start(o, s[:, :], allow_slow_non_contiguous=True))
            p.wait("gpsimd", "dW", p.cnt["dW"])

        _PH = os.environ.get("KMK_PH", "ABC")
        if _PH == "A":
            p.barrier_wait()
            for tau in range(NT):
                o_ = out_h[128 * tau:128 * (tau + 1), :]
                i_ = xres[:, DM * tau:DM * (tau + 1)]
                p.dma_op("scalar", "dO", lambda E, o=o_, i2=i_: E.dma_start(o, i2))
            p.wait("scalar", "dO", 16 * NT)
            p.play_into(nc)
            return nc

        with (
            nc.sbuf_tensor("w2sb", [128, NHB * DM], BF16) as w2sb,
            nc.sbuf_tensor("hs", [128, NHB * 512], BF16) as hs,
            nc.sbuf_tensor("sbc", [128, SEQ], BF16) as sbc,
            nc.sbuf_tensor("b2sb", [8, DM], BF16) as b2sb,
            nc.sbuf_tensor("w1b", [128, NB * 1024], BF16) as w1b,
        ):
            if os.environ.get("KMK_PH", "ABC") != "ABC":
                p.barrier_wait()
            else:
                LL = LIM - 1
                p.wait("gpsimd", "sT", C["logits"][LL])
                p.wait("gpsimd", "sV", C["combV1"][LL])
                p.wait("gpsimd", "sS", C["xn2bf"][LL])
                p.wait("sync", "sT", C["logits"][LL])
                p.wait("tensor", "sV", C["sco_last"])
                p.wait("tensor", "sS", C["xn2bf"][LL])
            for half in range(2):
                dxb = p.dma_op("sync", "dX", lambda E, h2=half: E.dma_start(
                    xn2s[0:8, :], b1_h[0:8, 1024 * h2:1024 * (h2 + 1)]))
                p.wait("tensor", "dX", dxb)
                for k in range(8):
                    kk = 8 * half + k
                    i_ = xn2s[0:8, 128 * k:128 * (k + 1)]
                    o_ = pb[5][:, 8 * kk:8 * (kk + 1)]
                    cnt = p.op("tensor", lambda E, o=o_, i2=i_: E.transpose(o, i2, i128f[0:8, 0:8]),
                               inc="sT")
                if half == 0:
                    p.wait("sync", "sT", cnt)
            C["b1Tr"] = cnt
            p.wait("vector", "sT", C["b1Tr"])
            C["b1Te"] = p.op("vector", lambda E: E.tensor_copy(b1T[:, :], pb[5][:, 0:128]), inc="sV")
            w1_dst = [w1a[:, :].rearrange("p (b h) -> p b h", b=NB),
                      w1b[:, :].rearrange("p (b h) -> p b h", b=NB)]
            s = w1_h[0].rearrange("(b p) h -> p b h", p=128)[:, :, 1024:2048]
            C["dw_w1"][(0, 1)] = p.dma_op("gpsimd", "dW", lambda E, s=s: E.dma_start(w1_dst[1], s))
            C["dw_b2"] = p.dma_op("gpsimd", "dW", lambda E: E.dma_start(b2sb[:, :], b2_h))
            w2_dst3 = w2sb[:, :].rearrange("p (k d) -> p k d", k=NHB)
            C["dw_w2"] = {}

            def emit_w2_dma(e, dc, wait_sT):
                s = w2_h[e].rearrange("(k p) d -> p k d", p=128)[:, :, 512 * dc:512 * (dc + 1)]
                d = w2_dst3[:, :, 512 * dc:512 * (dc + 1)]
                p.wait("gpsimd", "sT", wait_sT)
                C["dw_w2"][(e, dc)] = p.dma_op("gpsimd", "dW", lambda E, s=s, d=d: E.dma_start(d, s))

            def emit_w1_dma(e, half, wait_sT):
                s = w1_h[e].rearrange("(b p) h -> p b h", p=128)[:, :, 1024 * half:1024 * (half + 1)]
                p.wait("gpsimd", "sT", wait_sT)
                C["dw_w1"][(e, half)] = p.dma_op("gpsimd", "dW",
                                                 lambda E, s=s, h2=half: E.dma_start(w1_dst[h2], s))

            emit_w2_dma(0, 0, 0)
            emit_w2_dma(0, 1, 0)

            for name in ("relu", "hsmul", "w2mm", "oadd", "sbc_mm", "sbc_ev",
                         "h_last", "w2_last"):
                C[name] = {}

            NEXP = int(os.environ.get("KMK_NE", str(NE)))
            for e in range(NEXP):
                p.wait("tensor", "sV", C["b1Te"])
                for c4 in range(4):
                    bank = pb[4 + (c4 % 2)]
                    prior = (e, c4 - 2) if c4 >= 2 else ((e - 1, c4 + 2) if e >= 1 else None)
                    if prior is not None:
                        p.wait("tensor", "sV", C["sbc_ev"][prior])
                    l_ = esel[0:8, 128 * e:128 * (e + 1)]
                    r_ = scoresT[0:8, 512 * c4:512 * (c4 + 1)]
                    C["sbc_mm"][(e, c4)] = p.op("tensor", lambda E, o=bank[:, :], l=l_, r=r_:
                                                E.matmul(o, l, r, start=True, stop=True),
                                                inc="sT")
                    p.wait("vector", "sT", C["sbc_mm"][(e, c4)])
                    o_ = sbc[:, 512 * c4:512 * (c4 + 1)]
                    C["sbc_ev"][(e, c4)] = p.op("vector", lambda E, o=o_, i2=bank[:, :]:
                                                E.tensor_copy(o, i2), inc="sV")
                    if c4 == 3:
                        fsb = p.op("vector", lambda E: E.engine_nop(), inc="sVf")
                        p.wait("vector", "sVf", fsb)

                for c in range(NCH):
                    HB = (pb[0], pb[1], pb[6])
                    for k in range(NHB):
                        bank = HB[k % 3]
                        if k == 0:
                            p.wait("tensor", "dW", C["dw_w1"][(e, 0)])
                        if k == 8:
                            p.wait("tensor", "dW", C["dw_w1"][(e, 1)])
                        if k >= 3:
                            p.wait("tensor", "sS", C["relu"][(e, c, k - 3)])
                        elif c > 0:
                            p.wait("tensor", "sS", C["relu"][(e, c - 1, NHB - 3 + k)])
                        elif e > 0:
                            p.wait("tensor", "sS", C["relu"][(e - 1, NCH - 1, NHB - 3 + k)])
                        w1x = w1a if k < 8 else w1b
                        kk = k % 8
                        for b in range(NB):
                            l_ = w1x[:, 1024 * b + 128 * kk: 1024 * b + 128 * (kk + 1)]
                            r_ = xnT[:, SEQ * b + 512 * c: SEQ * b + 512 * (c + 1)]
                            cnt = p.op("tensor", lambda E, o=bank[:, :], l=l_, r=r_,
                                       st=(b == 0), sp=(b == NB - 1):
                                       E.matmul(o, l, r, start=st, stop=sp),
                                       inc="sT" if b == NB - 1 else None)
                        if c == NCH - 1 and k == 7:
                            C["h_last"][(e, 0)] = cnt
                        if c == NCH - 1 and k == NHB - 1:
                            C["h_last"][(e, 1)] = cnt
                        p.wait("scalar", "sT", cnt)
                        o_ = hs[:, 512 * k:512 * (k + 1)]
                        C["relu"][(e, c, k)] = p.op(
                            "scalar",
                            lambda E, o=o_, i2=bank[:, :], bi=b1T[:, 8 * k + e:8 * k + e + 1]:
                            E.activation(o, i2, AF.Relu, bias=bi), inc="sS")
                        p.wait("vector", "sS", C["relu"][(e, c, k)])
                        sb_ap = sbc[:, 512 * c:512 * (c + 1)]
                        C["hsmul"][(e, c, k)] = p.op(
                            "vector", lambda E, o=o_, s=sb_ap: E.tensor_tensor(o, o, s, ALU.mult),
                            inc="sV")
                    if e == 0 and c == 0:
                        p.wait("tensor", "dW", C["dw_b2"])
                    for dc in range(2):
                        p.wait("tensor", "dW", C["dw_w2"][(e, dc)])
                        WB = (pb[2], pb[3], pb[7])
                        for tt in range(4):
                            gidx = dc * 4 + tt
                            bank = WB[gidx % 3]
                            tau = 4 * c + tt
                            if gidx >= 3:
                                p.wait("tensor", "sV", C["oadd"][(e, c, gidx - 3)])
                            elif c > 0:
                                p.wait("tensor", "sV", C["oadd"][(e, c - 1, 5 + gidx)])
                            elif e > 0:
                                p.wait("tensor", "sV", C["oadd"][(e - 1, NCH - 1, 5 + gidx)])
                            for k in range(NHB):
                                p.wait("tensor", "sV", C["hsmul"][(e, c, k)])
                                l_ = hs[:, 512 * k + 128 * tt: 512 * k + 128 * (tt + 1)]
                                r_ = w2sb[:, DM * k + 512 * dc: DM * k + 512 * (dc + 1)]
                                is_last = (k == NHB - 1 and e != 0)
                                cnt = p.op("tensor", lambda E, o=bank[:, :], l=l_, r=r_,
                                           st=(k == 0), sp=is_last:
                                           E.matmul(o, l, r, start=st, stop=sp),
                                           inc="sT" if is_last else None)
                            if e == 0:
                                l_ = scoresT[0:8, 128 * tau:128 * (tau + 1)]
                                r_ = b2sb[0:8, 512 * dc:512 * (dc + 1)]
                                cnt = p.op("tensor", lambda E, o=bank[:, :], l=l_, r=r_:
                                           E.matmul(o, l, r, start=False, stop=True), inc="sT")
                            C["w2mm"][(e, c, gidx)] = cnt
                            if c == NCH - 1 and gidx == 7:
                                C["w2_last"][e] = cnt
                            p.wait("vector", "sT", cnt)
                            xsl = xres[:, DM * tau + 512 * dc: DM * tau + 512 * (dc + 1)]
                            C["oadd"][(e, c, gidx)] = p.op(
                                "vector", lambda E, x=xsl, b2_=bank[:, :]: E.tensor_add(x, x, b2_),
                                inc="sV")
                            if e == NEXP - 1 and dc == 1:
                                w0 = C["oadd"][(e, c, tt)]
                                w1v = C["oadd"][(e, c, 4 + tt)]
                                p.wait("scalar", "sV", max(w0, w1v))
                                o_ = out_h[128 * tau:128 * (tau + 1), :]
                                i_ = xres[:, DM * tau:DM * (tau + 1)]
                                p.dma_op("scalar", "dO", lambda E, o=o_, i2=i_: E.dma_start(o, i2))
                if e + 1 < NEXP:
                    emit_w1_dma(e + 1, 0, C["h_last"][(e, 0)])
                    emit_w1_dma(e + 1, 1, C["h_last"][(e, 1)])
                    emit_w2_dma(e + 1, 0, C["w2_last"][e])
                    emit_w2_dma(e + 1, 1, C["w2_last"][e])

            p.wait("scalar", "dO", p.cnt["dO"])
            p.play_into(nc)

    return nc


_NC_CACHE = None


def kernel(L0_attn_x, L0_attn_wqkv, L0_ffn_Wg, L0_ffn_W1, L0_ffn_b1, L0_ffn_W2, L0_ffn_b2):
    global _NC_CACHE
    x = np.ascontiguousarray(np.asarray(L0_attn_x, dtype=np.float32))
    wqkv = np.ascontiguousarray(np.asarray(L0_attn_wqkv, dtype=np.float32))
    wg = np.asarray(L0_ffn_Wg, dtype=np.float32)
    wgt = np.ascontiguousarray(wg.T)
    w1 = np.ascontiguousarray(np.asarray(L0_ffn_W1, dtype=np.float32))
    b1 = np.ascontiguousarray(np.asarray(L0_ffn_b1, dtype=np.float32))
    w2 = np.ascontiguousarray(np.asarray(L0_ffn_W2, dtype=np.float32))
    b2 = np.ascontiguousarray(np.asarray(L0_ffn_b2, dtype=np.float32))

    i128f = np.eye(128, dtype=np.float32)
    i128b = np.eye(128, dtype=np.float32).astype(ml_dtypes.bfloat16)
    ustr = np.triu(np.ones((128, 128), dtype=np.float32), k=1).astype(ml_dtypes.bfloat16)
    ones1 = np.ones((1, 128), dtype=ml_dtypes.bfloat16)
    onesc = np.ones((128, 1), dtype=ml_dtypes.bfloat16)
    esel = np.zeros((NE, NE * 128), dtype=ml_dtypes.bfloat16)
    for e in range(NE):
        esel[e, 128 * e:128 * (e + 1)] = 1
    iota = (np.arange(NT, dtype=np.float32)[None, :] * 128.0
            + np.arange(128, dtype=np.float32)[:, None])

    if _NC_CACHE is None:
        _NC_CACHE = build_nc()
    nc = _NC_CACHE

    base = {
        "wqkv": wqkv, "wgt": wgt, "b1": b1, "b2": b2,
        "i128f": i128f, "i128b": i128b, "ustr": ustr, "ones1": ones1,
        "onesc": onesc, "esel": esel, "iota": np.ascontiguousarray(iota),
    }
    for e in range(NE):
        base["w1_%d" % e] = w1[e]
        base["w2_%d" % e] = w2[e]
    in_maps = [dict(base, x=x[bidx]) for bidx in range(NCORE)]
    res = run_bass_kernel_spmd(nc, in_maps, core_ids=list(range(NCORE)))
    global LAST_RES
    LAST_RES = res
    return np.stack([res.results[bidx]["out"] for bidx in range(NCORE)], axis=0)
